# version 16
# baseline (speedup 1.0000x reference)
import math

import numpy as np
import ml_dtypes

import concourse.bass as bass
import concourse.mybir as mybir
from concourse.bass_utils import run_bass_kernel_spmd

F32 = mybir.dt.float32
BF16 = mybir.dt.bfloat16
F16 = mybir.dt.float16
I32 = mybir.dt.int32
AF = mybir.ActivationFunctionType
ALU = mybir.AluOpType
AX = mybir.AxisListType

T = 2048
C = 2048
NT = 16
NK = 16
NH = 16
HS = 128
I_FFN = 5632
GU_COLS = 2 * I_FFN
CSC = 1.0 / math.sqrt(HS)
NEG = -1.0e30
KB = 1024


class Ctr:
    def __init__(self, h):
        self.h = h
        self.n = 0

    def bump(self, k):
        self.n += k
        return self.n


def build_program(dbg=False, max_phase=9):
    nc = bass.Bass("TRN2", target_bir_lowering=False)

    x_in = nc.dram_tensor("x", [T, C], F32, kind="ExternalInput").ap()
    wattn = nc.dram_tensor("wattn", [48, 128, 2048], F32, kind="ExternalInput").ap()
    battn = nc.dram_tensor("battn", [1, 6144], F32, kind="ExternalInput").ap()
    wproj = nc.dram_tensor("wproj", [16, 128, 2048], F32, kind="ExternalInput").ap()
    bproj = nc.dram_tensor("bproj", [1, 2048], F32, kind="ExternalInput").ap()
    gu_pkt = nc.dram_tensor("gu_pkt", [2, 128, 1408], I32, kind="ExternalInput").ap()
    gu_sc = nc.dram_tensor("gu_sc", [2, 128, 1408], F16, kind="ExternalInput").ap()
    dn_pkt = nc.dram_tensor("dn_pkt", [704, 256], I32, kind="ExternalInput").ap()
    dn_sc = nc.dram_tensor("dn_sc", [704, 256], F16, kind="ExternalInput").ap()
    identb_in = nc.dram_tensor("identb", [128, 128], BF16, kind="ExternalInput").ap()
    identf_in = nc.dram_tensor("identf", [128, 128], F32, kind="ExternalInput").ap()
    mask_in = nc.dram_tensor("mask", [128, 128], F32, kind="ExternalInput").ap()
    masktb_in = nc.dram_tensor("masktb", [128, 128], BF16, kind="ExternalInput").ap()
    ones_in = nc.dram_tensor("ones1", [1, 128], F32, kind="ExternalInput").ap()

    y_out = nc.dram_tensor("y", [T, C], F32, kind="ExternalOutput").ap()

    skind = "ExternalOutput" if dbg else "Internal"
    qkT_d = nc.dram_tensor("qkT_d", [4096, T], BF16, kind=skind).ap()
    v_d = nc.dram_tensor("v_d", [NH, 128, T], BF16, kind=skind).ap()
    oT_d = nc.dram_tensor("oT_d", [C, T], BF16, kind=skind).ap()
    x1_d = nc.dram_tensor("x1_d", [T, C], F32, kind=skind).ap()
    wgu_sh = nc.dram_tensor("wgu_sh", [C, 1408], BF16, kind="Internal").ap()
    wdn_sh = nc.dram_tensor("wdn_sh", [I_FFN, 256], BF16, kind="Internal").ap()
    wgu_gd = nc.dram_tensor("wgu_gd", [8 * C, 1408], BF16, kind="Internal", addr_space="Shared").ap()
    wdn_gd = nc.dram_tensor("wdn_gd", [8 * I_FFN, 256], BF16, kind="Internal", addr_space="Shared").ap()
    hT_d = nc.dram_tensor("hT_d", [I_FFN, T], BF16, kind=skind).ap()

    ops = {k: [] for k in ("sync", "tensor", "vector", "scalar", "gpsimd")}
    EM = {k: ops[k].append for k in ops}

    def W(eng, ctr, val):
        if val and val > 0:
            EM[eng](lambda e, h=ctr.h, v=val: e.wait_ge(h, v))

    def DMA(eng, dst, src, ctr):
        n = ctr.bump(16)
        EM[eng](lambda e, d=dst, s=src, h=ctr.h: e.dma_start(d, s).then_inc(h, 16))
        return n

    def INC(eng, ctr, k=1):
        n = ctr.bump(k)
        if eng in ("vector", "scalar", "gpsimd"):
            EM[eng](lambda e, h=ctr.h, v=k: e.drain().then_inc(h, v))
        else:
            EM[eng](lambda e, h=ctr.h, v=k: e.nop().then_inc(h, v))
        return n

    def MM(psum_ap, lhsT, rhs, start, stop, inc=None):
        if inc is None:
            EM["tensor"](lambda e, o=psum_ap, l=lhsT, r=rhs, st=start, sp=stop:
                         e.matmul(o, l, r, start=st, stop=sp))
            return None
        n = inc.bump(1)
        EM["tensor"](lambda e, o=psum_ap, l=lhsT, r=rhs, st=start, sp=stop, h=inc.h:
                     e.matmul(o, l, r, start=st, stop=sp).then_inc(h, 1))
        return n

    with (
        nc.sbuf_tensor("arena", [128, 100 * KB], BF16) as arena_t,
        nc.psum_tensor("psA", [128, 2048], F32) as psA_t,
        nc.psum_tensor("psB", [128, 2048], F32) as psB_t,
        nc.semaphore("lds") as lds_h,
        nc.semaphore("ldg") as ldg_h,
        nc.semaphore("sts") as sts_h,
        nc.semaphore("stg") as stg_h,
        nc.semaphore("pe") as pe_h,
        nc.semaphore("dv") as dv_h,
        nc.semaphore("ac") as ac_h,
        nc.semaphore("cc") as cc_h,
        nc.Block() as block,
    ):
        arena = arena_t.ap()
        psA = psA_t.ap()
        psB = psB_t.ap()
        lds = Ctr(lds_h); ldg = Ctr(ldg_h); sts = Ctr(sts_h); stg = Ctr(stg_h)
        pe = Ctr(pe_h); dv = Ctr(dv_h); ac = Ctr(ac_h); cc = Ctr(cc_h)

        def cv(off_bytes, shape, dtype):
            assert off_bytes % 64 == 0
            nel = int(np.prod(shape))
            esz = {F32: 4, BF16: 2, F16: 2, I32: 4}[dtype]
            o = off_bytes // 2
            nbf = nel * esz // 2
            assert off_bytes + nel * esz <= 200 * KB, (off_bytes, nel, esz)
            apx = arena[:, o:o + nbf]
            if dtype != BF16:
                apx = apx.bitcast(dtype)
            if len(shape) == 1:
                return apx
            pat = "p (" + " ".join(f"d{i}" for i in range(len(shape))) + ") -> p " + \
                  " ".join(f"d{i}" for i in range(len(shape)))
            kw = {f"d{i}": s for i, s in enumerate(shape)}
            return apx.rearrange(pat, **kw)

        identb = cv(0x0000, [128], BF16)
        masktb = cv(0x2C00, [128], BF16)
        identf = cv(0x0100, [128], F32)
        mask_sb = cv(0x0300, [128], F32)
        battn_sb = cv(0x0500, [64], F32)
        battnq_sb = cv(0x0600, [16], F32)
        bpbc = cv(0x0700, [2048], F32)
        nmax_sb = cv(0x2700, [16], F32)
        rsum_sb = cv(0x2740, [16], F32)
        rinv_sb = cv(0x2780, [16], F32)
        s_sum = cv(0x27C0, [1], F32)
        s_nm = cv(0x2800, [1], F32)
        s_varp = cv(0x2840, [1], F32)
        s_rec = cv(0x2880, [1], F32)
        s_std = cv(0x2B80, [1], F32)
        eps5 = cv(0x2D00, [1], F32)
        eps6 = cv(0x2D40, [1], F32)
        s_rstd = cv(0x28C0, [1], F32)
        s_nmr = cv(0x2900, [1], F32)
        onesf = cv(0x2940, [128], F32)
        TB = 0x3400

        x_tiles = x_in.rearrange("(i p) c -> i p c", p=128)
        x1_tiles = x1_d.rearrange("(i p) c -> i p c", p=128)
        y_tiles = y_out.rearrange("(i p) c -> i p c", p=128)

        brow = cv(TB, [6144], F32)
        bprow = cv(TB + 24 * KB, [2048], F32)

        DMA("sync", identb, identb_in, lds)
        DMA("sync", identf, identf_in, lds)
        DMA("sync", mask_sb, mask_in, lds)
        DMA("sync", brow[0:1, :], battn, lds)
        DMA("sync", bprow[0:1, :], bproj, lds)
        DMA("sync", masktb, masktb_in, lds)
        n_ld0 = DMA("sync", onesf[0:1, :], ones_in, lds)

        W("tensor", lds, n_ld0)
        for m in range(48):
            MM(psA[:, m:m + 1], brow[0:1, bass.ts(m, 128)], onesf[0:1, 0:1], True, True)
        for nb in range(4):
            n_pe0 = MM(psB[:, bass.ts(nb, 512)], onesf[0:1, :], bprow[0:1, bass.ts(nb, 512)],
                       True, True, inc=pe if nb == 3 else None)

        EM["vector"](lambda e: e.memset(eps5[:], 1e-5))
        EM["vector"](lambda e: e.memset(eps6[:], 1e-6))
        W("vector", pe, n_pe0)
        EM["vector"](lambda e: e.tensor_copy(battn_sb[:, 0:48], psA[:, 0:48]))
        EM["vector"](lambda e: e.tensor_copy(bpbc[:], psB[:]))
        EM["vector"](lambda e: e.drain())
        EM["vector"](lambda e: e.tensor_scalar(battnq_sb[:], battn_sb[:, 0:16], CSC, None, ALU.mult))
        dv_p0 = INC("vector", dv)

        DQ = TB + 100 * KB
        dq_w = cv(DQ, [2, 1408], I32)
        dq_sc = cv(DQ + 12 * KB, [2, 1408], BF16)
        dq_mi = cv(DQ + 18 * KB, [1408], I32)
        dq_si = cv(DQ + 24 * KB, [1408], I32)
        dq_m = cv(DQ + 30 * KB, [1408], BF16)
        dq_sg = cv(DQ + 33 * KB, [1408], BF16)
        dq_r1 = cv(DQ + 36 * KB, [1408], BF16)
        dq_r2 = cv(DQ + 39 * KB, [1408], BF16)
        dq_t0 = cv(DQ + 42 * KB, [1408], BF16)
        dq_mag = cv(DQ + 45 * KB, [1408], BF16)
        dq_wv = cv(DQ + 48 * KB, [2, 1408], BF16)

        wgu_sc8 = wgu_sh.rearrange("(a r e) c -> a r e c", a=2, e=8)
        wdn_sc8 = wdn_sh.rearrange("(r e) c -> r e c", e=8)

        dqjobs = [("gu", cc_) for cc_ in range(2)] + [("dn", cc_) for cc_ in range(6)]
        dqstate = {"it": 0, "panel": -1, "ld_done": {}, "panels": [], "done": False,
                   "cc_emitted": False}
        ld_dqw = []; ld_dqs = []; dv_dq_panel = []; st_dq = []

        def dq_geom(pidx):
            kind, cc_ = dqjobs[pidx]
            if kind == "gu":
                cols, npart = 1408, 128
                wsrc = gu_pkt[cc_]
                ssrc = gu_sc[cc_]
            else:
                cols = 256
                npart = 128 if cc_ < 5 else 64
                wsrc = dn_pkt[cc_ * 128:cc_ * 128 + npart, :]
                ssrc = dn_sc[cc_ * 128:cc_ * 128 + npart, :]
            return kind, cc_, cols, npart, wsrc, ssrc

        def emit_dq_load(pidx):
            kind, cc_, cols, npart, wsrc, ssrc = dq_geom(pidx)
            pj = pidx % 2
            if pidx >= 2:
                W("sync", dv, dv_dq_panel[pidx - 2])
                W("gpsimd", dv, dv_dq_panel[pidx - 2])
            ld_dqw.append(DMA("sync", dq_w[0:npart, pj, 0:cols], wsrc, lds))
            ld_dqs.append(DMA("gpsimd", dq_sc[0:npart, pj, 0:cols], ssrc, ldg))

        def emit_dq_piece():
            st = dqstate
            if st["done"]:
                return
            it = st["it"]
            pidx, j = divmod(it, 8)
            if pidx >= len(dqjobs):
                st["done"] = True
                return
            if j == 0:
                if pidx == 0:
                    emit_dq_load(0)
                if pidx + 1 < len(dqjobs):
                    emit_dq_load(pidx + 1)
            kind, cc_, cols, npart, wsrc, ssrc = dq_geom(pidx)
            pj = pidx % 2
            W("vector", lds, lds.n)
            W("vector", ldg, ldg.n)
            wv_ = dq_w[0:npart, pj, 0:cols]
            sc_ = dq_sc[0:npart, pj, 0:cols]
            wb = dq_wv[0:npart, it % 2, 0:cols]
            mi = dq_mi[0:npart, 0:cols]; si = dq_si[0:npart, 0:cols]
            mg = dq_m[0:npart, 0:cols]; sg = dq_sg[0:npart, 0:cols]
            r1 = dq_r1[0:npart, 0:cols]; r2 = dq_r2[0:npart, 0:cols]
            t0 = dq_t0[0:npart, 0:cols]; mag = dq_mag[0:npart, 0:cols]
            EM["vector"](lambda e, w=wv_, j=j, o=mi:
                         e.tensor_scalar(o, w, 4 * j, 7, ALU.logical_shift_right, ALU.bitwise_and))
            EM["vector"](lambda e, w=wv_, j=j, o=si:
                         e.tensor_scalar(o, w, 4 * j + 3, 1, ALU.logical_shift_right, ALU.bitwise_and))
            EM["vector"](lambda e: e.drain())
            EM["vector"](lambda e, i=mi, o=mg: e.tensor_copy(o, i))
            EM["vector"](lambda e, i=si, o=sg: e.tensor_scalar(o, i, -2.0, 1.0, ALU.mult, ALU.add))
            EM["vector"](lambda e: e.drain())
            EM["vector"](lambda e, i=mg, o=r1: e.tensor_scalar(o, i, 4.0, -4.0, ALU.max, ALU.add))
            EM["vector"](lambda e, i=mg, o=r2: e.tensor_scalar(o, i, 6.0, -6.0, ALU.max, ALU.add))
            EM["vector"](lambda e: e.drain())
            EM["vector"](lambda e, i=mg, r=r2, o=t0:
                         e.scalar_tensor_tensor(o, i, 0.5, r, ALU.mult, ALU.add))
            EM["vector"](lambda e: e.drain())
            EM["vector"](lambda e, i=r1, r=t0, o=mag:
                         e.scalar_tensor_tensor(o, i, 0.5, r, ALU.mult, ALU.add))
            EM["vector"](lambda e: e.drain())
            EM["vector"](lambda e, a=mag, b=sg: e.tensor_tensor(a, a, b, ALU.mult))
            EM["vector"](lambda e: e.drain())
            if it >= 2:
                W("vector", stg, stg.n)
            EM["vector"](lambda e, a=mag, b=sc_, o=wb: e.tensor_tensor(o, a, b, ALU.mult))
            ndv = INC("vector", dv)
            if j == 7:
                dv_dq_panel.append(ndv)
            W("gpsimd", dv, ndv)
            if kind == "gu":
                dst = wgu_sc8[cc_, :, j, :]
            else:
                dst = wdn_sc8[:, j, :][cc_ * 128:cc_ * 128 + npart, :]
            st_dq.append(DMA("gpsimd", dst, wb, stg))
            st["it"] = it + 1
            if st["it"] == len(dqjobs) * 8:
                st["done"] = True

        def emit_dq_collective():
            if dqstate["cc_emitted"]:
                return
            while not dqstate["done"]:
                emit_dq_piece()
            W("gpsimd", stg, stg.n)
            EM["gpsimd"](lambda e: e.collective_compute(
                "AllGather", ALU.bypass, replica_groups=[list(range(8))],
                ins=[wgu_sh.opt()], outs=[wgu_gd.opt()]).then_inc(cc_h, 1))
            cc.bump(1)
            EM["gpsimd"](lambda e: e.collective_compute(
                "AllGather", ALU.bypass, replica_groups=[list(range(8))],
                ins=[wdn_sh.opt()], outs=[wdn_gd.opt()]).then_inc(cc_h, 1))
            dqstate["cc_emitted"] = True
            return cc.bump(1)


        if max_phase >= 1:
            XT = TB
            xT = cv(XT, [16, 2048], BF16)
            P1B = TB + 64 * KB
            xbf = cv(P1B, [2, 2048], BF16)

            pe_x = []; dv_tx = []; ld_x = []
            for i in range(NT):
                if i < 2:
                    if i == 0:
                        W("gpsimd", pe, n_pe0)
                    ld_x.append(DMA("gpsimd", xbf[:, i % 2, :], x_tiles[i], ldg))
                W("tensor", ldg, ldg.n)
                if i == 0:
                    W("tensor", dv, dv_p0)
                else:
                    W("tensor", dv, dv_tx[i - 1])
                for f in range(NK):
                    n_ = MM(psA[:, bass.ts(f, 128)], xbf[:, i % 2, bass.ts(f, 128)], identb,
                            True, True, inc=pe if f == NK - 1 else None)
                pe_x.append(n_)
                W("vector", pe, pe_x[i])
                EM["vector"](lambda e, i=i: e.tensor_copy(xT[:, :, bass.ts(i, 128)], psA[:]))
                dv_tx.append(INC("vector", dv))
                if i + 2 < NT:
                    W("gpsimd", pe, pe_x[i])
                    ld_x.append(DMA("gpsimd", xbf[:, i % 2, :], x_tiles[i + 2], ldg))

        if max_phase >= 2:
            wm = cv(P1B + 8 * KB, [2, 2048], BF16)
            qkvrow = cv(P1B + 16 * KB, [2, 2048], BF16)
            vtile = cv(P1B + 24 * KB, [2048], BF16)

            pe_qkv = []; ac_qkv = []; dv_v = []; st_qk = {}; st_v = []; pe_v = []
            ld_w2 = []
            ps2 = [psA, psB]

            def emit_tr_v(mv):
                h = mv - 32
                W("tensor", ac, ac_qkv[mv])
                for t in range(NT):
                    n_ = MM(ps2[mv % 2][:, bass.ts(t, 128)],
                            qkvrow[:, mv % 2, bass.ts(t, 128)], identb,
                            True, True, inc=pe if t == NT - 1 else None)
                pe_v.append(n_)
                W("vector", pe, pe_v[h])
                if h >= 1:
                    W("vector", sts, sts.n)
                EM["vector"](lambda e: e.tensor_copy(vtile[:], ps2[mv % 2][:]))
                dv_v.append(INC("vector", dv))
                W("sync", dv, dv_v[h])
                st_v.append(DMA("sync", v_d[h], vtile[:], sts))

            for m in range(49):
                if m < 2:
                    if m == 0:
                        W("gpsimd", pe, pe_x[NT - 1])
                    ld_w2.append(DMA("gpsimd", wm[:, m % 2, :], wattn[m], ldg))
                if m <= 47:
                    W("tensor", ldg, ldg.n)
                    if m == 0:
                        W("tensor", dv, dv_tx[NT - 1])
                    if m >= 2:
                        W("tensor", ac, ac_qkv[m - 2])
                        if m - 2 >= 32:
                            W("tensor", dv, dv_v[m - 34])
                    for nb in range(4):
                        for k in range(NK):
                            n_ = MM(ps2[m % 2][:, bass.ts(nb, 512)], wm[:, m % 2, bass.ts(k, 128)],
                                    xT[:, k, bass.ts(nb, 512)], k == 0, k == NK - 1,
                                    inc=pe if (nb == 3 and k == NK - 1) else None)
                    pe_qkv.append(n_)
                if m >= 1 and m - 1 >= 32:
                    emit_tr_v(m - 1)
                if m <= 47:
                    W("scalar", pe, pe_qkv[m])
                    if m >= 2:
                        mp = m - 2
                        if mp < 32:
                            W("scalar", sts, sts.n)
                        else:
                            W("scalar", pe, pe_v[mp - 32])
                    if m < 16:
                        EM["scalar"](lambda e, m=m: e.activation(
                            qkvrow[:, m % 2, :], ps2[m % 2][:], AF.Identity,
                            bias=battnq_sb[:, m:m + 1], scale=CSC))
                    else:
                        EM["scalar"](lambda e, m=m: e.activation(
                            qkvrow[:, m % 2, :], ps2[m % 2][:], AF.Identity,
                            bias=battn_sb[:, m:m + 1], scale=1.0))
                    ac_qkv.append(INC("scalar", ac))
                    if m < 32:
                        W("sync", ac, ac_qkv[m])
                        st_qk[m] = DMA("sync", qkT_d[bass.ts(m, 128), :], qkvrow[:, m % 2, :], sts)
                    if m + 2 < 48:
                        W("gpsimd", pe, pe_qkv[m])
                        ld_w2.append(DMA("gpsimd", wm[:, m % 2, :], wattn[m + 2], ldg))
                    if max_phase >= 3:
                        emit_dq_piece()

            pe_p2_done = pe.n
            sts_p2_done = sts.n

        if max_phase >= 3:
            cc_done = emit_dq_collective()
            dv_dq_done = dv.n
            stg_dq_done = stg.n

        if max_phase >= 4:
            AT = TB
            qh = cv(AT, [2, 2048], BF16)
            kh = cv(AT + 8 * KB, [2, 2048], BF16)
            vh = cv(AT + 16 * KB, [2, 2048], BF16)
            pn = cv(AT + 24 * KB, [2, 2048], BF16)
            PT = cv(AT + 32 * KB, [16, 512], BF16)
            orow = cv(AT + 48 * KB, [2, 2048], BF16)

            pe_head = []; st_o = []; ld_h4 = []
            oT5 = cv(TB + 56 * KB, [16, 2048], BF16)
            wp5 = cv(TB + 120 * KB, [16, 2048], BF16)

            def emit_head_loads(h):
                if h == 0:
                    W("sync", dv, dv_dq_done)
                    W("sync", stg, stg_dq_done)
                DMA("sync", qh[:, h % 2, :], qkT_d[bass.ts(h, 128), :], lds)
                DMA("sync", kh[:, h % 2, :], qkT_d[2048 + h * 128:2176 + h * 128, :], lds)
                ld_h4.append(DMA("sync", vh[:, h % 2, :], v_d[h], lds))

            emit_head_loads(0)
            emit_head_loads(1)

            ac_exp = {}; dv_ptev = {}; ac_oev = []
            ps4 = [psA, psB]
            last_oev = 0
            n_s = {}; n_pt = {}; n_red = {}; n_recip = {}; n_nrm = {}
            for h in range(NH):
                hp = h % 2
                W("tensor", lds, lds.n)
                for t in range(17):
                    tl = t - 1
                    if t <= 15:
                        qt = t
                        Lk = (qt + 1) * 128
                        par = qt % 2
                        P = ps4[par]
                        prev = dv_ptev.get((h, qt - 2))
                        if prev is None and qt < 2:
                            prev = dv_ptev.get((h - 1, 14 + par))
                        if prev:
                            W("tensor", dv, prev)
                        elif h == 0 and qt < 2:
                            W("tensor", dv, dv.n)
                            W("tensor", ac, ac.n)
                        if par == 1 and Lk > 1536 and last_oev:
                            W("tensor", ac, last_oev)
                        nnb = (Lk + 511) // 512
                        for nb in range(nnb):
                            N = min(512, Lk - nb * 512)
                            lastc = nb == nnb - 1
                            MM(P[:, nb * 512:nb * 512 + N], qh[:, hp, bass.ts(qt, 128)],
                               kh[:, hp, nb * 512:nb * 512 + N], True, not lastc)
                            if lastc:
                                n_s[t] = MM(P[:, bass.ts(qt, 128)], masktb, identb,
                                            False, True, inc=pe)
                    if tl >= 0:
                        parl = tl % 2
                        Pl = ps4[parl]
                        Lkl = (tl + 1) * 128
                        W("vector", ac, ac_exp[(h, tl)])
                        EM["vector"](lambda e, qt=tl: e.reciprocal(
                            rinv_sb[:, qt:qt + 1], rsum_sb[:, qt:qt + 1]))
                        EM["vector"](lambda e: e.drain())
                        EM["vector"](lambda e, L=Lkl, qt=tl, par=parl: e.tensor_scalar(
                            pn[:, par, 0:L], pn[:, par, 0:L], rinv_sb[:, qt:qt + 1],
                            None, ALU.mult))
                        n_nrm[tl] = INC("vector", dv)
                        W("tensor", dv, n_nrm[tl])
                        nkt = tl + 1
                        for kt in range(nkt):
                            n_pt[tl] = MM(Pl[:, bass.ts(kt, 128)], pn[:, parl, bass.ts(kt, 128)],
                                          identb, True, True, inc=pe if kt == nkt - 1 else None)
                    if t <= 15:
                        W("vector", pe, n_s[t])
                        EM["vector"](lambda e, L=Lk, qt=t, P=P: e.reduce_max(
                            nmax_sb[:, qt:qt + 1], P[:, 0:L], axis=AX.X, negate=True))
                        n_red[t] = INC("vector", dv)
                    if tl >= 0:
                        W("vector", pe, n_pt[tl])
                        EM["vector"](lambda e, nkt=tl + 1, q4=tl % 4, P=Pl: e.tensor_copy(
                            PT[:, 0:nkt, bass.ts(q4, 128)],
                            P[:, 0:nkt * 128].rearrange("p (a b) -> p a b", b=128)))
                        dv_ptev[(h, tl)] = INC("vector", dv)
                    if t <= 15:
                        W("scalar", dv, n_red[t])
                        EM["scalar"](lambda e, L=Lk, qt=t, par=par, P=P: e.activation(
                            pn[:, par, 0:L], P[:, 0:L], AF.Exp,
                            bias=nmax_sb[:, qt:qt + 1], scale=1.0,
                            accum_out=rsum_sb[:, qt:qt + 1]))
                        ac_exp[(h, t)] = INC("scalar", ac)
                    if t >= 4 and t % 4 == 0:
                        qg = t // 4 - 1
                        for r in range(1, 4):
                            kt = qg * 4 + r
                            EM["vector"](lambda e, kt=kt, r=r: e.memset(PT[:, kt, 0:r * 128], 0.0))
                        n_z = INC("vector", dv)
                        W("tensor", dv, n_z)
                        W("tensor", ac, ac_exp[(h, qg * 4 + 3)])
                        nkt_g = qg * 4 + 4
                        for kt in range(nkt_g):
                            n_pv = MM(psB[:, 1536:2048], vh[:, hp, bass.ts(kt, 128)], PT[:, kt, :],
                                      kt == 0, kt == nkt_g - 1,
                                      inc=pe if kt == nkt_g - 1 else None)
                        W("scalar", pe, n_pv)
                        if qg == 0 and h >= 2:
                            W("scalar", sts, sts.n)
                        EM["scalar"](lambda e, qg=qg, hp=hp: e.activation(
                            orow[:, hp, bass.ts(qg, 512)], psB[:, 1536:2048], AF.Identity,
                            bias=0.0, scale=1.0))
                        n_oev = INC("scalar", ac)
                        last_oev = n_oev
                        if qg == 3:
                            ac_oev.append(n_oev)
                pe_head.append(pe.n)
                W("sync", ac, ac_oev[h])
                st_o.append(DMA("sync", oT_d[bass.ts(h, 128), :], orow[:, h % 2, :], sts))
                W("sync", sts, sts.n)
                ld_p5oT = DMA("sync", oT5[:, h, :], oT_d[bass.ts(h, 128), :], lds)
                if h == 0:
                    for k5 in range(NK):
                        DMA("gpsimd", wp5[:, k5, :], wproj[k5], ldg)
                if h + 2 < NH:
                    W("sync", pe, pe_head[h])
                    emit_head_loads(h + 2)

            sts_attn_done = sts.n
            pe_attn_done = pe.n

        if max_phase >= 5:
            P5 = TB + 56 * KB
            oT = cv(P5, [16, 2048], BF16)
            wp = cv(TB + 120 * KB, [16, 2048], BF16)
            P5S = TB
            xres = cv(P5S, [2, 2048], F32)
            rsb = cv(P5S + 16 * KB, [2048], F32)
            sqs = cv(P5S + 24 * KB, [2048], F32)
            x1sb = cv(P5S + 32 * KB, [2, 2048], F32)

            W("sync", sts, sts_attn_done)
            W("sync", pe, pe_attn_done)
            ld_xr = {0: DMA("sync", xres[:, 0, :], x_tiles[0], lds),
                     1: DMA("sync", xres[:, 1, :], x_tiles[1], lds)}

            pe_p5 = []; dv_p5 = {}; ac_p5 = {}; st_x1 = []
            ps5 = [psA, psB]
            W("tensor", lds, lds.n)
            W("tensor", ldg, ldg.n)
            W("tensor", sts, sts_attn_done)
            for i in range(NT):
                if i <= 1:
                    W("tensor", ac, ac.n)
                    W("tensor", dv, dv.n)
                else:
                    W("tensor", dv, dv_p5[(i - 2, "bf")])
                for nb in range(4):
                    for k in range(NK):
                        n_ = MM(ps5[i % 2][:, bass.ts(nb, 512)], oT[:, k, bass.ts(i, 128)],
                                wp[:, k, bass.ts(nb, 512)], k == 0, k == NK - 1,
                                inc=pe if (nb == 3 and k == NK - 1) else None)
                pe_p5.append(n_)
                W("vector", pe, pe_p5[i])
                W("vector", lds, lds.n)
                if i >= 1:
                    W("vector", ac, ac_p5[(i - 1, "x1")])
                EM["vector"](lambda e, i=i: e.scalar_tensor_tensor(rsb[:], ps5[i % 2][:], 1.0, bpbc[:], ALU.mult, ALU.add))
                dv_p5[(i, "bf")] = INC("vector", dv)
                EM["vector"](lambda e, i=i: e.tensor_tensor(rsb[:], rsb[:], xres[:, i % 2, :], ALU.add))
                EM["vector"](lambda e: e.drain())
                EM["vector"](lambda e: e.reduce_sum(s_sum[:], rsb[:], axis=AX.X))
                EM["vector"](lambda e: e.drain())
                EM["vector"](lambda e: e.tensor_scalar(s_nm[:], s_sum[:], -1.0 / 2048.0, None, ALU.mult))
                dv_p5[(i, "r")] = INC("vector", dv)
                W("scalar", dv, dv_p5[(i, "r")])
                EM["scalar"](lambda e: e.activation(sqs[:], rsb[:], AF.Square, bias=s_nm[:, 0:1],
                                                    scale=1.0, accum_out=s_varp[:, 0:1]))
                EM["scalar"](lambda e: e.drain())
                EM["scalar"](lambda e: e.activation(s_std[:], s_varp[:], AF.Sqrt,
                                                    bias=eps5[:, 0:1], scale=1.0 / 2048.0))
                ac_p5[(i, "sq")] = INC("scalar", ac)
                W("vector", ac, ac_p5[(i, "sq")])
                EM["vector"](lambda e: e.reciprocal(s_rstd[:], s_std[:]))
                EM["vector"](lambda e: e.drain())
                EM["vector"](lambda e: e.tensor_tensor(s_nmr[:], s_nm[:], s_rstd[:], ALU.mult))
                dv_p5[(i, "n")] = INC("vector", dv)
                W("scalar", dv, dv_p5[(i, "n")])
                if i >= 2:
                    W("scalar", sts, sts.n)
                EM["scalar"](lambda e, i=i: e.activation(x1sb[:, i % 2, :], rsb[:], AF.Identity,
                                                         bias=s_nmr[:, 0:1], scale=s_rstd[:, 0:1]))
                ac_p5[(i, "x1")] = INC("scalar", ac)
                W("sync", ac, ac_p5[(i, "x1")])
                st_x1.append(DMA("sync", x1_tiles[i], x1sb[:, i % 2, :], sts))
                if i + 2 < NT:
                    W("sync", dv, dv_p5[(i, "r")])
                    ld_xr[i + 2] = DMA("sync", xres[:, i % 2, :], x_tiles[i + 2], lds)
            sts_x1_done = sts.n
            pe_p5_done = pe.n

        if max_phase >= 6:
            x1T = cv(XT, [16, 2048], BF16)
            P5B = TB + 64 * KB
            x1bf = cv(P5B, [2, 2048], BF16)

            pe_x1 = []; dv_tx1 = []; ld_x1 = []
            for i in range(NT):
                if i < 2:
                    if i == 0:
                        W("gpsimd", sts, sts_x1_done)
                        W("gpsimd", pe, pe_p5_done)
                    ld_x1.append(DMA("gpsimd", x1bf[:, i % 2, :], x1_tiles[i], ldg))
                W("tensor", ldg, ldg.n)
                if i == 0:
                    W("tensor", dv, dv.n)
                    W("tensor", ac, ac.n)
                else:
                    W("tensor", dv, dv_tx1[i - 1])
                for f in range(NK):
                    n_ = MM(psA[:, bass.ts(f, 128)], x1bf[:, i % 2, bass.ts(f, 128)], identb,
                            True, True, inc=pe if f == NK - 1 else None)
                pe_x1.append(n_)
                W("vector", pe, pe_x1[i])
                EM["vector"](lambda e, i=i: e.tensor_copy(x1T[:, :, bass.ts(i, 128)], psA[:]))
                dv_tx1.append(INC("vector", dv))
                if i + 2 < NT:
                    W("gpsimd", pe, pe_x1[i])
                    ld_x1.append(DMA("gpsimd", x1bf[:, i % 2, :], x1_tiles[i + 2], ldg))

        if max_phase >= 7:
            P6 = TB + 64 * KB
            wg6 = cv(P6, [16, 1408], BF16)
            wu6 = cv(P6 + 45 * KB, [16, 1408], BF16)
            sgt = cv(P6 + 90 * KB, [2048], BF16)
            hrow = cv(P6 + 94 * KB, [2, 2048], BF16)

            wgu_rk = wgu_gd.rearrange("(r k p) c -> r k p c", r=8, p=128)
            hT_rows = hT_d.rearrange("(k p) t -> k p t", p=128)
            pe_p6 = {}; st_h6 = []
            ac_p6s = None; dv_p6h = None
            for a in range(4):
                ld_g = 0
                for k in range(NK):
                    if k == 0:
                        if a == 0:
                            W("sync", pe, pe.n)
                        else:
                            W("sync", pe, pe_p6[(a - 1, 10)])
                if a == 0:
                    W("sync", cc, cc_done)
                for k in range(NK):
                    ld_g = DMA("sync", wg6[:, k, :], wgu_rk[a, k], lds)
                ld_u = 0
                for k in range(NK):
                    if k == 0:
                        if a == 0:
                            W("gpsimd", pe, pe.n)
                        else:
                            W("gpsimd", pe, pe_p6[(a - 1, 10)])
                for k in range(NK):
                    ld_u = DMA("gpsimd", wu6[:, k, :], wgu_rk[4 + a, k], ldg)
                for mpl in range(11):
                    mp = a * 11 + mpl
                    W("tensor", lds, ld_g)
                    W("tensor", ldg, ld_u)
                    if mp == 0:
                        W("tensor", dv, dv.n)
                        W("tensor", ac, ac.n)
                    else:
                        W("tensor", ac, ac_p6s)
                        W("tensor", dv, dv_p6h)
                    for nb in range(4):
                        for k in range(NK):
                            n_g = MM(psA[:, bass.ts(nb, 512)], wg6[:, k, bass.ts(mpl, 128)],
                                     x1T[:, k, bass.ts(nb, 512)], k == 0, k == NK - 1,
                                     inc=pe if (nb == 3 and k == NK - 1) else None)
                    for nb in range(4):
                        for k in range(NK):
                            n_u = MM(psB[:, bass.ts(nb, 512)], wu6[:, k, bass.ts(mpl, 128)],
                                     x1T[:, k, bass.ts(nb, 512)], k == 0, k == NK - 1,
                                     inc=pe if (nb == 3 and k == NK - 1) else None)
                    W("scalar", pe, n_g)
                    if mp >= 1:
                        W("scalar", dv, dv_p6h)
                    EM["scalar"](lambda e: e.activation(sgt[:], psA[:], AF.Silu, bias=0.0, scale=1.0))
                    ac_p6s = INC("scalar", ac)
                    W("vector", ac, ac_p6s)
                    W("vector", pe, n_u)
                    if mp >= 2:
                        W("vector", sts, sts.n)
                    EM["vector"](lambda e, mp=mp: e.tensor_tensor(hrow[:, mp % 2, :], sgt[:],
                                                                  psB[:], ALU.mult))
                    dv_p6h = INC("vector", dv)
                    pe_p6[(a, mpl)] = pe.n
                    W("sync", dv, dv_p6h)
                    st_h6.append(DMA("sync", hT_rows[0].tensor.ap()
                                     .rearrange("(k p) t -> k p t", p=128)[mp] if False else
                                     hT_d[bass.ts(mp, 128), :], hrow[:, mp % 2, :], sts))
            sts_h_done = sts.n

        if max_phase >= 8:
            P7 = TB
            hT7 = cv(P7, [2, 44, 512], BF16)
            dT = cv(P7 + 88 * KB, [16, 512], F32)
            wdc = cv(P7 + 120 * KB, [2, 8, 512], BF16)
            EPI = P7 + 136 * KB
            sq7 = cv(EPI, [2048], F32)
            x1r7 = cv(EPI + 8 * KB, [2, 2048], F32)
            dsb = cv(EPI + 24 * KB, [2048], F32)
            r2sb = cv(EPI + 32 * KB, [2048], F32)
            outsb = cv(EPI + 40 * KB, [2048], F32)

            wdn_pkc8 = wdn_gd.rearrange("(r K p) c -> r p K c", r=8, p=128)
            hT_pkc = hT_d.rearrange("(K p) t -> p K t", p=128)
            accs = [psA[:, bass.ts(m4, 512)] for m4 in range(4)]

            st_y = []
            pe_dn_hist = {}
            for tg in range(4):
                tb = tg % 2
                ld_h7 = 0
                for gb in range(11):
                    if gb == 0:
                        W("gpsimd", ldg, ldg.n)
                        if tg == 0:
                            W("gpsimd", sts, sts_h_done)
                            W("gpsimd", pe, pe.n)
                            W("gpsimd", ac, ac.n)
                            W("gpsimd", dv, dv.n)
                        elif tg >= 2:
                            W("gpsimd", pe, pe_dn_hist[tg - 2])
                    ld_h7 = DMA("gpsimd", hT7[:, tb, gb * 4:gb * 4 + 4, :],
                                hT_pkc[:, gb * 4:gb * 4 + 4, bass.ts(tg, 512)], ldg)
                for mg in range(4):
                    pe_p7mm = []
                    GS = [8, 8, 8, 8, 8, 4]
                    GO = [0, 8, 16, 24, 32, 40]
                    def emit_wdc_load(g):
                        n = GS[g]
                        DMA("sync", wdc[:, g % 2, 0:n, 0:256],
                            wdn_pkc8[2 * mg, :, GO[g]:GO[g] + n, :], lds)
                        return DMA("sync", wdc[:, g % 2, 0:n, 256:512],
                                   wdn_pkc8[2 * mg + 1, :, GO[g]:GO[g] + n, :], lds)
                    for g in range(2):
                        if g == 0:
                            W("sync", pe, pe.n)
                            if tg == 0 and mg == 0:
                                W("sync", cc, cc_done)
                                W("sync", ac, ac.n)
                                W("sync", dv, dv.n)
                        emit_wdc_load(g)
                    for kq in range(44):
                        g = kq // 8
                        gk = kq % 8
                        if gk == 0:
                            W("tensor", lds, lds.n)
                        if kq == 0:
                            W("tensor", ldg, ldg.n)
                            if mg == 0:
                                if tg == 0:
                                    W("tensor", dv, dv.n)
                                    W("tensor", ac, ac.n)
                                else:
                                    W("tensor", dv, dv_dT)
                            if mg > 0 or tg > 0:
                                W("tensor", dv, dv_dT_last)
                        for m4 in range(4):
                            n_ = MM(accs[m4], wdc[:, g % 2, gk, bass.ts(m4, 128)],
                                    hT7[:, tb, kq, :], kq == 0, kq == 43,
                                    inc=pe if (m4 == 3) else None)
                        pe_p7mm.append(n_)
                        if gk == GS[g] - 1 and g + 2 < 6:
                            W("sync", pe, pe_p7mm[kq])
                            emit_wdc_load(g + 2)
                    W("vector", pe, pe_p7mm[-1])
                    for m4 in range(4):
                        m = mg * 4 + m4
                        EM["vector"](lambda e, m=m, m4=m4: e.tensor_copy(dT[:, m, :], accs[m4]))
                    dv_dT_last = INC("vector", dv)
                dv_dT = dv_dT_last
                pe_dn_hist[tg] = pe.n
                for il in range(4):
                    i = tg * 4 + il
                    if il < 2:
                        if il == 0 and tg == 0:
                            W("scalar", dv, dv_dT)
                        n_x1r = DMA("scalar", x1r7[:, il % 2, :], x1_tiles[i], sts)
                        if il == 0:
                            ld_x1r = {}
                        ld_x1r[il] = n_x1r
                    W("tensor", dv, dv_dT)
                    if il >= 1:
                        W("tensor", dv, dv_p7tr[il - 1])
                    elif tg > 0:
                        W("tensor", dv, dv_p7tr[3])
                        W("tensor", ac, ac.n)
                    for m in range(NK):
                        n_tr = MM(psB[:, bass.ts(m, 128)], dT[:, m, bass.ts(il, 128)], identf,
                                  True, True, inc=pe if m == NK - 1 else None)
                    if il == 0:
                        dv_p7tr = []
                    W("vector", pe, n_tr)
                    if il >= 1 or tg > 0:
                        W("vector", ac, ac_ep_last)
                    EM["vector"](lambda e: e.tensor_copy(dsb[:], psB[:]))
                    dv_p7tr.append(INC("vector", dv))
                    W("scalar", dv, dv_p7tr[il])
                    EM["scalar"](lambda e: e.activation(sq7[:], dsb[:], AF.Square, bias=0.0,
                                                        scale=1.0, accum_out=s_varp[:, 0:1]))
                    EM["scalar"](lambda e: e.drain())
                    EM["scalar"](lambda e: e.activation(s_std[:], s_varp[:], AF.Sqrt,
                                                        bias=eps6[:, 0:1], scale=1.0 / 2048.0))
                    n_ac1 = INC("scalar", ac)
                    W("vector", ac, n_ac1)
                    W("vector", sts, sts.n)
                    EM["vector"](lambda e: e.reciprocal(s_rstd[:], s_std[:]))
                    EM["vector"](lambda e: e.drain())
                    EM["vector"](lambda e, il=il: e.scalar_tensor_tensor(
                        r2sb[:], dsb[:], s_rstd[:, 0:1], x1r7[:, il % 2, :], ALU.mult, ALU.add))
                    EM["vector"](lambda e: e.drain())
                    EM["vector"](lambda e: e.reduce_sum(s_sum[:], r2sb[:], axis=AX.X))
                    EM["vector"](lambda e: e.drain())
                    EM["vector"](lambda e: e.tensor_scalar(s_nm[:], s_sum[:], -1.0 / 2048.0, None, ALU.mult))
                    n_dv2 = INC("vector", dv)
                    W("scalar", dv, n_dv2)
                    EM["scalar"](lambda e: e.activation(sq7[:], r2sb[:], AF.Square, bias=s_nm[:, 0:1],
                                                        scale=1.0, accum_out=s_varp[:, 0:1]))
                    EM["scalar"](lambda e: e.drain())
                    EM["scalar"](lambda e: e.activation(s_std[:], s_varp[:], AF.Sqrt,
                                                        bias=eps5[:, 0:1], scale=1.0 / 2048.0))
                    n_ac3 = INC("scalar", ac)
                    W("vector", ac, n_ac3)
                    EM["vector"](lambda e: e.reciprocal(s_rstd[:], s_std[:]))
                    EM["vector"](lambda e: e.drain())
                    EM["vector"](lambda e: e.tensor_tensor(s_nmr[:], s_nm[:], s_rstd[:], ALU.mult))
                    n_dv4 = INC("vector", dv)
                    W("scalar", dv, n_dv4)
                    W("scalar", sts, sts.n)
                    EM["scalar"](lambda e, il=il: e.activation(outsb[:], r2sb[:], AF.Identity,
                                                               bias=s_nmr[:, 0:1], scale=s_rstd[:, 0:1]))
                    EM["scalar"](lambda e: e.drain())
                    st_y.append(DMA("scalar", y_tiles[i], outsb[:], sts))
                    ac_ep_last = INC("scalar", ac)
                    if il + 2 < 4:
                        ld_x1r[il + 2] = DMA("scalar", x1r7[:, il % 2, :], x1_tiles[i + 2], sts)

        W("sync", sts, sts.n)
        W("sync", stg, stg.n)
        W("gpsimd", ldg, ldg.n)
        W("sync", lds, lds.n)

        @block.sync
        def _(e):
            for f in ops["sync"]:
                f(e)

        @block.tensor
        def _(e):
            for f in ops["tensor"]:
                f(e)

        @block.vector
        def _(e):
            for f in ops["vector"]:
                f(e)

        @block.scalar
        def _(e):
            for f in ops["scalar"]:
                f(e)

        @block.gpsimd
        def _(e):
            for f in ops["gpsimd"]:
                f(e)

    return nc


def prep_inputs(L0_attn_x, L0_attn_w_attn, L0_attn_b_attn, L0_attn_w_proj,
                L0_attn_b_proj, L0_ffn_gate_up_packed, L0_ffn_gate_up_scales,
                L0_ffn_down_packed, L0_ffn_down_scales):
    x = np.ascontiguousarray(np.asarray(L0_attn_x, dtype=np.float32))
    wa = np.asarray(L0_attn_w_attn, dtype=np.float32)
    wattn = np.ascontiguousarray(wa.reshape(16, 128, 48, 128).transpose(2, 1, 0, 3)
                                 .reshape(48, 128, 2048))
    battn = np.ascontiguousarray(np.asarray(L0_attn_b_attn, dtype=np.float32).reshape(1, 6144))
    wproj = np.ascontiguousarray(np.asarray(L0_attn_w_proj, dtype=np.float32).reshape(16, 128, 2048))
    bprojv = np.ascontiguousarray(np.asarray(L0_attn_b_proj, dtype=np.float32).reshape(1, 2048))
    gu_pkt_f = np.asarray(L0_ffn_gate_up_packed, dtype=np.int32).T.reshape(2, 128, GU_COLS)
    gu_sc_f = np.repeat(np.asarray(L0_ffn_gate_up_scales, dtype=np.float16),
                        16, axis=0).reshape(2, 128, GU_COLS)
    dn_pkt_f = np.asarray(L0_ffn_down_packed, dtype=np.int32).T
    dn_sc_f = np.repeat(np.asarray(L0_ffn_down_scales, dtype=np.float16), 16, axis=0)
    identb = np.eye(128, dtype=ml_dtypes.bfloat16)
    identf = np.eye(128, dtype=np.float32)
    mask = np.triu(np.full((128, 128), NEG, dtype=np.float32), k=1)
    masktb = np.ascontiguousarray(mask.T).astype(ml_dtypes.bfloat16)
    ones1 = np.ones((1, 128), dtype=np.float32)

    common = dict(wattn=wattn, battn=battn, wproj=wproj, bproj=bprojv,
                  identb=identb, identf=identf, mask=mask, masktb=masktb, ones1=ones1)
    in_maps = []
    for b in range(8):
        m = dict(common)
        m["x"] = np.ascontiguousarray(x[b])
        m["gu_pkt"] = np.ascontiguousarray(gu_pkt_f[:, :, b * 1408:(b + 1) * 1408])
        m["gu_sc"] = np.ascontiguousarray(gu_sc_f[:, :, b * 1408:(b + 1) * 1408])
        m["dn_pkt"] = np.ascontiguousarray(dn_pkt_f[:, b * 256:(b + 1) * 256])
        m["dn_sc"] = np.ascontiguousarray(dn_sc_f[:, b * 256:(b + 1) * 256])
        in_maps.append(m)
    return in_maps


def kernel(**inputs):
    in_maps = prep_inputs(**inputs)
    nc = build_program(dbg=False)
    res = run_bass_kernel_spmd(nc, in_maps, core_ids=list(range(8)))
    out = np.stack([np.asarray(r["y"], dtype=np.float32) for r in res.results], axis=0)
    return out


# version 17
# speedup vs baseline: 1.1208x; 1.1208x over previous
import math

import numpy as np
import ml_dtypes

import concourse.bass as bass
import concourse.mybir as mybir
from concourse.bass_utils import run_bass_kernel_spmd

F32 = mybir.dt.float32
BF16 = mybir.dt.bfloat16
F16 = mybir.dt.float16
I32 = mybir.dt.int32
AF = mybir.ActivationFunctionType
ALU = mybir.AluOpType
AX = mybir.AxisListType

T = 2048
C = 2048
NT = 16
NK = 16
NH = 16
HS = 128
I_FFN = 5632
GU_COLS = 2 * I_FFN
CSC = 1.0 / math.sqrt(HS)
NEG = -1.0e30
KB = 1024


class Ctr:
    def __init__(self, h):
        self.h = h
        self.n = 0

    def bump(self, k):
        self.n += k
        return self.n


def build_program(dbg=False, max_phase=9):
    nc = bass.Bass("TRN2", target_bir_lowering=False)

    x_in = nc.dram_tensor("x", [T, C], F32, kind="ExternalInput").ap()
    wattn = nc.dram_tensor("wattn", [48, 128, 2048], F32, kind="ExternalInput").ap()
    battn = nc.dram_tensor("battn", [1, 6144], F32, kind="ExternalInput").ap()
    wproj = nc.dram_tensor("wproj", [16, 128, 2048], F32, kind="ExternalInput").ap()
    bproj = nc.dram_tensor("bproj", [1, 2048], F32, kind="ExternalInput").ap()
    gu_pkt = nc.dram_tensor("gu_pkt", [2, 128, 1408], I32, kind="ExternalInput").ap()
    gu_sc = nc.dram_tensor("gu_sc", [2, 128, 1408], F16, kind="ExternalInput").ap()
    dn_pkt = nc.dram_tensor("dn_pkt", [704, 256], I32, kind="ExternalInput").ap()
    dn_sc = nc.dram_tensor("dn_sc", [704, 256], F16, kind="ExternalInput").ap()
    identb_in = nc.dram_tensor("identb", [128, 128], BF16, kind="ExternalInput").ap()
    identf_in = nc.dram_tensor("identf", [128, 128], F32, kind="ExternalInput").ap()
    mask_in = nc.dram_tensor("mask", [128, 128], F32, kind="ExternalInput").ap()
    masktb_in = nc.dram_tensor("masktb", [128, 128], BF16, kind="ExternalInput").ap()
    ones_in = nc.dram_tensor("ones1", [1, 128], F32, kind="ExternalInput").ap()

    y_out = nc.dram_tensor("y", [T, C], F32, kind="ExternalOutput").ap()

    skind = "ExternalOutput" if dbg else "Internal"
    qkT_d = nc.dram_tensor("qkT_d", [4096, T], BF16, kind=skind).ap()
    v_d = nc.dram_tensor("v_d", [NH, 128, T], BF16, kind=skind).ap()
    oT_d = nc.dram_tensor("oT_d", [C, T], BF16, kind=skind).ap()
    x1_d = nc.dram_tensor("x1_d", [T, C], F32, kind=skind).ap()
    wgu_sh = nc.dram_tensor("wgu_sh", [C, 1408], BF16, kind="Internal").ap()
    wdn_sh = nc.dram_tensor("wdn_sh", [I_FFN, 256], BF16, kind="Internal").ap()
    wgu_gd = nc.dram_tensor("wgu_gd", [8 * C, 1408], BF16, kind="Internal", addr_space="Shared").ap()
    wdn_gd = nc.dram_tensor("wdn_gd", [8 * I_FFN, 256], BF16, kind="Internal", addr_space="Shared").ap()
    hT_d = nc.dram_tensor("hT_d", [I_FFN, T], BF16, kind=skind).ap()

    ops = {k: [] for k in ("sync", "tensor", "vector", "scalar", "gpsimd")}
    EM = {k: ops[k].append for k in ops}

    def W(eng, ctr, val):
        if val and val > 0:
            EM[eng](lambda e, h=ctr.h, v=val: e.wait_ge(h, v))

    def DMA(eng, dst, src, ctr):
        n = ctr.bump(16)
        EM[eng](lambda e, d=dst, s=src, h=ctr.h: e.dma_start(d, s).then_inc(h, 16))
        return n

    def INC(eng, ctr, k=1):
        n = ctr.bump(k)
        if eng in ("vector", "scalar", "gpsimd"):
            EM[eng](lambda e, h=ctr.h, v=k: e.drain().then_inc(h, v))
        else:
            EM[eng](lambda e, h=ctr.h, v=k: e.nop().then_inc(h, v))
        return n

    def MM(psum_ap, lhsT, rhs, start, stop, inc=None):
        if inc is None:
            EM["tensor"](lambda e, o=psum_ap, l=lhsT, r=rhs, st=start, sp=stop:
                         e.matmul(o, l, r, start=st, stop=sp))
            return None
        n = inc.bump(1)
        EM["tensor"](lambda e, o=psum_ap, l=lhsT, r=rhs, st=start, sp=stop, h=inc.h:
                     e.matmul(o, l, r, start=st, stop=sp).then_inc(h, 1))
        return n

    with (
        nc.sbuf_tensor("arena", [128, 100 * KB], BF16) as arena_t,
        nc.psum_tensor("psA", [128, 2048], F32) as psA_t,
        nc.psum_tensor("psB", [128, 2048], F32) as psB_t,
        nc.semaphore("lds") as lds_h,
        nc.semaphore("ldg") as ldg_h,
        nc.semaphore("sts") as sts_h,
        nc.semaphore("stg") as stg_h,
        nc.semaphore("pe") as pe_h,
        nc.semaphore("dv") as dv_h,
        nc.semaphore("ac") as ac_h,
        nc.semaphore("cc") as cc_h,
        nc.semaphore("wma") as wma_h,
        nc.semaphore("wmb") as wmb_h,
        nc.semaphore("wda") as wda_h,
        nc.semaphore("wdb") as wdb_h,
        nc.Block() as block,
    ):
        arena = arena_t.ap()
        psA = psA_t.ap()
        psB = psB_t.ap()
        lds = Ctr(lds_h); ldg = Ctr(ldg_h); sts = Ctr(sts_h); stg = Ctr(stg_h)
        pe = Ctr(pe_h); dv = Ctr(dv_h); ac = Ctr(ac_h); cc = Ctr(cc_h)
        wms = [Ctr(wma_h), Ctr(wmb_h)]
        wds = [Ctr(wda_h), Ctr(wdb_h)]

        def cv(off_bytes, shape, dtype):
            assert off_bytes % 64 == 0
            nel = int(np.prod(shape))
            esz = {F32: 4, BF16: 2, F16: 2, I32: 4}[dtype]
            o = off_bytes // 2
            nbf = nel * esz // 2
            assert off_bytes + nel * esz <= 200 * KB, (off_bytes, nel, esz)
            apx = arena[:, o:o + nbf]
            if dtype != BF16:
                apx = apx.bitcast(dtype)
            if len(shape) == 1:
                return apx
            pat = "p (" + " ".join(f"d{i}" for i in range(len(shape))) + ") -> p " + \
                  " ".join(f"d{i}" for i in range(len(shape)))
            kw = {f"d{i}": s for i, s in enumerate(shape)}
            return apx.rearrange(pat, **kw)

        identb = cv(0x0000, [128], BF16)
        masktb = cv(0x2C00, [128], BF16)
        identf = cv(0x0100, [128], F32)
        mask_sb = cv(0x0300, [128], F32)
        battn_sb = cv(0x0500, [64], F32)
        battnq_sb = cv(0x0600, [16], F32)
        bpbc = cv(0x0700, [2048], F32)
        nmax_sb = cv(0x2700, [16], F32)
        rsum_sb = cv(0x2740, [16], F32)
        rinv_sb = cv(0x2780, [16], F32)
        s_sum = cv(0x27C0, [1], F32)
        s_nm = cv(0x2800, [1], F32)
        s_varp = cv(0x2840, [1], F32)
        s_rec = cv(0x2880, [1], F32)
        s_std = cv(0x2B80, [1], F32)
        eps5 = cv(0x2D00, [1], F32)
        eps6 = cv(0x2D40, [1], F32)
        s_rstd = cv(0x28C0, [1], F32)
        s_nmr = cv(0x2900, [1], F32)
        onesf = cv(0x2940, [128], F32)
        TB = 0x3400

        x_tiles = x_in.rearrange("(i p) c -> i p c", p=128)
        x1_tiles = x1_d.rearrange("(i p) c -> i p c", p=128)
        y_tiles = y_out.rearrange("(i p) c -> i p c", p=128)

        brow = cv(TB, [6144], F32)
        bprow = cv(TB + 24 * KB, [2048], F32)

        DMA("sync", identb, identb_in, lds)
        DMA("sync", identf, identf_in, lds)
        DMA("sync", mask_sb, mask_in, lds)
        DMA("sync", brow[0:1, :], battn, lds)
        DMA("sync", bprow[0:1, :], bproj, lds)
        DMA("sync", masktb, masktb_in, lds)
        n_ld0 = DMA("sync", onesf[0:1, :], ones_in, lds)

        W("tensor", lds, n_ld0)
        for m in range(48):
            MM(psA[:, m:m + 1], brow[0:1, bass.ts(m, 128)], onesf[0:1, 0:1], True, True)
        for nb in range(4):
            n_pe0 = MM(psB[:, bass.ts(nb, 512)], onesf[0:1, :], bprow[0:1, bass.ts(nb, 512)],
                       True, True, inc=pe if nb == 3 else None)

        EM["vector"](lambda e: e.memset(eps5[:], 1e-5))
        EM["vector"](lambda e: e.memset(eps6[:], 1e-6))
        W("vector", pe, n_pe0)
        EM["vector"](lambda e: e.tensor_copy(battn_sb[:, 0:48], psA[:, 0:48]))
        EM["vector"](lambda e: e.tensor_copy(bpbc[:], psB[:]))
        EM["vector"](lambda e: e.drain())
        EM["vector"](lambda e: e.tensor_scalar(battnq_sb[:], battn_sb[:, 0:16], CSC, None, ALU.mult))
        dv_p0 = INC("vector", dv)

        DQ = TB + 100 * KB
        dq_w = cv(DQ, [2, 1408], I32)
        dq_sc = cv(DQ + 12 * KB, [2, 1408], BF16)
        dq_mi = cv(DQ + 18 * KB, [1408], I32)
        dq_si = cv(DQ + 24 * KB, [1408], I32)
        dq_m = cv(DQ + 30 * KB, [1408], BF16)
        dq_sg = cv(DQ + 33 * KB, [1408], BF16)
        dq_r1 = cv(DQ + 36 * KB, [1408], BF16)
        dq_r2 = cv(DQ + 39 * KB, [1408], BF16)
        dq_t0 = cv(DQ + 42 * KB, [1408], BF16)
        dq_mag = cv(DQ + 45 * KB, [1408], BF16)
        dq_wv = cv(DQ + 48 * KB, [2, 1408], BF16)

        wgu_sc8 = wgu_sh.rearrange("(a r e) c -> a r e c", a=2, e=8)
        wdn_sc8 = wdn_sh.rearrange("(r e) c -> r e c", e=8)

        dqjobs = [("gu", cc_) for cc_ in range(2)] + [("dn", cc_) for cc_ in range(6)]
        dqstate = {"it": 0, "panel": -1, "ld_done": {}, "panels": [], "done": False,
                   "cc_emitted": False}
        ld_dqw = []; ld_dqs = []; dv_dq_panel = []; st_dq = []

        def dq_geom(pidx):
            kind, cc_ = dqjobs[pidx]
            if kind == "gu":
                cols, npart = 1408, 128
                wsrc = gu_pkt[cc_]
                ssrc = gu_sc[cc_]
            else:
                cols = 256
                npart = 128 if cc_ < 5 else 64
                wsrc = dn_pkt[cc_ * 128:cc_ * 128 + npart, :]
                ssrc = dn_sc[cc_ * 128:cc_ * 128 + npart, :]
            return kind, cc_, cols, npart, wsrc, ssrc

        def emit_dq_load(pidx):
            kind, cc_, cols, npart, wsrc, ssrc = dq_geom(pidx)
            pj = pidx % 2
            if pidx >= 2:
                W("sync", dv, dv_dq_panel[pidx - 2])
                W("gpsimd", dv, dv_dq_panel[pidx - 2])
            ld_dqw.append(DMA("sync", dq_w[0:npart, pj, 0:cols], wsrc, lds))
            ld_dqs.append(DMA("gpsimd", dq_sc[0:npart, pj, 0:cols], ssrc, ldg))

        def emit_dq_piece():
            st = dqstate
            if st["done"]:
                return
            it = st["it"]
            pidx, j = divmod(it, 8)
            if pidx >= len(dqjobs):
                st["done"] = True
                return
            if j == 0:
                if pidx == 0:
                    emit_dq_load(0)
                if pidx + 1 < len(dqjobs):
                    emit_dq_load(pidx + 1)
            kind, cc_, cols, npart, wsrc, ssrc = dq_geom(pidx)
            pj = pidx % 2
            W("vector", lds, lds.n)
            W("vector", ldg, ldg.n)
            wv_ = dq_w[0:npart, pj, 0:cols]
            sc_ = dq_sc[0:npart, pj, 0:cols]
            wb = dq_wv[0:npart, it % 2, 0:cols]
            mi = dq_mi[0:npart, 0:cols]; si = dq_si[0:npart, 0:cols]
            mg = dq_m[0:npart, 0:cols]; sg = dq_sg[0:npart, 0:cols]
            r1 = dq_r1[0:npart, 0:cols]; r2 = dq_r2[0:npart, 0:cols]
            t0 = dq_t0[0:npart, 0:cols]; mag = dq_mag[0:npart, 0:cols]
            EM["vector"](lambda e, w=wv_, j=j, o=mi:
                         e.tensor_scalar(o, w, 4 * j, 7, ALU.logical_shift_right, ALU.bitwise_and))
            EM["vector"](lambda e, w=wv_, j=j, o=si:
                         e.tensor_scalar(o, w, 4 * j + 3, 1, ALU.logical_shift_right, ALU.bitwise_and))
            EM["vector"](lambda e: e.drain())
            EM["vector"](lambda e, i=mi, o=mg: e.tensor_copy(o, i))
            EM["vector"](lambda e, i=si, o=sg: e.tensor_scalar(o, i, -2.0, 1.0, ALU.mult, ALU.add))
            EM["vector"](lambda e: e.drain())
            EM["vector"](lambda e, i=mg, o=r1: e.tensor_scalar(o, i, 4.0, -4.0, ALU.max, ALU.add))
            EM["vector"](lambda e, i=mg, o=r2: e.tensor_scalar(o, i, 6.0, -6.0, ALU.max, ALU.add))
            EM["vector"](lambda e: e.drain())
            EM["vector"](lambda e, i=mg, r=r2, o=t0:
                         e.scalar_tensor_tensor(o, i, 0.5, r, ALU.mult, ALU.add))
            EM["vector"](lambda e: e.drain())
            EM["vector"](lambda e, i=r1, r=t0, o=mag:
                         e.scalar_tensor_tensor(o, i, 0.5, r, ALU.mult, ALU.add))
            EM["vector"](lambda e: e.drain())
            EM["vector"](lambda e, a=mag, b=sg: e.tensor_tensor(a, a, b, ALU.mult))
            EM["vector"](lambda e: e.drain())
            if it >= 2:
                W("vector", stg, stg.n)
            EM["vector"](lambda e, a=mag, b=sc_, o=wb: e.tensor_tensor(o, a, b, ALU.mult))
            ndv = INC("vector", dv)
            if j == 7:
                dv_dq_panel.append(ndv)
            W("gpsimd", dv, ndv)
            if kind == "gu":
                dst = wgu_sc8[cc_, :, j, :]
            else:
                dst = wdn_sc8[:, j, :][cc_ * 128:cc_ * 128 + npart, :]
            st_dq.append(DMA("gpsimd", dst, wb, stg))
            st["it"] = it + 1
            if st["it"] == len(dqjobs) * 8:
                st["done"] = True

        def emit_dq_collective():
            if dqstate["cc_emitted"]:
                return
            while not dqstate["done"]:
                emit_dq_piece()
            W("gpsimd", stg, stg.n)
            EM["gpsimd"](lambda e: e.collective_compute(
                "AllGather", ALU.bypass, replica_groups=[list(range(8))],
                ins=[wgu_sh.opt()], outs=[wgu_gd.opt()]).then_inc(cc_h, 1))
            cc.bump(1)
            EM["gpsimd"](lambda e: e.collective_compute(
                "AllGather", ALU.bypass, replica_groups=[list(range(8))],
                ins=[wdn_sh.opt()], outs=[wdn_gd.opt()]).then_inc(cc_h, 1))
            dqstate["cc_emitted"] = True
            return cc.bump(1)


        if max_phase >= 1:
            XT = TB
            xT = cv(XT, [16, 2048], BF16)
            P1B = TB + 64 * KB
            xbf = cv(P1B, [2, 2048], BF16)

            pe_x = []; dv_tx = []; ld_x = []
            for i in range(NT):
                if i < 2:
                    if i == 0:
                        W("gpsimd", pe, n_pe0)
                    ld_x.append(DMA("gpsimd", xbf[:, i % 2, :], x_tiles[i], ldg))
                W("tensor", ldg, ldg.n)
                if i == 0:
                    W("tensor", dv, dv_p0)
                else:
                    W("tensor", dv, dv_tx[i - 1])
                for f in range(NK):
                    n_ = MM(psA[:, bass.ts(f, 128)], xbf[:, i % 2, bass.ts(f, 128)], identb,
                            True, True, inc=pe if f == NK - 1 else None)
                pe_x.append(n_)
                W("vector", pe, pe_x[i])
                EM["vector"](lambda e, i=i: e.tensor_copy(xT[:, :, bass.ts(i, 128)], psA[:]))
                dv_tx.append(INC("vector", dv))
                if i + 2 < NT:
                    W("gpsimd", pe, pe_x[i])
                    ld_x.append(DMA("gpsimd", xbf[:, i % 2, :], x_tiles[i + 2], ldg))

        if max_phase >= 2:
            wm = cv(P1B + 8 * KB, [2, 2048], BF16)
            qkvrow = cv(P1B + 16 * KB, [2, 2048], BF16)
            vtile = cv(P1B + 24 * KB, [2048], BF16)

            pe_qkv = []; ac_qkv = []; dv_v = []; st_qk = {}; st_v = []; pe_v = []
            ld_w2 = {}
            ps2 = [psA, psB]

            def emit_tr_v(mv):
                h = mv - 32
                W("tensor", ac, ac_qkv[mv])
                for t in range(NT):
                    n_ = MM(ps2[mv % 2][:, bass.ts(t, 128)],
                            qkvrow[:, mv % 2, bass.ts(t, 128)], identb,
                            True, True, inc=pe if t == NT - 1 else None)
                pe_v.append(n_)
                W("vector", pe, pe_v[h])
                if h >= 1:
                    W("vector", sts, sts.n)
                EM["vector"](lambda e: e.tensor_copy(vtile[:], ps2[mv % 2][:]))
                dv_v.append(INC("vector", dv))
                W("sync", dv, dv_v[h])
                st_v.append(DMA("sync", v_d[h], vtile[:], sts))

            for m in range(49):
                if m < 2:
                    if m == 0:
                        W("gpsimd", pe, pe_x[NT - 1])
                    ld_w2[m] = DMA("gpsimd", wm[:, m % 2, :], wattn[m], wms[m % 2])
                if m <= 47:
                    W("tensor", wms[m % 2], ld_w2[m])
                    if m == 0:
                        W("tensor", dv, dv_tx[NT - 1])
                    if m >= 2:
                        W("tensor", ac, ac_qkv[m - 2])
                        if m - 2 >= 32:
                            W("tensor", dv, dv_v[m - 34])
                    for nb in range(4):
                        for k in range(NK):
                            n_ = MM(ps2[m % 2][:, bass.ts(nb, 512)], wm[:, m % 2, bass.ts(k, 128)],
                                    xT[:, k, bass.ts(nb, 512)], k == 0, k == NK - 1,
                                    inc=pe if (nb == 3 and k == NK - 1) else None)
                    pe_qkv.append(n_)
                if m >= 1 and m - 1 >= 32:
                    emit_tr_v(m - 1)
                if m <= 47:
                    W("scalar", pe, pe_qkv[m])
                    if m >= 2:
                        mp = m - 2
                        if mp < 32:
                            W("scalar", sts, sts.n)
                        else:
                            W("scalar", pe, pe_v[mp - 32])
                    if m < 16:
                        EM["scalar"](lambda e, m=m: e.activation(
                            qkvrow[:, m % 2, :], ps2[m % 2][:], AF.Identity,
                            bias=battnq_sb[:, m:m + 1], scale=CSC))
                    else:
                        EM["scalar"](lambda e, m=m: e.activation(
                            qkvrow[:, m % 2, :], ps2[m % 2][:], AF.Identity,
                            bias=battn_sb[:, m:m + 1], scale=1.0))
                    ac_qkv.append(INC("scalar", ac))
                    if m < 32:
                        W("sync", ac, ac_qkv[m])
                        st_qk[m] = DMA("sync", qkT_d[bass.ts(m, 128), :], qkvrow[:, m % 2, :], sts)
                    if m + 2 < 48:
                        W("gpsimd", pe, pe_qkv[m])
                        ld_w2[m + 2] = DMA("gpsimd", wm[:, m % 2, :], wattn[m + 2], wms[m % 2])
                    if max_phase >= 3:
                        emit_dq_piece()

            pe_p2_done = pe.n
            sts_p2_done = sts.n

        if max_phase >= 3:
            cc_done = emit_dq_collective()
            dv_dq_done = dv.n
            stg_dq_done = stg.n

        if max_phase >= 4:
            AT = TB
            qh = cv(AT, [2, 2048], BF16)
            kh = cv(AT + 8 * KB, [2, 2048], BF16)
            vh = cv(AT + 16 * KB, [2, 2048], BF16)
            pn = cv(AT + 24 * KB, [2, 2048], BF16)
            PT = cv(AT + 32 * KB, [16, 512], BF16)
            orow = cv(AT + 48 * KB, [2, 2048], BF16)

            pe_head = []; st_o = []; ld_h4 = []
            oT5 = cv(TB + 56 * KB, [16, 2048], BF16)
            wp5 = cv(TB + 120 * KB, [16, 2048], BF16)

            def emit_head_loads(h):
                if h == 0:
                    W("sync", dv, dv_dq_done)
                    W("sync", stg, stg_dq_done)
                DMA("sync", qh[:, h % 2, :], qkT_d[bass.ts(h, 128), :], lds)
                DMA("sync", kh[:, h % 2, :], qkT_d[2048 + h * 128:2176 + h * 128, :], lds)
                ld_h4.append(DMA("sync", vh[:, h % 2, :], v_d[h], lds))

            emit_head_loads(0)
            emit_head_loads(1)

            ac_exp = {}; dv_ptev = {}; ac_oev = []
            ps4 = [psA, psB]
            last_oev = 0
            n_s = {}; n_pt = {}; n_red = {}; n_recip = {}; n_nrm = {}
            for h in range(NH):
                hp = h % 2
                W("tensor", lds, lds.n)
                for t in range(17):
                    tl = t - 1
                    if t <= 15:
                        qt = t
                        Lk = (qt + 1) * 128
                        par = qt % 2
                        P = ps4[par]
                        prev = dv_ptev.get((h, qt - 2))
                        if prev is None and qt < 2:
                            prev = dv_ptev.get((h - 1, 14 + par))
                        if prev:
                            W("tensor", dv, prev)
                        elif h == 0 and qt < 2:
                            W("tensor", dv, dv.n)
                            W("tensor", ac, ac.n)
                        if par == 1 and Lk > 1536 and last_oev:
                            W("tensor", ac, last_oev)
                        nnb = (Lk + 511) // 512
                        for nb in range(nnb):
                            N = min(512, Lk - nb * 512)
                            lastc = nb == nnb - 1
                            MM(P[:, nb * 512:nb * 512 + N], qh[:, hp, bass.ts(qt, 128)],
                               kh[:, hp, nb * 512:nb * 512 + N], True, not lastc)
                            if lastc:
                                n_s[t] = MM(P[:, bass.ts(qt, 128)], masktb, identb,
                                            False, True, inc=pe)
                    if tl >= 0:
                        parl = tl % 2
                        Pl = ps4[parl]
                        Lkl = (tl + 1) * 128
                        W("vector", ac, ac_exp[(h, tl)])
                        EM["vector"](lambda e, qt=tl: e.reciprocal(
                            rinv_sb[:, qt:qt + 1], rsum_sb[:, qt:qt + 1]))
                        EM["vector"](lambda e: e.drain())
                        EM["vector"](lambda e, L=Lkl, qt=tl, par=parl: e.tensor_scalar(
                            pn[:, par, 0:L], pn[:, par, 0:L], rinv_sb[:, qt:qt + 1],
                            None, ALU.mult))
                        n_nrm[tl] = INC("vector", dv)
                        W("tensor", dv, n_nrm[tl])
                        nkt = tl + 1
                        for kt in range(nkt):
                            n_pt[tl] = MM(Pl[:, bass.ts(kt, 128)], pn[:, parl, bass.ts(kt, 128)],
                                          identb, True, True, inc=pe if kt == nkt - 1 else None)
                    if t <= 15:
                        W("vector", pe, n_s[t])
                        EM["vector"](lambda e, L=Lk, qt=t, P=P: e.reduce_max(
                            nmax_sb[:, qt:qt + 1], P[:, 0:L], axis=AX.X, negate=True))
                        n_red[t] = INC("vector", dv)
                    if tl >= 0:
                        W("vector", pe, n_pt[tl])
                        EM["vector"](lambda e, nkt=tl + 1, q4=tl % 4, P=Pl: e.tensor_copy(
                            PT[:, 0:nkt, bass.ts(q4, 128)],
                            P[:, 0:nkt * 128].rearrange("p (a b) -> p a b", b=128)))
                        dv_ptev[(h, tl)] = INC("vector", dv)
                    if t <= 15:
                        W("scalar", dv, n_red[t])
                        EM["scalar"](lambda e, L=Lk, qt=t, par=par, P=P: e.activation(
                            pn[:, par, 0:L], P[:, 0:L], AF.Exp,
                            bias=nmax_sb[:, qt:qt + 1], scale=1.0,
                            accum_out=rsum_sb[:, qt:qt + 1]))
                        ac_exp[(h, t)] = INC("scalar", ac)
                    if t >= 4 and t % 4 == 0:
                        qg = t // 4 - 1
                        for r in range(1, 4):
                            kt = qg * 4 + r
                            EM["vector"](lambda e, kt=kt, r=r: e.memset(PT[:, kt, 0:r * 128], 0.0))
                        n_z = INC("vector", dv)
                        W("tensor", dv, n_z)
                        W("tensor", ac, ac_exp[(h, qg * 4 + 3)])
                        nkt_g = qg * 4 + 4
                        for kt in range(nkt_g):
                            n_pv = MM(psB[:, 1536:2048], vh[:, hp, bass.ts(kt, 128)], PT[:, kt, :],
                                      kt == 0, kt == nkt_g - 1,
                                      inc=pe if kt == nkt_g - 1 else None)
                        W("scalar", pe, n_pv)
                        if qg == 0 and h >= 2:
                            W("scalar", sts, sts.n)
                        EM["scalar"](lambda e, qg=qg, hp=hp: e.activation(
                            orow[:, hp, bass.ts(qg, 512)], psB[:, 1536:2048], AF.Identity,
                            bias=0.0, scale=1.0))
                        n_oev = INC("scalar", ac)
                        last_oev = n_oev
                        if qg == 3:
                            ac_oev.append(n_oev)
                pe_head.append(pe.n)
                W("sync", ac, ac_oev[h])
                st_o.append(DMA("sync", oT_d[bass.ts(h, 128), :], orow[:, h % 2, :], sts))
                W("sync", sts, sts.n)
                ld_p5oT = DMA("sync", oT5[:, h, :], oT_d[bass.ts(h, 128), :], lds)
                if h == 0:
                    for k5 in range(NK):
                        DMA("gpsimd", wp5[:, k5, :], wproj[k5], ldg)
                if h + 2 < NH:
                    W("sync", pe, pe_head[h])
                    emit_head_loads(h + 2)

            sts_attn_done = sts.n
            pe_attn_done = pe.n

        if max_phase >= 5:
            P5 = TB + 56 * KB
            oT = cv(P5, [16, 2048], BF16)
            wp = cv(TB + 120 * KB, [16, 2048], BF16)
            P5S = TB
            xres = cv(P5S, [2, 2048], F32)
            rsb = cv(P5S + 16 * KB, [2048], F32)
            sqs = cv(P5S + 24 * KB, [2048], F32)
            x1sb = cv(P5S + 32 * KB, [2, 2048], F32)

            W("sync", sts, sts_attn_done)
            W("sync", pe, pe_attn_done)
            ld_xr = {0: DMA("sync", xres[:, 0, :], x_tiles[0], lds),
                     1: DMA("sync", xres[:, 1, :], x_tiles[1], lds)}

            pe_p5 = []; dv_p5 = {}; ac_p5 = {}; st_x1 = []
            ps5 = [psA, psB]
            W("tensor", lds, lds.n)
            W("tensor", ldg, ldg.n)
            W("tensor", sts, sts_attn_done)
            for i in range(NT):
                if i <= 1:
                    W("tensor", ac, ac.n)
                    W("tensor", dv, dv.n)
                else:
                    W("tensor", dv, dv_p5[(i - 2, "bf")])
                for nb in range(4):
                    for k in range(NK):
                        n_ = MM(ps5[i % 2][:, bass.ts(nb, 512)], oT[:, k, bass.ts(i, 128)],
                                wp[:, k, bass.ts(nb, 512)], k == 0, k == NK - 1,
                                inc=pe if (nb == 3 and k == NK - 1) else None)
                pe_p5.append(n_)
                W("vector", pe, pe_p5[i])
                W("vector", lds, lds.n)
                if i >= 1:
                    W("vector", ac, ac_p5[(i - 1, "x1")])
                EM["vector"](lambda e, i=i: e.scalar_tensor_tensor(rsb[:], ps5[i % 2][:], 1.0, bpbc[:], ALU.mult, ALU.add))
                dv_p5[(i, "bf")] = INC("vector", dv)
                EM["vector"](lambda e, i=i: e.tensor_tensor(rsb[:], rsb[:], xres[:, i % 2, :], ALU.add))
                EM["vector"](lambda e: e.drain())
                EM["vector"](lambda e: e.reduce_sum(s_sum[:], rsb[:], axis=AX.X))
                EM["vector"](lambda e: e.drain())
                EM["vector"](lambda e: e.tensor_scalar(s_nm[:], s_sum[:], -1.0 / 2048.0, None, ALU.mult))
                dv_p5[(i, "r")] = INC("vector", dv)
                W("scalar", dv, dv_p5[(i, "r")])
                EM["scalar"](lambda e: e.activation(sqs[:], rsb[:], AF.Square, bias=s_nm[:, 0:1],
                                                    scale=1.0, accum_out=s_varp[:, 0:1]))
                EM["scalar"](lambda e: e.drain())
                EM["scalar"](lambda e: e.activation(s_std[:], s_varp[:], AF.Sqrt,
                                                    bias=eps5[:, 0:1], scale=1.0 / 2048.0))
                ac_p5[(i, "sq")] = INC("scalar", ac)
                W("vector", ac, ac_p5[(i, "sq")])
                EM["vector"](lambda e: e.reciprocal(s_rstd[:], s_std[:]))
                EM["vector"](lambda e: e.drain())
                EM["vector"](lambda e: e.tensor_tensor(s_nmr[:], s_nm[:], s_rstd[:], ALU.mult))
                dv_p5[(i, "n")] = INC("vector", dv)
                W("scalar", dv, dv_p5[(i, "n")])
                if i >= 2:
                    W("scalar", sts, sts.n)
                EM["scalar"](lambda e, i=i: e.activation(x1sb[:, i % 2, :], rsb[:], AF.Identity,
                                                         bias=s_nmr[:, 0:1], scale=s_rstd[:, 0:1]))
                ac_p5[(i, "x1")] = INC("scalar", ac)
                W("sync", ac, ac_p5[(i, "x1")])
                st_x1.append(DMA("sync", x1_tiles[i], x1sb[:, i % 2, :], sts))
                if i + 2 < NT:
                    W("sync", dv, dv_p5[(i, "r")])
                    ld_xr[i + 2] = DMA("sync", xres[:, i % 2, :], x_tiles[i + 2], lds)
            sts_x1_done = sts.n
            pe_p5_done = pe.n

        if max_phase >= 6:
            x1T = cv(XT, [16, 2048], BF16)
            P5B = TB + 64 * KB
            x1bf = cv(P5B, [2, 2048], BF16)

            pe_x1 = []; dv_tx1 = []; ld_x1 = []
            for i in range(NT):
                if i < 2:
                    if i == 0:
                        W("gpsimd", sts, sts_x1_done)
                        W("gpsimd", pe, pe_p5_done)
                    ld_x1.append(DMA("gpsimd", x1bf[:, i % 2, :], x1_tiles[i], ldg))
                W("tensor", ldg, ldg.n)
                if i == 0:
                    W("tensor", dv, dv.n)
                    W("tensor", ac, ac.n)
                else:
                    W("tensor", dv, dv_tx1[i - 1])
                for f in range(NK):
                    n_ = MM(psA[:, bass.ts(f, 128)], x1bf[:, i % 2, bass.ts(f, 128)], identb,
                            True, True, inc=pe if f == NK - 1 else None)
                pe_x1.append(n_)
                W("vector", pe, pe_x1[i])
                EM["vector"](lambda e, i=i: e.tensor_copy(x1T[:, :, bass.ts(i, 128)], psA[:]))
                dv_tx1.append(INC("vector", dv))
                if i + 2 < NT:
                    W("gpsimd", pe, pe_x1[i])
                    ld_x1.append(DMA("gpsimd", x1bf[:, i % 2, :], x1_tiles[i + 2], ldg))

        if max_phase >= 7:
            P6 = TB + 64 * KB
            wg6 = cv(P6, [16, 1408], BF16)
            wu6 = cv(P6 + 45 * KB, [16, 1408], BF16)
            sgt = cv(P6 + 90 * KB, [2048], BF16)
            hrow = cv(P6 + 94 * KB, [2, 2048], BF16)

            wgu_rk = wgu_gd.rearrange("(r k p) c -> r k p c", r=8, p=128)
            hT_rows = hT_d.rearrange("(k p) t -> k p t", p=128)
            pe_p6 = {}; st_h6 = []
            ac_p6s = None; dv_p6h = None
            for a in range(4):
                ld_g = 0
                for k in range(NK):
                    if k == 0:
                        if a == 0:
                            W("sync", pe, pe.n)
                        else:
                            W("sync", pe, pe_p6[(a - 1, 10)])
                if a == 0:
                    W("sync", cc, cc_done)
                for k in range(NK):
                    ld_g = DMA("sync", wg6[:, k, :], wgu_rk[a, k], lds)
                ld_u = 0
                for k in range(NK):
                    if k == 0:
                        if a == 0:
                            W("gpsimd", pe, pe.n)
                        else:
                            W("gpsimd", pe, pe_p6[(a - 1, 10)])
                for k in range(NK):
                    ld_u = DMA("gpsimd", wu6[:, k, :], wgu_rk[4 + a, k], ldg)
                for mpl in range(11):
                    mp = a * 11 + mpl
                    W("tensor", lds, ld_g)
                    W("tensor", ldg, ld_u)
                    if mp == 0:
                        W("tensor", dv, dv.n)
                        W("tensor", ac, ac.n)
                    else:
                        W("tensor", ac, ac_p6s)
                        W("tensor", dv, dv_p6h)
                    for nb in range(4):
                        for k in range(NK):
                            n_g = MM(psA[:, bass.ts(nb, 512)], wg6[:, k, bass.ts(mpl, 128)],
                                     x1T[:, k, bass.ts(nb, 512)], k == 0, k == NK - 1,
                                     inc=pe if (nb == 3 and k == NK - 1) else None)
                    for nb in range(4):
                        for k in range(NK):
                            n_u = MM(psB[:, bass.ts(nb, 512)], wu6[:, k, bass.ts(mpl, 128)],
                                     x1T[:, k, bass.ts(nb, 512)], k == 0, k == NK - 1,
                                     inc=pe if (nb == 3 and k == NK - 1) else None)
                    W("scalar", pe, n_g)
                    if mp >= 1:
                        W("scalar", dv, dv_p6h)
                    EM["scalar"](lambda e: e.activation(sgt[:], psA[:], AF.Silu, bias=0.0, scale=1.0))
                    ac_p6s = INC("scalar", ac)
                    W("vector", ac, ac_p6s)
                    W("vector", pe, n_u)
                    if mp >= 2:
                        W("vector", sts, sts.n)
                    EM["vector"](lambda e, mp=mp: e.tensor_tensor(hrow[:, mp % 2, :], sgt[:],
                                                                  psB[:], ALU.mult))
                    dv_p6h = INC("vector", dv)
                    pe_p6[(a, mpl)] = pe.n
                    W("sync", dv, dv_p6h)
                    st_h6.append(DMA("sync", hT_rows[0].tensor.ap()
                                     .rearrange("(k p) t -> k p t", p=128)[mp] if False else
                                     hT_d[bass.ts(mp, 128), :], hrow[:, mp % 2, :], sts))
            sts_h_done = sts.n

        if max_phase >= 8:
            P7 = TB
            hT7 = cv(P7, [2, 44, 512], BF16)
            dT = cv(P7 + 88 * KB, [16, 512], F32)
            wdc = cv(P7 + 120 * KB, [2, 8, 512], BF16)
            EPI = P7 + 136 * KB
            sq7 = cv(EPI, [2048], F32)
            x1r7 = cv(EPI + 8 * KB, [2, 2048], F32)
            dsb = cv(EPI + 24 * KB, [2048], F32)
            r2sb = cv(EPI + 32 * KB, [2048], F32)
            outsb = cv(EPI + 40 * KB, [2048], F32)

            wdn_pkc8 = wdn_gd.rearrange("(r K p) c -> r p K c", r=8, p=128)
            hT_pkc = hT_d.rearrange("(K p) t -> p K t", p=128)
            accs = [psA[:, bass.ts(m4, 512)] for m4 in range(4)]

            st_y = []
            pe_dn_hist = {}
            for tg in range(4):
                tb = tg % 2
                ld_h7 = 0
                for gb in range(11):
                    if gb == 0:
                        W("gpsimd", ldg, ldg.n)
                        if tg == 0:
                            W("gpsimd", sts, sts_h_done)
                            W("gpsimd", pe, pe.n)
                            W("gpsimd", ac, ac.n)
                            W("gpsimd", dv, dv.n)
                        elif tg >= 2:
                            W("gpsimd", pe, pe_dn_hist[tg - 2])
                    ld_h7 = DMA("gpsimd", hT7[:, tb, gb * 4:gb * 4 + 4, :],
                                hT_pkc[:, gb * 4:gb * 4 + 4, bass.ts(tg, 512)], ldg)
                for mg in range(4):
                    pe_p7mm = []
                    GS = [8, 8, 8, 8, 8, 4]
                    GO = [0, 8, 16, 24, 32, 40]
                    def emit_wdc_load(g):
                        n = GS[g]
                        DMA("sync", wdc[:, g % 2, 0:n, 0:256],
                            wdn_pkc8[2 * mg, :, GO[g]:GO[g] + n, :], wds[g % 2])
                        return DMA("sync", wdc[:, g % 2, 0:n, 256:512],
                                   wdn_pkc8[2 * mg + 1, :, GO[g]:GO[g] + n, :], wds[g % 2])
                    ld_wd = {}
                    for g in range(2):
                        if g == 0:
                            W("sync", pe, pe.n)
                            if tg == 0 and mg == 0:
                                W("sync", cc, cc_done)
                                W("sync", ac, ac.n)
                                W("sync", dv, dv.n)
                        ld_wd[g] = emit_wdc_load(g)
                    for kq in range(44):
                        g = kq // 8
                        gk = kq % 8
                        if gk == 0:
                            W("tensor", wds[g % 2], ld_wd[g])
                        if kq == 0:
                            W("tensor", ldg, ldg.n)
                            if mg == 0:
                                if tg == 0:
                                    W("tensor", dv, dv.n)
                                    W("tensor", ac, ac.n)
                                else:
                                    W("tensor", dv, dv_dT)
                            if mg > 0 or tg > 0:
                                W("tensor", dv, dv_dT_last)
                        for m4 in range(4):
                            n_ = MM(accs[m4], wdc[:, g % 2, gk, bass.ts(m4, 128)],
                                    hT7[:, tb, kq, :], kq == 0, kq == 43,
                                    inc=pe if (m4 == 3) else None)
                        pe_p7mm.append(n_)
                        if gk == GS[g] - 1 and g + 2 < 6:
                            W("sync", pe, pe_p7mm[kq])
                            ld_wd[g + 2] = emit_wdc_load(g + 2)
                    W("vector", pe, pe_p7mm[-1])
                    for m4 in range(4):
                        m = mg * 4 + m4
                        EM["vector"](lambda e, m=m, m4=m4: e.tensor_copy(dT[:, m, :], accs[m4]))
                    dv_dT_last = INC("vector", dv)
                dv_dT = dv_dT_last
                pe_dn_hist[tg] = pe.n
                for il in range(4):
                    i = tg * 4 + il
                    if il < 2:
                        if il == 0 and tg == 0:
                            W("scalar", dv, dv_dT)
                        n_x1r = DMA("scalar", x1r7[:, il % 2, :], x1_tiles[i], sts)
                        if il == 0:
                            ld_x1r = {}
                        ld_x1r[il] = n_x1r
                    W("tensor", dv, dv_dT)
                    if il >= 1:
                        W("tensor", dv, dv_p7tr[il - 1])
                    elif tg > 0:
                        W("tensor", dv, dv_p7tr[3])
                        W("tensor", ac, ac.n)
                    for m in range(NK):
                        n_tr = MM(psB[:, bass.ts(m, 128)], dT[:, m, bass.ts(il, 128)], identf,
                                  True, True, inc=pe if m == NK - 1 else None)
                    if il == 0:
                        dv_p7tr = []
                    W("vector", pe, n_tr)
                    if il >= 1 or tg > 0:
                        W("vector", ac, ac_ep_last)
                    EM["vector"](lambda e: e.tensor_copy(dsb[:], psB[:]))
                    dv_p7tr.append(INC("vector", dv))
                    W("scalar", dv, dv_p7tr[il])
                    EM["scalar"](lambda e: e.activation(sq7[:], dsb[:], AF.Square, bias=0.0,
                                                        scale=1.0, accum_out=s_varp[:, 0:1]))
                    EM["scalar"](lambda e: e.drain())
                    EM["scalar"](lambda e: e.activation(s_std[:], s_varp[:], AF.Sqrt,
                                                        bias=eps6[:, 0:1], scale=1.0 / 2048.0))
                    n_ac1 = INC("scalar", ac)
                    W("vector", ac, n_ac1)
                    W("vector", sts, sts.n)
                    EM["vector"](lambda e: e.reciprocal(s_rstd[:], s_std[:]))
                    EM["vector"](lambda e: e.drain())
                    EM["vector"](lambda e, il=il: e.scalar_tensor_tensor(
                        r2sb[:], dsb[:], s_rstd[:, 0:1], x1r7[:, il % 2, :], ALU.mult, ALU.add))
                    EM["vector"](lambda e: e.drain())
                    EM["vector"](lambda e: e.reduce_sum(s_sum[:], r2sb[:], axis=AX.X))
                    EM["vector"](lambda e: e.drain())
                    EM["vector"](lambda e: e.tensor_scalar(s_nm[:], s_sum[:], -1.0 / 2048.0, None, ALU.mult))
                    n_dv2 = INC("vector", dv)
                    W("scalar", dv, n_dv2)
                    EM["scalar"](lambda e: e.activation(sq7[:], r2sb[:], AF.Square, bias=s_nm[:, 0:1],
                                                        scale=1.0, accum_out=s_varp[:, 0:1]))
                    EM["scalar"](lambda e: e.drain())
                    EM["scalar"](lambda e: e.activation(s_std[:], s_varp[:], AF.Sqrt,
                                                        bias=eps5[:, 0:1], scale=1.0 / 2048.0))
                    n_ac3 = INC("scalar", ac)
                    W("vector", ac, n_ac3)
                    EM["vector"](lambda e: e.reciprocal(s_rstd[:], s_std[:]))
                    EM["vector"](lambda e: e.drain())
                    EM["vector"](lambda e: e.tensor_tensor(s_nmr[:], s_nm[:], s_rstd[:], ALU.mult))
                    n_dv4 = INC("vector", dv)
                    W("scalar", dv, n_dv4)
                    W("scalar", sts, sts.n)
                    EM["scalar"](lambda e, il=il: e.activation(outsb[:], r2sb[:], AF.Identity,
                                                               bias=s_nmr[:, 0:1], scale=s_rstd[:, 0:1]))
                    EM["scalar"](lambda e: e.drain())
                    st_y.append(DMA("scalar", y_tiles[i], outsb[:], sts))
                    ac_ep_last = INC("scalar", ac)
                    if il + 2 < 4:
                        ld_x1r[il + 2] = DMA("scalar", x1r7[:, il % 2, :], x1_tiles[i + 2], sts)

        W("sync", sts, sts.n)
        W("sync", stg, stg.n)
        W("gpsimd", ldg, ldg.n)
        W("sync", lds, lds.n)

        @block.sync
        def _(e):
            for f in ops["sync"]:
                f(e)

        @block.tensor
        def _(e):
            for f in ops["tensor"]:
                f(e)

        @block.vector
        def _(e):
            for f in ops["vector"]:
                f(e)

        @block.scalar
        def _(e):
            for f in ops["scalar"]:
                f(e)

        @block.gpsimd
        def _(e):
            for f in ops["gpsimd"]:
                f(e)

    return nc


def prep_inputs(L0_attn_x, L0_attn_w_attn, L0_attn_b_attn, L0_attn_w_proj,
                L0_attn_b_proj, L0_ffn_gate_up_packed, L0_ffn_gate_up_scales,
                L0_ffn_down_packed, L0_ffn_down_scales):
    x = np.ascontiguousarray(np.asarray(L0_attn_x, dtype=np.float32))
    wa = np.asarray(L0_attn_w_attn, dtype=np.float32)
    wattn = np.ascontiguousarray(wa.reshape(16, 128, 48, 128).transpose(2, 1, 0, 3)
                                 .reshape(48, 128, 2048))
    battn = np.ascontiguousarray(np.asarray(L0_attn_b_attn, dtype=np.float32).reshape(1, 6144))
    wproj = np.ascontiguousarray(np.asarray(L0_attn_w_proj, dtype=np.float32).reshape(16, 128, 2048))
    bprojv = np.ascontiguousarray(np.asarray(L0_attn_b_proj, dtype=np.float32).reshape(1, 2048))
    gu_pkt_f = np.asarray(L0_ffn_gate_up_packed, dtype=np.int32).T.reshape(2, 128, GU_COLS)
    gu_sc_f = np.repeat(np.asarray(L0_ffn_gate_up_scales, dtype=np.float16),
                        16, axis=0).reshape(2, 128, GU_COLS)
    dn_pkt_f = np.asarray(L0_ffn_down_packed, dtype=np.int32).T
    dn_sc_f = np.repeat(np.asarray(L0_ffn_down_scales, dtype=np.float16), 16, axis=0)
    identb = np.eye(128, dtype=ml_dtypes.bfloat16)
    identf = np.eye(128, dtype=np.float32)
    mask = np.triu(np.full((128, 128), NEG, dtype=np.float32), k=1)
    masktb = np.ascontiguousarray(mask.T).astype(ml_dtypes.bfloat16)
    ones1 = np.ones((1, 128), dtype=np.float32)

    common = dict(wattn=wattn, battn=battn, wproj=wproj, bproj=bprojv,
                  identb=identb, identf=identf, mask=mask, masktb=masktb, ones1=ones1)
    in_maps = []
    for b in range(8):
        m = dict(common)
        m["x"] = np.ascontiguousarray(x[b])
        m["gu_pkt"] = np.ascontiguousarray(gu_pkt_f[:, :, b * 1408:(b + 1) * 1408])
        m["gu_sc"] = np.ascontiguousarray(gu_sc_f[:, :, b * 1408:(b + 1) * 1408])
        m["dn_pkt"] = np.ascontiguousarray(dn_pkt_f[:, b * 256:(b + 1) * 256])
        m["dn_sc"] = np.ascontiguousarray(dn_sc_f[:, b * 256:(b + 1) * 256])
        in_maps.append(m)
    return in_maps


def kernel(**inputs):
    in_maps = prep_inputs(**inputs)
    nc = build_program(dbg=False)
    res = run_bass_kernel_spmd(nc, in_maps, core_ids=list(range(8)))
    out = np.stack([np.asarray(r["y"], dtype=np.float32) for r in res.results], axis=0)
    return out


# version 18
# speedup vs baseline: 1.1666x; 1.0409x over previous
import math

import numpy as np
import ml_dtypes

import concourse.bass as bass
import concourse.mybir as mybir
from concourse.bass_utils import run_bass_kernel_spmd

F32 = mybir.dt.float32
BF16 = mybir.dt.bfloat16
F16 = mybir.dt.float16
I32 = mybir.dt.int32
AF = mybir.ActivationFunctionType
ALU = mybir.AluOpType
AX = mybir.AxisListType

T = 2048
C = 2048
NT = 16
NK = 16
NH = 16
HS = 128
I_FFN = 5632
GU_COLS = 2 * I_FFN
CSC = 1.0 / math.sqrt(HS)
NEG = -1.0e30
KB = 1024


class Ctr:
    def __init__(self, h):
        self.h = h
        self.n = 0

    def bump(self, k):
        self.n += k
        return self.n


def build_program(dbg=False, max_phase=9):
    nc = bass.Bass("TRN2", target_bir_lowering=False)

    x_in = nc.dram_tensor("x", [T, C], F32, kind="ExternalInput").ap()
    wattn = nc.dram_tensor("wattn", [48, 128, 2048], F32, kind="ExternalInput").ap()
    battn = nc.dram_tensor("battn", [1, 6144], F32, kind="ExternalInput").ap()
    wproj = nc.dram_tensor("wproj", [16, 128, 2048], F32, kind="ExternalInput").ap()
    bproj = nc.dram_tensor("bproj", [1, 2048], F32, kind="ExternalInput").ap()
    gu_pkt = nc.dram_tensor("gu_pkt", [2, 128, 1408], I32, kind="ExternalInput").ap()
    gu_sc = nc.dram_tensor("gu_sc", [2, 128, 1408], F16, kind="ExternalInput").ap()
    dn_pkt = nc.dram_tensor("dn_pkt", [704, 256], I32, kind="ExternalInput").ap()
    dn_sc = nc.dram_tensor("dn_sc", [704, 256], F16, kind="ExternalInput").ap()
    identb_in = nc.dram_tensor("identb", [128, 128], BF16, kind="ExternalInput").ap()
    identf_in = nc.dram_tensor("identf", [128, 128], F32, kind="ExternalInput").ap()
    mask_in = nc.dram_tensor("mask", [128, 128], F32, kind="ExternalInput").ap()
    masktb_in = nc.dram_tensor("masktb", [128, 128], BF16, kind="ExternalInput").ap()
    ones_in = nc.dram_tensor("ones1", [1, 128], F32, kind="ExternalInput").ap()

    y_out = nc.dram_tensor("y", [T, C], F32, kind="ExternalOutput").ap()

    skind = "ExternalOutput" if dbg else "Internal"
    qkT_d = nc.dram_tensor("qkT_d", [4096, T], BF16, kind=skind).ap()
    v_d = nc.dram_tensor("v_d", [NH, 128, T], BF16, kind=skind).ap()
    oT_d = nc.dram_tensor("oT_d", [C, T], BF16, kind=skind).ap()
    x1_d = nc.dram_tensor("x1_d", [T, C], F32, kind=skind).ap()
    wgu_sh = nc.dram_tensor("wgu_sh", [C, 1408], BF16, kind="Internal").ap()
    wdn_sh = nc.dram_tensor("wdn_sh", [I_FFN, 256], BF16, kind="Internal").ap()
    wgu_gd = nc.dram_tensor("wgu_gd", [8 * C, 1408], BF16, kind="Internal", addr_space="Shared").ap()
    wdn_gd = nc.dram_tensor("wdn_gd", [8 * I_FFN, 256], BF16, kind="Internal", addr_space="Shared").ap()
    hT_d = nc.dram_tensor("hT_d", [I_FFN, T], BF16, kind=skind).ap()

    ops = {k: [] for k in ("sync", "tensor", "vector", "scalar", "gpsimd")}
    EM = {k: ops[k].append for k in ops}

    def W(eng, ctr, val):
        if val and val > 0:
            EM[eng](lambda e, h=ctr.h, v=val: e.wait_ge(h, v))

    def DMA(eng, dst, src, ctr):
        n = ctr.bump(16)
        EM[eng](lambda e, d=dst, s=src, h=ctr.h: e.dma_start(d, s).then_inc(h, 16))
        return n

    def INC(eng, ctr, k=1):
        n = ctr.bump(k)
        if eng in ("vector", "scalar", "gpsimd"):
            EM[eng](lambda e, h=ctr.h, v=k: e.drain().then_inc(h, v))
        else:
            EM[eng](lambda e, h=ctr.h, v=k: e.nop().then_inc(h, v))
        return n

    def MM(psum_ap, lhsT, rhs, start, stop, inc=None):
        if inc is None:
            EM["tensor"](lambda e, o=psum_ap, l=lhsT, r=rhs, st=start, sp=stop:
                         e.matmul(o, l, r, start=st, stop=sp))
            return None
        n = inc.bump(1)
        EM["tensor"](lambda e, o=psum_ap, l=lhsT, r=rhs, st=start, sp=stop, h=inc.h:
                     e.matmul(o, l, r, start=st, stop=sp).then_inc(h, 1))
        return n

    from contextlib import ExitStack
    with (
        nc.sbuf_tensor("arena", [128, 100 * KB], BF16) as arena_t,
        nc.psum_tensor("psA", [128, 2048], F32) as psA_t,
        nc.psum_tensor("psB", [128, 2048], F32) as psB_t,
        ExitStack() as _sems,
        nc.Block() as block,
    ):
        (lds_h, ldg_h, sts_h, stg_h, pe_h, dv_h, ac_h, cc_h,
         wma_h, wmb_h, wda_h, wdb_h, hda_h, hdb_h) = [
            _sems.enter_context(nc.semaphore(nm)) for nm in
            ("lds", "ldg", "sts", "stg", "pe", "dv", "ac", "cc",
             "wma", "wmb", "wda", "wdb", "hda", "hdb")]
        arena = arena_t.ap()
        psA = psA_t.ap()
        psB = psB_t.ap()
        lds = Ctr(lds_h); ldg = Ctr(ldg_h); sts = Ctr(sts_h); stg = Ctr(stg_h)
        pe = Ctr(pe_h); dv = Ctr(dv_h); ac = Ctr(ac_h); cc = Ctr(cc_h)
        wms = [Ctr(wma_h), Ctr(wmb_h)]
        wds = [Ctr(wda_h), Ctr(wdb_h)]
        hds = [Ctr(hda_h), Ctr(hdb_h)]

        def cv(off_bytes, shape, dtype):
            assert off_bytes % 64 == 0
            nel = int(np.prod(shape))
            esz = {F32: 4, BF16: 2, F16: 2, I32: 4}[dtype]
            o = off_bytes // 2
            nbf = nel * esz // 2
            assert off_bytes + nel * esz <= 200 * KB, (off_bytes, nel, esz)
            apx = arena[:, o:o + nbf]
            if dtype != BF16:
                apx = apx.bitcast(dtype)
            if len(shape) == 1:
                return apx
            pat = "p (" + " ".join(f"d{i}" for i in range(len(shape))) + ") -> p " + \
                  " ".join(f"d{i}" for i in range(len(shape)))
            kw = {f"d{i}": s for i, s in enumerate(shape)}
            return apx.rearrange(pat, **kw)

        identb = cv(0x0000, [128], BF16)
        masktb = cv(0x2C00, [128], BF16)
        identf = cv(0x0100, [128], F32)
        mask_sb = cv(0x0300, [128], F32)
        battn_sb = cv(0x0500, [64], F32)
        battnq_sb = cv(0x0600, [16], F32)
        bpbc = cv(0x0700, [2048], F32)
        nmax_sb = cv(0x2700, [16], F32)
        rsum_sb = cv(0x2740, [16], F32)
        rinv_sb = cv(0x2780, [16], F32)
        s_sum = cv(0x27C0, [1], F32)
        s_nm = cv(0x2800, [1], F32)
        s_varp = cv(0x2840, [1], F32)
        s_rec = cv(0x2880, [1], F32)
        s_std = cv(0x2B80, [1], F32)
        eps5 = cv(0x2D00, [1], F32)
        eps6 = cv(0x2D40, [1], F32)
        s_rstd = cv(0x28C0, [1], F32)
        s_nmr = cv(0x2900, [1], F32)
        onesf = cv(0x2940, [128], F32)
        TB = 0x3400

        x_tiles = x_in.rearrange("(i p) c -> i p c", p=128)
        x1_tiles = x1_d.rearrange("(i p) c -> i p c", p=128)
        y_tiles = y_out.rearrange("(i p) c -> i p c", p=128)

        brow = cv(TB, [6144], F32)
        bprow = cv(TB + 24 * KB, [2048], F32)

        DMA("sync", identb, identb_in, lds)
        DMA("sync", identf, identf_in, lds)
        DMA("sync", mask_sb, mask_in, lds)
        DMA("sync", brow[0:1, :], battn, lds)
        DMA("sync", bprow[0:1, :], bproj, lds)
        DMA("sync", masktb, masktb_in, lds)
        n_ld0 = DMA("sync", onesf[0:1, :], ones_in, lds)

        W("tensor", lds, n_ld0)
        for m in range(48):
            MM(psA[:, m:m + 1], brow[0:1, bass.ts(m, 128)], onesf[0:1, 0:1], True, True)
        for nb in range(4):
            n_pe0 = MM(psB[:, bass.ts(nb, 512)], onesf[0:1, :], bprow[0:1, bass.ts(nb, 512)],
                       True, True, inc=pe if nb == 3 else None)

        EM["vector"](lambda e: e.memset(eps5[:], 1e-5))
        EM["vector"](lambda e: e.memset(eps6[:], 1e-6))
        W("vector", pe, n_pe0)
        EM["vector"](lambda e: e.tensor_copy(battn_sb[:, 0:48], psA[:, 0:48]))
        EM["vector"](lambda e: e.tensor_copy(bpbc[:], psB[:]))
        EM["vector"](lambda e: e.drain())
        EM["vector"](lambda e: e.tensor_scalar(battnq_sb[:], battn_sb[:, 0:16], CSC, None, ALU.mult))
        dv_p0 = INC("vector", dv)

        DQ = TB + 100 * KB
        dq_w = cv(DQ, [2, 1408], I32)
        dq_sc = cv(DQ + 12 * KB, [2, 1408], BF16)
        dq_mi = cv(DQ + 18 * KB, [1408], I32)
        dq_si = cv(DQ + 24 * KB, [1408], I32)
        dq_m = cv(DQ + 30 * KB, [1408], BF16)
        dq_sg = cv(DQ + 33 * KB, [1408], BF16)
        dq_r1 = cv(DQ + 36 * KB, [1408], BF16)
        dq_r2 = cv(DQ + 39 * KB, [1408], BF16)
        dq_t0 = cv(DQ + 42 * KB, [1408], BF16)
        dq_mag = cv(DQ + 45 * KB, [1408], BF16)
        dq_wv = cv(DQ + 48 * KB, [2, 1408], BF16)

        wgu_sc8 = wgu_sh.rearrange("(a r e) c -> a r e c", a=2, e=8)
        wdn_sc8 = wdn_sh.rearrange("(r e) c -> r e c", e=8)

        dqjobs = [("gu", cc_) for cc_ in range(2)] + [("dn", cc_) for cc_ in range(6)]
        dqstate = {"it": 0, "panel": -1, "ld_done": {}, "panels": [], "done": False,
                   "cc_emitted": False}
        ld_dqw = []; ld_dqs = []; dv_dq_panel = []; st_dq = []

        def dq_geom(pidx):
            kind, cc_ = dqjobs[pidx]
            if kind == "gu":
                cols, npart = 1408, 128
                wsrc = gu_pkt[cc_]
                ssrc = gu_sc[cc_]
            else:
                cols = 256
                npart = 128 if cc_ < 5 else 64
                wsrc = dn_pkt[cc_ * 128:cc_ * 128 + npart, :]
                ssrc = dn_sc[cc_ * 128:cc_ * 128 + npart, :]
            return kind, cc_, cols, npart, wsrc, ssrc

        def emit_dq_load(pidx):
            kind, cc_, cols, npart, wsrc, ssrc = dq_geom(pidx)
            pj = pidx % 2
            if pidx >= 2:
                W("sync", dv, dv_dq_panel[pidx - 2])
                W("gpsimd", dv, dv_dq_panel[pidx - 2])
            ld_dqw.append(DMA("sync", dq_w[0:npart, pj, 0:cols], wsrc, lds))
            ld_dqs.append(DMA("gpsimd", dq_sc[0:npart, pj, 0:cols], ssrc, ldg))

        def emit_dq_piece():
            st = dqstate
            if st["done"]:
                return
            it = st["it"]
            pidx, j = divmod(it, 8)
            if pidx >= len(dqjobs):
                st["done"] = True
                return
            if j == 0:
                if pidx == 0:
                    emit_dq_load(0)
                if pidx + 1 < len(dqjobs):
                    emit_dq_load(pidx + 1)
            kind, cc_, cols, npart, wsrc, ssrc = dq_geom(pidx)
            pj = pidx % 2
            W("vector", lds, lds.n)
            W("vector", ldg, ldg.n)
            wv_ = dq_w[0:npart, pj, 0:cols]
            sc_ = dq_sc[0:npart, pj, 0:cols]
            wb = dq_wv[0:npart, it % 2, 0:cols]
            mi = dq_mi[0:npart, 0:cols]; si = dq_si[0:npart, 0:cols]
            mg = dq_m[0:npart, 0:cols]; sg = dq_sg[0:npart, 0:cols]
            r1 = dq_r1[0:npart, 0:cols]; r2 = dq_r2[0:npart, 0:cols]
            t0 = dq_t0[0:npart, 0:cols]; mag = dq_mag[0:npart, 0:cols]
            EM["vector"](lambda e, w=wv_, j=j, o=mi:
                         e.tensor_scalar(o, w, 4 * j, 7, ALU.logical_shift_right, ALU.bitwise_and))
            EM["vector"](lambda e, w=wv_, j=j, o=si:
                         e.tensor_scalar(o, w, 4 * j + 3, 1, ALU.logical_shift_right, ALU.bitwise_and))
            EM["vector"](lambda e: e.drain())
            EM["vector"](lambda e, i=mi, o=mg: e.tensor_copy(o, i))
            EM["vector"](lambda e, i=si, o=sg: e.tensor_scalar(o, i, -2.0, 1.0, ALU.mult, ALU.add))
            EM["vector"](lambda e: e.drain())
            EM["vector"](lambda e, i=mg, o=r1: e.tensor_scalar(o, i, 4.0, -4.0, ALU.max, ALU.add))
            EM["vector"](lambda e, i=mg, o=r2: e.tensor_scalar(o, i, 6.0, -6.0, ALU.max, ALU.add))
            EM["vector"](lambda e: e.drain())
            EM["vector"](lambda e, i=mg, r=r2, o=t0:
                         e.scalar_tensor_tensor(o, i, 0.5, r, ALU.mult, ALU.add))
            EM["vector"](lambda e: e.drain())
            EM["vector"](lambda e, i=r1, r=t0, o=mag:
                         e.scalar_tensor_tensor(o, i, 0.5, r, ALU.mult, ALU.add))
            EM["vector"](lambda e: e.drain())
            EM["vector"](lambda e, a=mag, b=sg: e.tensor_tensor(a, a, b, ALU.mult))
            EM["vector"](lambda e: e.drain())
            if it >= 2:
                W("vector", stg, stg.n)
            EM["vector"](lambda e, a=mag, b=sc_, o=wb: e.tensor_tensor(o, a, b, ALU.mult))
            ndv = INC("vector", dv)
            if j == 7:
                dv_dq_panel.append(ndv)
            W("gpsimd", dv, ndv)
            if kind == "gu":
                dst = wgu_sc8[cc_, :, j, :]
            else:
                dst = wdn_sc8[:, j, :][cc_ * 128:cc_ * 128 + npart, :]
            st_dq.append(DMA("gpsimd", dst, wb, stg))
            st["it"] = it + 1
            if st["it"] == len(dqjobs) * 8:
                st["done"] = True

        def emit_dq_collective():
            if dqstate["cc_emitted"]:
                return
            while not dqstate["done"]:
                emit_dq_piece()
            W("gpsimd", stg, stg.n)
            EM["gpsimd"](lambda e: e.collective_compute(
                "AllGather", ALU.bypass, replica_groups=[list(range(8))],
                ins=[wgu_sh.opt()], outs=[wgu_gd.opt()]).then_inc(cc_h, 1))
            cc.bump(1)
            EM["gpsimd"](lambda e: e.collective_compute(
                "AllGather", ALU.bypass, replica_groups=[list(range(8))],
                ins=[wdn_sh.opt()], outs=[wdn_gd.opt()]).then_inc(cc_h, 1))
            dqstate["cc_emitted"] = True
            return cc.bump(1)


        if max_phase >= 1:
            XT = TB
            xT = cv(XT, [16, 2048], BF16)
            P1B = TB + 64 * KB
            xbf = cv(P1B, [2, 2048], BF16)

            pe_x = []; dv_tx = []; ld_x = []
            for i in range(NT):
                if i < 2:
                    if i == 0:
                        W("gpsimd", pe, n_pe0)
                    ld_x.append(DMA("gpsimd", xbf[:, i % 2, :], x_tiles[i], ldg))
                W("tensor", ldg, ldg.n)
                if i == 0:
                    W("tensor", dv, dv_p0)
                else:
                    W("tensor", dv, dv_tx[i - 1])
                for f in range(NK):
                    n_ = MM(psA[:, bass.ts(f, 128)], xbf[:, i % 2, bass.ts(f, 128)], identb,
                            True, True, inc=pe if f == NK - 1 else None)
                pe_x.append(n_)
                W("vector", pe, pe_x[i])
                EM["vector"](lambda e, i=i: e.tensor_copy(xT[:, :, bass.ts(i, 128)], psA[:]))
                dv_tx.append(INC("vector", dv))
                if i + 2 < NT:
                    W("gpsimd", pe, pe_x[i])
                    ld_x.append(DMA("gpsimd", xbf[:, i % 2, :], x_tiles[i + 2], ldg))

        if max_phase >= 2:
            wm = cv(P1B + 8 * KB, [2, 2048], BF16)
            qkvrow = cv(P1B + 16 * KB, [2, 2048], BF16)
            vtile = cv(P1B + 24 * KB, [2048], BF16)

            pe_qkv = []; ac_qkv = []; dv_v = []; st_qk = {}; st_v = []; pe_v = []
            ld_w2 = {}
            ps2 = [psA, psB]

            def emit_tr_v(mv):
                h = mv - 32
                W("tensor", ac, ac_qkv[mv])
                for t in range(NT):
                    n_ = MM(ps2[mv % 2][:, bass.ts(t, 128)],
                            qkvrow[:, mv % 2, bass.ts(t, 128)], identb,
                            True, True, inc=pe if t == NT - 1 else None)
                pe_v.append(n_)
                W("vector", pe, pe_v[h])
                if h >= 1:
                    W("vector", sts, sts.n)
                EM["vector"](lambda e: e.tensor_copy(vtile[:], ps2[mv % 2][:]))
                dv_v.append(INC("vector", dv))
                W("sync", dv, dv_v[h])
                st_v.append(DMA("sync", v_d[h], vtile[:], sts))

            for m in range(49):
                if m < 2:
                    if m == 0:
                        W("gpsimd", pe, pe_x[NT - 1])
                    ld_w2[m] = DMA("gpsimd", wm[:, m % 2, :], wattn[m], wms[m % 2])
                if m <= 47:
                    W("tensor", wms[m % 2], ld_w2[m])
                    if m == 0:
                        W("tensor", dv, dv_tx[NT - 1])
                    if m >= 2:
                        W("tensor", ac, ac_qkv[m - 2])
                        if m - 2 >= 32:
                            W("tensor", dv, dv_v[m - 34])
                    for nb in range(4):
                        for k in range(NK):
                            n_ = MM(ps2[m % 2][:, bass.ts(nb, 512)], wm[:, m % 2, bass.ts(k, 128)],
                                    xT[:, k, bass.ts(nb, 512)], k == 0, k == NK - 1,
                                    inc=pe if (nb == 3 and k == NK - 1) else None)
                    pe_qkv.append(n_)
                if m >= 1 and m - 1 >= 32:
                    emit_tr_v(m - 1)
                if m <= 47:
                    W("scalar", pe, pe_qkv[m])
                    if m >= 2:
                        mp = m - 2
                        if mp < 32:
                            W("scalar", sts, sts.n)
                        else:
                            W("scalar", pe, pe_v[mp - 32])
                    if m < 16:
                        EM["scalar"](lambda e, m=m: e.activation(
                            qkvrow[:, m % 2, :], ps2[m % 2][:], AF.Identity,
                            bias=battnq_sb[:, m:m + 1], scale=CSC))
                    else:
                        EM["scalar"](lambda e, m=m: e.activation(
                            qkvrow[:, m % 2, :], ps2[m % 2][:], AF.Identity,
                            bias=battn_sb[:, m:m + 1], scale=1.0))
                    ac_qkv.append(INC("scalar", ac))
                    if m < 32:
                        W("sync", ac, ac_qkv[m])
                        st_qk[m] = DMA("sync", qkT_d[bass.ts(m, 128), :], qkvrow[:, m % 2, :], sts)
                    if m + 2 < 48:
                        W("gpsimd", pe, pe_qkv[m])
                        ld_w2[m + 2] = DMA("gpsimd", wm[:, m % 2, :], wattn[m + 2], wms[m % 2])
                    if max_phase >= 3:
                        emit_dq_piece()

            pe_p2_done = pe.n
            sts_p2_done = sts.n

        if max_phase >= 3:
            cc_done = emit_dq_collective()
            dv_dq_done = dv.n
            stg_dq_done = stg.n

        if max_phase >= 4:
            AT = TB
            qh = cv(AT, [2, 2048], BF16)
            kh = cv(AT + 8 * KB, [2, 2048], BF16)
            vh = cv(AT + 16 * KB, [2, 2048], BF16)
            pn = cv(AT + 24 * KB, [2, 2048], BF16)
            PT = cv(AT + 32 * KB, [16, 512], BF16)
            orow = cv(AT + 48 * KB, [2, 2048], BF16)

            pe_head = []; st_o = []; ld_h4 = []
            oT5 = cv(TB + 56 * KB, [16, 2048], BF16)
            wp5 = cv(TB + 120 * KB, [16, 2048], BF16)

            def emit_head_loads(h):
                if h == 0:
                    W("sync", dv, dv_dq_done)
                    W("sync", stg, stg_dq_done)
                DMA("sync", qh[:, h % 2, :], qkT_d[bass.ts(h, 128), :], hds[h % 2])
                DMA("sync", kh[:, h % 2, :], qkT_d[2048 + h * 128:2176 + h * 128, :], hds[h % 2])
                ld_h4.append(DMA("sync", vh[:, h % 2, :], v_d[h], hds[h % 2]))

            emit_head_loads(0)
            emit_head_loads(1)

            ac_exp = {}; dv_ptev = {}; ac_oev = []
            ps4 = [psA, psB]
            last_oev = 0
            n_s = {}; n_pt = {}; n_red = {}; n_recip = {}; n_nrm = {}
            for h in range(NH):
                hp = h % 2
                W("tensor", hds[h % 2], ld_h4[h])
                for t in range(17):
                    tl = t - 1
                    if t <= 15:
                        qt = t
                        Lk = (qt + 1) * 128
                        par = qt % 2
                        P = ps4[par]
                        prev = dv_ptev.get((h, qt - 2))
                        if prev is None and qt < 2:
                            prev = dv_ptev.get((h - 1, 14 + par))
                        if prev:
                            W("tensor", dv, prev)
                        elif h == 0 and qt < 2:
                            W("tensor", dv, dv.n)
                            W("tensor", ac, ac.n)
                        if par == 1 and Lk > 1536 and last_oev:
                            W("tensor", ac, last_oev)
                        nnb = (Lk + 511) // 512
                        for nb in range(nnb):
                            N = min(512, Lk - nb * 512)
                            lastc = nb == nnb - 1
                            MM(P[:, nb * 512:nb * 512 + N], qh[:, hp, bass.ts(qt, 128)],
                               kh[:, hp, nb * 512:nb * 512 + N], True, not lastc)
                            if lastc:
                                n_s[t] = MM(P[:, bass.ts(qt, 128)], masktb, identb,
                                            False, True, inc=pe)
                    if tl >= 0:
                        parl = tl % 2
                        Pl = ps4[parl]
                        Lkl = (tl + 1) * 128
                        W("vector", ac, ac_exp[(h, tl)])
                        EM["vector"](lambda e, qt=tl: e.reciprocal(
                            rinv_sb[:, qt:qt + 1], rsum_sb[:, qt:qt + 1]))
                        EM["vector"](lambda e: e.drain())
                        EM["vector"](lambda e, L=Lkl, qt=tl, par=parl: e.tensor_scalar(
                            pn[:, par, 0:L], pn[:, par, 0:L], rinv_sb[:, qt:qt + 1],
                            None, ALU.mult))
                        n_nrm[tl] = INC("vector", dv)
                        W("tensor", dv, n_nrm[tl])
                        nkt = tl + 1
                        for kt in range(nkt):
                            n_pt[tl] = MM(Pl[:, bass.ts(kt, 128)], pn[:, parl, bass.ts(kt, 128)],
                                          identb, True, True, inc=pe if kt == nkt - 1 else None)
                    if t <= 15:
                        W("vector", pe, n_s[t])
                        EM["vector"](lambda e, L=Lk, qt=t, P=P: e.reduce_max(
                            nmax_sb[:, qt:qt + 1], P[:, 0:L], axis=AX.X, negate=True))
                        n_red[t] = INC("vector", dv)
                    if tl >= 0:
                        W("vector", pe, n_pt[tl])
                        EM["vector"](lambda e, nkt=tl + 1, q4=tl % 4, P=Pl: e.tensor_copy(
                            PT[:, 0:nkt, bass.ts(q4, 128)],
                            P[:, 0:nkt * 128].rearrange("p (a b) -> p a b", b=128)))
                        dv_ptev[(h, tl)] = INC("vector", dv)
                    if t <= 15:
                        W("scalar", dv, n_red[t])
                        EM["scalar"](lambda e, L=Lk, qt=t, par=par, P=P: e.activation(
                            pn[:, par, 0:L], P[:, 0:L], AF.Exp,
                            bias=nmax_sb[:, qt:qt + 1], scale=1.0,
                            accum_out=rsum_sb[:, qt:qt + 1]))
                        ac_exp[(h, t)] = INC("scalar", ac)
                    if t >= 4 and t % 4 == 0:
                        qg = t // 4 - 1
                        for r in range(1, 4):
                            kt = qg * 4 + r
                            EM["vector"](lambda e, kt=kt, r=r: e.memset(PT[:, kt, 0:r * 128], 0.0))
                        n_z = INC("vector", dv)
                        W("tensor", dv, n_z)
                        W("tensor", ac, ac_exp[(h, qg * 4 + 3)])
                        nkt_g = qg * 4 + 4
                        for kt in range(nkt_g):
                            n_pv = MM(psB[:, 1536:2048], vh[:, hp, bass.ts(kt, 128)], PT[:, kt, :],
                                      kt == 0, kt == nkt_g - 1,
                                      inc=pe if kt == nkt_g - 1 else None)
                        W("scalar", pe, n_pv)
                        if qg == 0 and h >= 2:
                            W("scalar", sts, sts.n)
                        EM["scalar"](lambda e, qg=qg, hp=hp: e.activation(
                            orow[:, hp, bass.ts(qg, 512)], psB[:, 1536:2048], AF.Identity,
                            bias=0.0, scale=1.0))
                        n_oev = INC("scalar", ac)
                        last_oev = n_oev
                        if qg == 3:
                            ac_oev.append(n_oev)
                pe_head.append(pe.n)
                W("sync", ac, ac_oev[h])
                st_o.append(DMA("sync", oT_d[bass.ts(h, 128), :], orow[:, h % 2, :], sts))
                W("sync", sts, sts.n)
                ld_p5oT = DMA("sync", oT5[:, h, :], oT_d[bass.ts(h, 128), :], lds)
                if h == 0:
                    for k5 in range(NK):
                        DMA("gpsimd", wp5[:, k5, :], wproj[k5], ldg)
                if h + 2 < NH:
                    W("sync", pe, pe_head[h])
                    emit_head_loads(h + 2)

            sts_attn_done = sts.n
            pe_attn_done = pe.n

        if max_phase >= 5:
            P5 = TB + 56 * KB
            oT = cv(P5, [16, 2048], BF16)
            wp = cv(TB + 120 * KB, [16, 2048], BF16)
            P5S = TB
            xres = cv(P5S, [2, 2048], F32)
            rsb = cv(P5S + 16 * KB, [2048], F32)
            sqs = cv(P5S + 24 * KB, [2048], F32)
            x1sb = cv(P5S + 32 * KB, [2, 2048], F32)

            W("sync", sts, sts_attn_done)
            W("sync", pe, pe_attn_done)
            ld_xr = {0: DMA("sync", xres[:, 0, :], x_tiles[0], lds),
                     1: DMA("sync", xres[:, 1, :], x_tiles[1], lds)}

            pe_p5 = []; dv_p5 = {}; ac_p5 = {}; st_x1 = []
            ps5 = [psA, psB]
            W("tensor", lds, lds.n)
            W("tensor", ldg, ldg.n)
            W("tensor", sts, sts_attn_done)
            for i in range(NT):
                if i <= 1:
                    W("tensor", ac, ac.n)
                    W("tensor", dv, dv.n)
                else:
                    W("tensor", dv, dv_p5[(i - 2, "bf")])
                for nb in range(4):
                    for k in range(NK):
                        n_ = MM(ps5[i % 2][:, bass.ts(nb, 512)], oT[:, k, bass.ts(i, 128)],
                                wp[:, k, bass.ts(nb, 512)], k == 0, k == NK - 1,
                                inc=pe if (nb == 3 and k == NK - 1) else None)
                pe_p5.append(n_)
                W("vector", pe, pe_p5[i])
                W("vector", lds, lds.n)
                if i >= 1:
                    W("vector", ac, ac_p5[(i - 1, "x1")])
                EM["vector"](lambda e, i=i: e.scalar_tensor_tensor(rsb[:], ps5[i % 2][:], 1.0, bpbc[:], ALU.mult, ALU.add))
                dv_p5[(i, "bf")] = INC("vector", dv)
                EM["vector"](lambda e, i=i: e.tensor_tensor(rsb[:], rsb[:], xres[:, i % 2, :], ALU.add))
                EM["vector"](lambda e: e.drain())
                EM["vector"](lambda e: e.reduce_sum(s_sum[:], rsb[:], axis=AX.X))
                EM["vector"](lambda e: e.drain())
                EM["vector"](lambda e: e.tensor_scalar(s_nm[:], s_sum[:], -1.0 / 2048.0, None, ALU.mult))
                dv_p5[(i, "r")] = INC("vector", dv)
                W("scalar", dv, dv_p5[(i, "r")])
                EM["scalar"](lambda e: e.activation(sqs[:], rsb[:], AF.Square, bias=s_nm[:, 0:1],
                                                    scale=1.0, accum_out=s_varp[:, 0:1]))
                EM["scalar"](lambda e: e.drain())
                EM["scalar"](lambda e: e.activation(s_std[:], s_varp[:], AF.Sqrt,
                                                    bias=eps5[:, 0:1], scale=1.0 / 2048.0))
                ac_p5[(i, "sq")] = INC("scalar", ac)
                W("vector", ac, ac_p5[(i, "sq")])
                EM["vector"](lambda e: e.reciprocal(s_rstd[:], s_std[:]))
                EM["vector"](lambda e: e.drain())
                EM["vector"](lambda e: e.tensor_tensor(s_nmr[:], s_nm[:], s_rstd[:], ALU.mult))
                dv_p5[(i, "n")] = INC("vector", dv)
                W("scalar", dv, dv_p5[(i, "n")])
                if i >= 2:
                    W("scalar", sts, sts.n)
                EM["scalar"](lambda e, i=i: e.activation(x1sb[:, i % 2, :], rsb[:], AF.Identity,
                                                         bias=s_nmr[:, 0:1], scale=s_rstd[:, 0:1]))
                ac_p5[(i, "x1")] = INC("scalar", ac)
                W("sync", ac, ac_p5[(i, "x1")])
                st_x1.append(DMA("sync", x1_tiles[i], x1sb[:, i % 2, :], sts))
                if i + 2 < NT:
                    W("sync", dv, dv_p5[(i, "r")])
                    ld_xr[i + 2] = DMA("sync", xres[:, i % 2, :], x_tiles[i + 2], lds)
            sts_x1_done = sts.n
            pe_p5_done = pe.n

        if max_phase >= 6:
            x1T = cv(XT, [16, 2048], BF16)
            P5B = TB + 64 * KB
            x1bf = cv(P5B, [2, 2048], BF16)

            pe_x1 = []; dv_tx1 = []; ld_x1 = []
            for i in range(NT):
                if i < 2:
                    if i == 0:
                        W("gpsimd", sts, sts_x1_done)
                        W("gpsimd", pe, pe_p5_done)
                    ld_x1.append(DMA("gpsimd", x1bf[:, i % 2, :], x1_tiles[i], ldg))
                W("tensor", ldg, ldg.n)
                if i == 0:
                    W("tensor", dv, dv.n)
                    W("tensor", ac, ac.n)
                else:
                    W("tensor", dv, dv_tx1[i - 1])
                for f in range(NK):
                    n_ = MM(psA[:, bass.ts(f, 128)], x1bf[:, i % 2, bass.ts(f, 128)], identb,
                            True, True, inc=pe if f == NK - 1 else None)
                pe_x1.append(n_)
                W("vector", pe, pe_x1[i])
                EM["vector"](lambda e, i=i: e.tensor_copy(x1T[:, :, bass.ts(i, 128)], psA[:]))
                dv_tx1.append(INC("vector", dv))
                if i + 2 < NT:
                    W("gpsimd", pe, pe_x1[i])
                    ld_x1.append(DMA("gpsimd", x1bf[:, i % 2, :], x1_tiles[i + 2], ldg))

        if max_phase >= 7:
            P6 = TB + 64 * KB
            wg6 = cv(P6, [16, 1408], BF16)
            wu6 = cv(P6 + 45 * KB, [16, 1408], BF16)
            sgt = cv(P6 + 90 * KB, [2048], BF16)
            hrow = cv(P6 + 94 * KB, [2, 2048], BF16)

            wgu_rk = wgu_gd.rearrange("(r k p) c -> r k p c", r=8, p=128)
            hT_rows = hT_d.rearrange("(k p) t -> k p t", p=128)
            pe_p6 = {}; st_h6 = []
            ac_p6s = None; dv_p6h = None
            for a in range(4):
                ld_g = 0
                for k in range(NK):
                    if k == 0:
                        if a == 0:
                            W("sync", pe, pe.n)
                        else:
                            W("sync", pe, pe_p6[(a - 1, 10)])
                if a == 0:
                    W("sync", cc, cc_done)
                for k in range(NK):
                    ld_g = DMA("sync", wg6[:, k, :], wgu_rk[a, k], lds)
                ld_u = 0
                for k in range(NK):
                    if k == 0:
                        if a == 0:
                            W("gpsimd", pe, pe.n)
                        else:
                            W("gpsimd", pe, pe_p6[(a - 1, 10)])
                for k in range(NK):
                    ld_u = DMA("gpsimd", wu6[:, k, :], wgu_rk[4 + a, k], ldg)
                for mpl in range(11):
                    mp = a * 11 + mpl
                    W("tensor", lds, ld_g)
                    W("tensor", ldg, ld_u)
                    if mp == 0:
                        W("tensor", dv, dv.n)
                        W("tensor", ac, ac.n)
                    else:
                        W("tensor", ac, ac_p6s)
                        W("tensor", dv, dv_p6h)
                    for nb in range(4):
                        for k in range(NK):
                            n_g = MM(psA[:, bass.ts(nb, 512)], wg6[:, k, bass.ts(mpl, 128)],
                                     x1T[:, k, bass.ts(nb, 512)], k == 0, k == NK - 1,
                                     inc=pe if (nb == 3 and k == NK - 1) else None)
                    for nb in range(4):
                        for k in range(NK):
                            n_u = MM(psB[:, bass.ts(nb, 512)], wu6[:, k, bass.ts(mpl, 128)],
                                     x1T[:, k, bass.ts(nb, 512)], k == 0, k == NK - 1,
                                     inc=pe if (nb == 3 and k == NK - 1) else None)
                    W("scalar", pe, n_g)
                    if mp >= 1:
                        W("scalar", dv, dv_p6h)
                    EM["scalar"](lambda e: e.activation(sgt[:], psA[:], AF.Silu, bias=0.0, scale=1.0))
                    ac_p6s = INC("scalar", ac)
                    W("vector", ac, ac_p6s)
                    W("vector", pe, n_u)
                    if mp >= 2:
                        W("vector", sts, sts.n)
                    EM["vector"](lambda e, mp=mp: e.tensor_tensor(hrow[:, mp % 2, :], sgt[:],
                                                                  psB[:], ALU.mult))
                    dv_p6h = INC("vector", dv)
                    pe_p6[(a, mpl)] = pe.n
                    W("sync", dv, dv_p6h)
                    st_h6.append(DMA("sync", hT_rows[0].tensor.ap()
                                     .rearrange("(k p) t -> k p t", p=128)[mp] if False else
                                     hT_d[bass.ts(mp, 128), :], hrow[:, mp % 2, :], sts))
            sts_h_done = sts.n

        if max_phase >= 8:
            P7 = TB
            hT7 = cv(P7, [2, 44, 512], BF16)
            dT = cv(P7 + 88 * KB, [16, 512], F32)
            wdc = cv(P7 + 120 * KB, [2, 8, 512], BF16)
            EPI = P7 + 136 * KB
            sq7 = cv(EPI, [2048], F32)
            x1r7 = cv(EPI + 8 * KB, [2, 2048], F32)
            dsb = cv(EPI + 24 * KB, [2048], F32)
            r2sb = cv(EPI + 32 * KB, [2048], F32)
            outsb = cv(EPI + 40 * KB, [2048], F32)

            wdn_pkc8 = wdn_gd.rearrange("(r K p) c -> r p K c", r=8, p=128)
            hT_pkc = hT_d.rearrange("(K p) t -> p K t", p=128)
            accs = [psA[:, bass.ts(m4, 512)] for m4 in range(4)]

            st_y = []
            pe_dn_hist = {}
            for tg in range(4):
                tb = tg % 2
                ld_h7 = 0
                for gb in range(11):
                    if gb == 0:
                        W("gpsimd", ldg, ldg.n)
                        if tg == 0:
                            W("gpsimd", sts, sts_h_done)
                            W("gpsimd", pe, pe.n)
                            W("gpsimd", ac, ac.n)
                            W("gpsimd", dv, dv.n)
                        elif tg >= 2:
                            W("gpsimd", pe, pe_dn_hist[tg - 2])
                    ld_h7 = DMA("gpsimd", hT7[:, tb, gb * 4:gb * 4 + 4, :],
                                hT_pkc[:, gb * 4:gb * 4 + 4, bass.ts(tg, 512)], ldg)
                for mg in range(4):
                    pe_p7mm = []
                    GS = [8, 8, 8, 8, 8, 4]
                    GO = [0, 8, 16, 24, 32, 40]
                    def emit_wdc_load(g):
                        n = GS[g]
                        DMA("sync", wdc[:, g % 2, 0:n, 0:256],
                            wdn_pkc8[2 * mg, :, GO[g]:GO[g] + n, :], wds[g % 2])
                        return DMA("sync", wdc[:, g % 2, 0:n, 256:512],
                                   wdn_pkc8[2 * mg + 1, :, GO[g]:GO[g] + n, :], wds[g % 2])
                    ld_wd = {}
                    for g in range(2):
                        if g == 0:
                            W("sync", pe, pe.n)
                            if tg == 0 and mg == 0:
                                W("sync", cc, cc_done)
                                W("sync", ac, ac.n)
                                W("sync", dv, dv.n)
                        ld_wd[g] = emit_wdc_load(g)
                    for kq in range(44):
                        g = kq // 8
                        gk = kq % 8
                        if gk == 0:
                            W("tensor", wds[g % 2], ld_wd[g])
                        if kq == 0:
                            W("tensor", ldg, ldg.n)
                            if mg == 0:
                                if tg == 0:
                                    W("tensor", dv, dv.n)
                                    W("tensor", ac, ac.n)
                                else:
                                    W("tensor", dv, dv_dT)
                            if mg > 0 or tg > 0:
                                W("tensor", dv, dv_dT_last)
                        for m4 in range(4):
                            n_ = MM(accs[m4], wdc[:, g % 2, gk, bass.ts(m4, 128)],
                                    hT7[:, tb, kq, :], kq == 0, kq == 43,
                                    inc=pe if (m4 == 3) else None)
                        pe_p7mm.append(n_)
                        if gk == GS[g] - 1 and g + 2 < 6:
                            W("sync", pe, pe_p7mm[kq])
                            ld_wd[g + 2] = emit_wdc_load(g + 2)
                    W("vector", pe, pe_p7mm[-1])
                    for m4 in range(4):
                        m = mg * 4 + m4
                        EM["vector"](lambda e, m=m, m4=m4: e.tensor_copy(dT[:, m, :], accs[m4]))
                    dv_dT_last = INC("vector", dv)
                dv_dT = dv_dT_last
                pe_dn_hist[tg] = pe.n
                for il in range(4):
                    i = tg * 4 + il
                    if il < 2:
                        if il == 0 and tg == 0:
                            W("scalar", dv, dv_dT)
                        n_x1r = DMA("scalar", x1r7[:, il % 2, :], x1_tiles[i], sts)
                        if il == 0:
                            ld_x1r = {}
                        ld_x1r[il] = n_x1r
                    W("tensor", dv, dv_dT)
                    if il >= 1:
                        W("tensor", dv, dv_p7tr[il - 1])
                    elif tg > 0:
                        W("tensor", dv, dv_p7tr[3])
                        W("tensor", ac, ac.n)
                    for m in range(NK):
                        n_tr = MM(psB[:, bass.ts(m, 128)], dT[:, m, bass.ts(il, 128)], identf,
                                  True, True, inc=pe if m == NK - 1 else None)
                    if il == 0:
                        dv_p7tr = []
                    W("vector", pe, n_tr)
                    if il >= 1 or tg > 0:
                        W("vector", ac, ac_ep_last)
                    EM["vector"](lambda e: e.tensor_copy(dsb[:], psB[:]))
                    dv_p7tr.append(INC("vector", dv))
                    W("scalar", dv, dv_p7tr[il])
                    EM["scalar"](lambda e: e.activation(sq7[:], dsb[:], AF.Square, bias=0.0,
                                                        scale=1.0, accum_out=s_varp[:, 0:1]))
                    EM["scalar"](lambda e: e.drain())
                    EM["scalar"](lambda e: e.activation(s_std[:], s_varp[:], AF.Sqrt,
                                                        bias=eps6[:, 0:1], scale=1.0 / 2048.0))
                    n_ac1 = INC("scalar", ac)
                    W("vector", ac, n_ac1)
                    W("vector", sts, sts.n)
                    EM["vector"](lambda e: e.reciprocal(s_rstd[:], s_std[:]))
                    EM["vector"](lambda e: e.drain())
                    EM["vector"](lambda e, il=il: e.scalar_tensor_tensor(
                        r2sb[:], dsb[:], s_rstd[:, 0:1], x1r7[:, il % 2, :], ALU.mult, ALU.add))
                    EM["vector"](lambda e: e.drain())
                    EM["vector"](lambda e: e.reduce_sum(s_sum[:], r2sb[:], axis=AX.X))
                    EM["vector"](lambda e: e.drain())
                    EM["vector"](lambda e: e.tensor_scalar(s_nm[:], s_sum[:], -1.0 / 2048.0, None, ALU.mult))
                    n_dv2 = INC("vector", dv)
                    W("scalar", dv, n_dv2)
                    EM["scalar"](lambda e: e.activation(sq7[:], r2sb[:], AF.Square, bias=s_nm[:, 0:1],
                                                        scale=1.0, accum_out=s_varp[:, 0:1]))
                    EM["scalar"](lambda e: e.drain())
                    EM["scalar"](lambda e: e.activation(s_std[:], s_varp[:], AF.Sqrt,
                                                        bias=eps5[:, 0:1], scale=1.0 / 2048.0))
                    n_ac3 = INC("scalar", ac)
                    W("vector", ac, n_ac3)
                    EM["vector"](lambda e: e.reciprocal(s_rstd[:], s_std[:]))
                    EM["vector"](lambda e: e.drain())
                    EM["vector"](lambda e: e.tensor_tensor(s_nmr[:], s_nm[:], s_rstd[:], ALU.mult))
                    n_dv4 = INC("vector", dv)
                    W("scalar", dv, n_dv4)
                    W("scalar", sts, sts.n)
                    EM["scalar"](lambda e, il=il: e.activation(outsb[:], r2sb[:], AF.Identity,
                                                               bias=s_nmr[:, 0:1], scale=s_rstd[:, 0:1]))
                    EM["scalar"](lambda e: e.drain())
                    st_y.append(DMA("scalar", y_tiles[i], outsb[:], sts))
                    ac_ep_last = INC("scalar", ac)
                    if il + 2 < 4:
                        ld_x1r[il + 2] = DMA("scalar", x1r7[:, il % 2, :], x1_tiles[i + 2], sts)

        W("sync", sts, sts.n)
        W("sync", stg, stg.n)
        W("gpsimd", ldg, ldg.n)
        W("sync", lds, lds.n)

        @block.sync
        def _(e):
            for f in ops["sync"]:
                f(e)

        @block.tensor
        def _(e):
            for f in ops["tensor"]:
                f(e)

        @block.vector
        def _(e):
            for f in ops["vector"]:
                f(e)

        @block.scalar
        def _(e):
            for f in ops["scalar"]:
                f(e)

        @block.gpsimd
        def _(e):
            for f in ops["gpsimd"]:
                f(e)

    return nc


def prep_inputs(L0_attn_x, L0_attn_w_attn, L0_attn_b_attn, L0_attn_w_proj,
                L0_attn_b_proj, L0_ffn_gate_up_packed, L0_ffn_gate_up_scales,
                L0_ffn_down_packed, L0_ffn_down_scales):
    x = np.ascontiguousarray(np.asarray(L0_attn_x, dtype=np.float32))
    wa = np.asarray(L0_attn_w_attn, dtype=np.float32)
    wattn = np.ascontiguousarray(wa.reshape(16, 128, 48, 128).transpose(2, 1, 0, 3)
                                 .reshape(48, 128, 2048))
    battn = np.ascontiguousarray(np.asarray(L0_attn_b_attn, dtype=np.float32).reshape(1, 6144))
    wproj = np.ascontiguousarray(np.asarray(L0_attn_w_proj, dtype=np.float32).reshape(16, 128, 2048))
    bprojv = np.ascontiguousarray(np.asarray(L0_attn_b_proj, dtype=np.float32).reshape(1, 2048))
    gu_pkt_f = np.asarray(L0_ffn_gate_up_packed, dtype=np.int32).T.reshape(2, 128, GU_COLS)
    gu_sc_f = np.repeat(np.asarray(L0_ffn_gate_up_scales, dtype=np.float16),
                        16, axis=0).reshape(2, 128, GU_COLS)
    dn_pkt_f = np.asarray(L0_ffn_down_packed, dtype=np.int32).T
    dn_sc_f = np.repeat(np.asarray(L0_ffn_down_scales, dtype=np.float16), 16, axis=0)
    identb = np.eye(128, dtype=ml_dtypes.bfloat16)
    identf = np.eye(128, dtype=np.float32)
    mask = np.triu(np.full((128, 128), NEG, dtype=np.float32), k=1)
    masktb = np.ascontiguousarray(mask.T).astype(ml_dtypes.bfloat16)
    ones1 = np.ones((1, 128), dtype=np.float32)

    common = dict(wattn=wattn, battn=battn, wproj=wproj, bproj=bprojv,
                  identb=identb, identf=identf, mask=mask, masktb=masktb, ones1=ones1)
    in_maps = []
    for b in range(8):
        m = dict(common)
        m["x"] = np.ascontiguousarray(x[b])
        m["gu_pkt"] = np.ascontiguousarray(gu_pkt_f[:, :, b * 1408:(b + 1) * 1408])
        m["gu_sc"] = np.ascontiguousarray(gu_sc_f[:, :, b * 1408:(b + 1) * 1408])
        m["dn_pkt"] = np.ascontiguousarray(dn_pkt_f[:, b * 256:(b + 1) * 256])
        m["dn_sc"] = np.ascontiguousarray(dn_sc_f[:, b * 256:(b + 1) * 256])
        in_maps.append(m)
    return in_maps


def kernel(**inputs):
    in_maps = prep_inputs(**inputs)
    nc = build_program(dbg=False)
    res = run_bass_kernel_spmd(nc, in_maps, core_ids=list(range(8)))
    out = np.stack([np.asarray(r["y"], dtype=np.float32) for r in res.results], axis=0)
    return out


# version 20
# speedup vs baseline: 1.1735x; 1.0059x over previous
import math

import numpy as np
import ml_dtypes

import concourse.bass as bass
import concourse.mybir as mybir
from concourse.bass_utils import run_bass_kernel_spmd

F32 = mybir.dt.float32
BF16 = mybir.dt.bfloat16
F16 = mybir.dt.float16
I32 = mybir.dt.int32
AF = mybir.ActivationFunctionType
ALU = mybir.AluOpType
AX = mybir.AxisListType

T = 2048
C = 2048
NT = 16
NK = 16
NH = 16
HS = 128
I_FFN = 5632
GU_COLS = 2 * I_FFN
CSC = 1.0 / math.sqrt(HS)
NEG = -1.0e30
KB = 1024


class Ctr:
    def __init__(self, h):
        self.h = h
        self.n = 0

    def bump(self, k):
        self.n += k
        return self.n


def build_program(dbg=False, max_phase=9):
    nc = bass.Bass("TRN2", target_bir_lowering=False)

    x_in = nc.dram_tensor("x", [T, C], F32, kind="ExternalInput").ap()
    wattn = nc.dram_tensor("wattn", [48, 128, 2048], F32, kind="ExternalInput").ap()
    battn = nc.dram_tensor("battn", [1, 6144], F32, kind="ExternalInput").ap()
    wproj = nc.dram_tensor("wproj", [16, 128, 2048], F32, kind="ExternalInput").ap()
    bproj = nc.dram_tensor("bproj", [1, 2048], F32, kind="ExternalInput").ap()
    gu_pkt = nc.dram_tensor("gu_pkt", [2, 128, 1408], I32, kind="ExternalInput").ap()
    gu_sc = nc.dram_tensor("gu_sc", [2, 128, 1408], F16, kind="ExternalInput").ap()
    dn_pkt = nc.dram_tensor("dn_pkt", [704, 256], I32, kind="ExternalInput").ap()
    dn_sc = nc.dram_tensor("dn_sc", [704, 256], F16, kind="ExternalInput").ap()
    identb_in = nc.dram_tensor("identb", [128, 128], BF16, kind="ExternalInput").ap()
    identf_in = nc.dram_tensor("identf", [128, 128], F32, kind="ExternalInput").ap()
    mask_in = nc.dram_tensor("mask", [128, 128], F32, kind="ExternalInput").ap()
    masktb_in = nc.dram_tensor("masktb", [128, 128], BF16, kind="ExternalInput").ap()
    ones_in = nc.dram_tensor("ones1", [1, 128], F32, kind="ExternalInput").ap()

    y_out = nc.dram_tensor("y", [T, C], F32, kind="ExternalOutput").ap()

    skind = "ExternalOutput" if dbg else "Internal"
    qkT_d = nc.dram_tensor("qkT_d", [4096, T], BF16, kind=skind).ap()
    v_d = nc.dram_tensor("v_d", [NH, 128, T], BF16, kind=skind).ap()
    oT_d = nc.dram_tensor("oT_d", [C, T], BF16, kind=skind).ap()
    x1_d = nc.dram_tensor("x1_d", [T, C], F32, kind=skind).ap()
    wgu_sh = nc.dram_tensor("wgu_sh", [C, 1408], BF16, kind="Internal").ap()
    wdn_sh = nc.dram_tensor("wdn_sh", [I_FFN, 256], BF16, kind="Internal").ap()
    wgu_gd = nc.dram_tensor("wgu_gd", [8 * C, 1408], BF16, kind="Internal", addr_space="Shared").ap()
    wdn_gd = nc.dram_tensor("wdn_gd", [8 * I_FFN, 256], BF16, kind="Internal", addr_space="Shared").ap()
    hT_d = nc.dram_tensor("hT_d", [I_FFN, T], BF16, kind=skind).ap()

    ops = {k: [] for k in ("sync", "tensor", "vector", "scalar", "gpsimd")}
    EM = {k: ops[k].append for k in ops}

    def W(eng, ctr, val):
        if val and val > 0:
            EM[eng](lambda e, h=ctr.h, v=val: e.wait_ge(h, v))

    def DMA(eng, dst, src, ctr):
        n = ctr.bump(16)
        EM[eng](lambda e, d=dst, s=src, h=ctr.h: e.dma_start(d, s).then_inc(h, 16))
        return n

    def INC(eng, ctr, k=1):
        n = ctr.bump(k)
        if eng in ("vector", "scalar", "gpsimd"):
            EM[eng](lambda e, h=ctr.h, v=k: e.drain().then_inc(h, v))
        else:
            EM[eng](lambda e, h=ctr.h, v=k: e.nop().then_inc(h, v))
        return n

    def MM(psum_ap, lhsT, rhs, start, stop, inc=None):
        if inc is None:
            EM["tensor"](lambda e, o=psum_ap, l=lhsT, r=rhs, st=start, sp=stop:
                         e.matmul(o, l, r, start=st, stop=sp))
            return None
        n = inc.bump(1)
        EM["tensor"](lambda e, o=psum_ap, l=lhsT, r=rhs, st=start, sp=stop, h=inc.h:
                     e.matmul(o, l, r, start=st, stop=sp).then_inc(h, 1))
        return n

    from contextlib import ExitStack
    with (
        nc.sbuf_tensor("arena", [128, 100 * KB], BF16) as arena_t,
        nc.psum_tensor("psA", [128, 2048], F32) as psA_t,
        nc.psum_tensor("psB", [128, 2048], F32) as psB_t,
        ExitStack() as _sems,
        nc.Block() as block,
    ):
        (lds_h, ldg_h, sts_h, stg_h, pe_h, dv_h, ac_h, cc_h,
         wma_h, wmb_h, wda_h, wdb_h, hda_h, hdb_h, xla_h, xlb_h) = [
            _sems.enter_context(nc.semaphore(nm)) for nm in
            ("lds", "ldg", "sts", "stg", "pe", "dv", "ac", "cc",
             "wma", "wmb", "wda", "wdb", "hda", "hdb", "xla", "xlb")]
        arena = arena_t.ap()
        psA = psA_t.ap()
        psB = psB_t.ap()
        lds = Ctr(lds_h); ldg = Ctr(ldg_h); sts = Ctr(sts_h); stg = Ctr(stg_h)
        pe = Ctr(pe_h); dv = Ctr(dv_h); ac = Ctr(ac_h); cc = Ctr(cc_h)
        wms = [Ctr(wma_h), Ctr(wmb_h)]
        wds = [Ctr(wda_h), Ctr(wdb_h)]
        hds = [Ctr(hda_h), Ctr(hdb_h)]
        xls = [Ctr(xla_h), Ctr(xlb_h)]

        def cv(off_bytes, shape, dtype):
            assert off_bytes % 64 == 0
            nel = int(np.prod(shape))
            esz = {F32: 4, BF16: 2, F16: 2, I32: 4}[dtype]
            o = off_bytes // 2
            nbf = nel * esz // 2
            assert off_bytes + nel * esz <= 200 * KB, (off_bytes, nel, esz)
            apx = arena[:, o:o + nbf]
            if dtype != BF16:
                apx = apx.bitcast(dtype)
            if len(shape) == 1:
                return apx
            pat = "p (" + " ".join(f"d{i}" for i in range(len(shape))) + ") -> p " + \
                  " ".join(f"d{i}" for i in range(len(shape)))
            kw = {f"d{i}": s for i, s in enumerate(shape)}
            return apx.rearrange(pat, **kw)

        identb = cv(0x0000, [128], BF16)
        masktb = cv(0x2C00, [128], BF16)
        identf = cv(0x0100, [128], F32)
        mask_sb = cv(0x0300, [128], F32)
        battn_sb = cv(0x0500, [64], F32)
        battnq_sb = cv(0x0600, [16], F32)
        bpbc = cv(0x0700, [2048], F32)
        nmax_sb = cv(0x2700, [16], F32)
        rsum_sb = cv(0x2740, [16], F32)
        rinv_sb = cv(0x2780, [16], F32)
        s_sum = cv(0x27C0, [1], F32)
        s_nm = cv(0x2800, [1], F32)
        s_varp = cv(0x2840, [1], F32)
        s_rec = cv(0x2880, [1], F32)
        s_std = cv(0x2B80, [1], F32)
        eps5 = cv(0x2D00, [1], F32)
        eps6 = cv(0x2D40, [1], F32)
        s_rstd = cv(0x28C0, [1], F32)
        s_nmr = cv(0x2900, [1], F32)
        onesf = cv(0x2940, [128], F32)
        TB = 0x3400

        x_tiles = x_in.rearrange("(i p) c -> i p c", p=128)
        x1_tiles = x1_d.rearrange("(i p) c -> i p c", p=128)
        y_tiles = y_out.rearrange("(i p) c -> i p c", p=128)

        brow = cv(TB, [6144], F32)
        bprow = cv(TB + 24 * KB, [2048], F32)

        DMA("sync", identb, identb_in, lds)
        DMA("sync", identf, identf_in, lds)
        DMA("sync", mask_sb, mask_in, lds)
        DMA("sync", brow[0:1, :], battn, lds)
        DMA("sync", bprow[0:1, :], bproj, lds)
        DMA("sync", masktb, masktb_in, lds)
        n_ld0 = DMA("sync", onesf[0:1, :], ones_in, lds)

        W("tensor", lds, n_ld0)
        for m in range(48):
            MM(psA[:, m:m + 1], brow[0:1, bass.ts(m, 128)], onesf[0:1, 0:1], True, True)
        for nb in range(4):
            n_pe0 = MM(psB[:, bass.ts(nb, 512)], onesf[0:1, :], bprow[0:1, bass.ts(nb, 512)],
                       True, True, inc=pe if nb == 3 else None)

        EM["vector"](lambda e: e.memset(eps5[:], 1e-5))
        EM["vector"](lambda e: e.memset(eps6[:], 1e-6))
        W("vector", pe, n_pe0)
        EM["vector"](lambda e: e.tensor_copy(battn_sb[:, 0:48], psA[:, 0:48]))
        EM["vector"](lambda e: e.tensor_copy(bpbc[:], psB[:]))
        EM["vector"](lambda e: e.drain())
        EM["vector"](lambda e: e.tensor_scalar(battnq_sb[:], battn_sb[:, 0:16], CSC, None, ALU.mult))
        dv_p0 = INC("vector", dv)

        DQ = TB + 100 * KB
        dq_w = cv(DQ, [2, 1408], I32)
        dq_sc = cv(DQ + 12 * KB, [2, 1408], BF16)
        dq_mi = cv(DQ + 18 * KB, [1408], I32)
        dq_si = cv(DQ + 24 * KB, [1408], I32)
        dq_m = cv(DQ + 30 * KB, [1408], BF16)
        dq_sg = cv(DQ + 33 * KB, [1408], BF16)
        dq_r1 = cv(DQ + 36 * KB, [1408], BF16)
        dq_r2 = cv(DQ + 39 * KB, [1408], BF16)
        dq_t0 = cv(DQ + 42 * KB, [1408], BF16)
        dq_mag = cv(DQ + 45 * KB, [1408], BF16)
        dq_wv = cv(DQ + 48 * KB, [2, 1408], BF16)

        wgu_sc8 = wgu_sh.rearrange("(a r e) c -> a r e c", a=2, e=8)
        wdn_sc8 = wdn_sh.rearrange("(r e) c -> r e c", e=8)

        dqjobs = [("gu", cc_) for cc_ in range(2)] + [("dn", cc_) for cc_ in range(6)]
        dqstate = {"it": 0, "panel": -1, "ld_done": {}, "panels": [], "done": False,
                   "cc_emitted": False}
        ld_dqw = []; ld_dqs = []; dv_dq_panel = []; st_dq = []

        def dq_geom(pidx):
            kind, cc_ = dqjobs[pidx]
            if kind == "gu":
                cols, npart = 1408, 128
                wsrc = gu_pkt[cc_]
                ssrc = gu_sc[cc_]
            else:
                cols = 256
                npart = 128 if cc_ < 5 else 64
                wsrc = dn_pkt[cc_ * 128:cc_ * 128 + npart, :]
                ssrc = dn_sc[cc_ * 128:cc_ * 128 + npart, :]
            return kind, cc_, cols, npart, wsrc, ssrc

        def emit_dq_load(pidx):
            kind, cc_, cols, npart, wsrc, ssrc = dq_geom(pidx)
            pj = pidx % 2
            if pidx >= 2:
                W("sync", dv, dv_dq_panel[pidx - 2])
                W("gpsimd", dv, dv_dq_panel[pidx - 2])
            ld_dqw.append(DMA("sync", dq_w[0:npart, pj, 0:cols], wsrc, lds))
            ld_dqs.append(DMA("gpsimd", dq_sc[0:npart, pj, 0:cols], ssrc, ldg))

        def emit_dq_piece():
            st = dqstate
            if st["done"]:
                return
            it = st["it"]
            pidx, j = divmod(it, 8)
            if pidx >= len(dqjobs):
                st["done"] = True
                return
            if j == 0:
                if pidx == 0:
                    emit_dq_load(0)
                if pidx + 1 < len(dqjobs):
                    emit_dq_load(pidx + 1)
            kind, cc_, cols, npart, wsrc, ssrc = dq_geom(pidx)
            pj = pidx % 2
            W("vector", lds, lds.n)
            W("vector", ldg, ldg.n)
            wv_ = dq_w[0:npart, pj, 0:cols]
            sc_ = dq_sc[0:npart, pj, 0:cols]
            wb = dq_wv[0:npart, it % 2, 0:cols]
            mi = dq_mi[0:npart, 0:cols]; si = dq_si[0:npart, 0:cols]
            mg = dq_m[0:npart, 0:cols]; sg = dq_sg[0:npart, 0:cols]
            r1 = dq_r1[0:npart, 0:cols]; r2 = dq_r2[0:npart, 0:cols]
            t0 = dq_t0[0:npart, 0:cols]; mag = dq_mag[0:npart, 0:cols]
            EM["vector"](lambda e, w=wv_, j=j, o=mi:
                         e.tensor_scalar(o, w, 4 * j, 7, ALU.logical_shift_right, ALU.bitwise_and))
            EM["vector"](lambda e, w=wv_, j=j, o=si:
                         e.tensor_scalar(o, w, 4 * j + 3, 1, ALU.logical_shift_right, ALU.bitwise_and))
            EM["vector"](lambda e: e.drain())
            EM["vector"](lambda e, i=mi, o=mg: e.tensor_copy(o, i))
            EM["vector"](lambda e, i=si, o=sg: e.tensor_scalar(o, i, -2.0, 1.0, ALU.mult, ALU.add))
            EM["vector"](lambda e: e.drain())
            EM["vector"](lambda e, i=mg, o=r1: e.tensor_scalar(o, i, 4.0, -4.0, ALU.max, ALU.add))
            EM["vector"](lambda e, i=mg, o=r2: e.tensor_scalar(o, i, 6.0, -6.0, ALU.max, ALU.add))
            EM["vector"](lambda e: e.drain())
            EM["vector"](lambda e, i=mg, r=r2, o=t0:
                         e.scalar_tensor_tensor(o, i, 0.5, r, ALU.mult, ALU.add))
            EM["vector"](lambda e: e.drain())
            EM["vector"](lambda e, i=r1, r=t0, o=mag:
                         e.scalar_tensor_tensor(o, i, 0.5, r, ALU.mult, ALU.add))
            EM["vector"](lambda e: e.drain())
            EM["vector"](lambda e, a=mag, b=sg: e.tensor_tensor(a, a, b, ALU.mult))
            EM["vector"](lambda e: e.drain())
            if it >= 2:
                W("vector", stg, stg.n)
            EM["vector"](lambda e, a=mag, b=sc_, o=wb: e.tensor_tensor(o, a, b, ALU.mult))
            ndv = INC("vector", dv)
            if j == 7:
                dv_dq_panel.append(ndv)
            W("gpsimd", dv, ndv)
            if kind == "gu":
                dst = wgu_sc8[cc_, :, j, :]
            else:
                dst = wdn_sc8[:, j, :][cc_ * 128:cc_ * 128 + npart, :]
            st_dq.append(DMA("gpsimd", dst, wb, stg))
            st["it"] = it + 1
            if st["it"] == len(dqjobs) * 8:
                st["done"] = True

        def emit_dq_collective():
            if dqstate["cc_emitted"]:
                return
            while not dqstate["done"]:
                emit_dq_piece()
            W("gpsimd", stg, stg.n)
            EM["gpsimd"](lambda e: e.collective_compute(
                "AllGather", ALU.bypass, replica_groups=[list(range(8))],
                ins=[wgu_sh.opt()], outs=[wgu_gd.opt()]).then_inc(cc_h, 1))
            cc.bump(1)
            EM["gpsimd"](lambda e: e.collective_compute(
                "AllGather", ALU.bypass, replica_groups=[list(range(8))],
                ins=[wdn_sh.opt()], outs=[wdn_gd.opt()]).then_inc(cc_h, 1))
            dqstate["cc_emitted"] = True
            return cc.bump(1)


        if max_phase >= 1:
            XT = TB
            xT = cv(XT, [16, 2048], BF16)
            P1B = TB + 64 * KB
            xbf = cv(P1B, [2, 2048], BF16)

            pe_x = []; dv_tx = []; ld_x = {}
            for i in range(NT):
                if i < 2:
                    if i == 0:
                        W("gpsimd", pe, n_pe0)
                    ld_x[i] = DMA("gpsimd", xbf[:, i % 2, :], x_tiles[i], xls[i % 2])
                W("tensor", xls[i % 2], ld_x[i])
                if i == 0:
                    W("tensor", dv, dv_p0)
                else:
                    W("tensor", dv, dv_tx[i - 1])
                for f in range(NK):
                    n_ = MM(psA[:, bass.ts(f, 128)], xbf[:, i % 2, bass.ts(f, 128)], identb,
                            True, True, inc=pe if f == NK - 1 else None)
                pe_x.append(n_)
                W("vector", pe, pe_x[i])
                EM["vector"](lambda e, i=i: e.tensor_copy(xT[:, :, bass.ts(i, 128)], psA[:]))
                dv_tx.append(INC("vector", dv))
                if i + 2 < NT:
                    W("gpsimd", pe, pe_x[i])
                    ld_x[i + 2] = DMA("gpsimd", xbf[:, i % 2, :], x_tiles[i + 2], xls[i % 2])

        if max_phase >= 2:
            wm = cv(P1B + 8 * KB, [2, 2048], BF16)
            qkvrow = cv(P1B + 16 * KB, [2, 2048], BF16)
            vtile = cv(P1B + 24 * KB, [2048], BF16)

            pe_qkv = []; ac_qkv = []; dv_v = []; st_qk = {}; st_v = []; pe_v = []
            ld_w2 = {}
            ps2 = [psA, psB]

            def emit_tr_v(mv):
                h = mv - 32
                W("tensor", ac, ac_qkv[mv])
                for t in range(NT):
                    n_ = MM(ps2[mv % 2][:, bass.ts(t, 128)],
                            qkvrow[:, mv % 2, bass.ts(t, 128)], identb,
                            True, True, inc=pe if t == NT - 1 else None)
                pe_v.append(n_)
                W("vector", pe, pe_v[h])
                if h >= 1:
                    W("vector", sts, sts.n)
                EM["vector"](lambda e: e.tensor_copy(vtile[:], ps2[mv % 2][:]))
                dv_v.append(INC("vector", dv))
                W("sync", dv, dv_v[h])
                st_v.append(DMA("sync", v_d[h], vtile[:], sts))

            for m in range(49):
                if m < 2:
                    if m == 0:
                        W("gpsimd", pe, pe_x[NT - 1])
                    ld_w2[m] = DMA("gpsimd", wm[:, m % 2, :], wattn[m], wms[m % 2])
                if m <= 47:
                    W("tensor", wms[m % 2], ld_w2[m])
                    if m == 0:
                        W("tensor", dv, dv_tx[NT - 1])
                    if m >= 2:
                        W("tensor", ac, ac_qkv[m - 2])
                        if m - 2 >= 32:
                            W("tensor", dv, dv_v[m - 34])
                    for nb in range(4):
                        for k in range(NK):
                            n_ = MM(ps2[m % 2][:, bass.ts(nb, 512)], wm[:, m % 2, bass.ts(k, 128)],
                                    xT[:, k, bass.ts(nb, 512)], k == 0, k == NK - 1,
                                    inc=pe if (nb == 3 and k == NK - 1) else None)
                    pe_qkv.append(n_)
                if m >= 1 and m - 1 >= 32:
                    emit_tr_v(m - 1)
                if m <= 47:
                    W("scalar", pe, pe_qkv[m])
                    if m >= 2:
                        mp = m - 2
                        if mp < 32:
                            W("scalar", sts, sts.n)
                        else:
                            W("scalar", pe, pe_v[mp - 32])
                    if m < 16:
                        EM["scalar"](lambda e, m=m: e.activation(
                            qkvrow[:, m % 2, :], ps2[m % 2][:], AF.Identity,
                            bias=battnq_sb[:, m:m + 1], scale=CSC))
                    else:
                        EM["scalar"](lambda e, m=m: e.activation(
                            qkvrow[:, m % 2, :], ps2[m % 2][:], AF.Identity,
                            bias=battn_sb[:, m:m + 1], scale=1.0))
                    ac_qkv.append(INC("scalar", ac))
                    if m < 32:
                        W("sync", ac, ac_qkv[m])
                        st_qk[m] = DMA("sync", qkT_d[bass.ts(m, 128), :], qkvrow[:, m % 2, :], sts)
                    if m + 2 < 48:
                        W("gpsimd", pe, pe_qkv[m])
                        ld_w2[m + 2] = DMA("gpsimd", wm[:, m % 2, :], wattn[m + 2], wms[m % 2])
                    if max_phase >= 3:
                        emit_dq_piece()

            pe_p2_done = pe.n
            sts_p2_done = sts.n

        if max_phase >= 3:
            cc_done = emit_dq_collective()
            dv_dq_done = dv.n
            stg_dq_done = stg.n

        if max_phase >= 4:
            AT = TB
            qh = cv(AT, [2, 2048], BF16)
            kh = cv(AT + 8 * KB, [2, 2048], BF16)
            vh = cv(AT + 16 * KB, [2, 2048], BF16)
            pn = cv(AT + 24 * KB, [2, 2048], BF16)
            PT = cv(AT + 32 * KB, [16, 512], BF16)
            orow = cv(AT + 48 * KB, [2, 2048], BF16)

            pe_head = []; st_o = []; ld_h4 = []
            oT5 = cv(TB + 56 * KB, [16, 2048], BF16)
            wp5 = cv(TB + 120 * KB, [16, 2048], BF16)

            def emit_head_loads(h):
                if h == 0:
                    W("sync", dv, dv_dq_done)
                    W("sync", stg, stg_dq_done)
                DMA("sync", qh[:, h % 2, :], qkT_d[bass.ts(h, 128), :], hds[h % 2])
                DMA("sync", kh[:, h % 2, :], qkT_d[2048 + h * 128:2176 + h * 128, :], hds[h % 2])
                ld_h4.append(DMA("sync", vh[:, h % 2, :], v_d[h], hds[h % 2]))

            emit_head_loads(0)
            emit_head_loads(1)

            ac_exp = {}; dv_ptev = {}; ac_oev = []
            ps4 = [psA, psB]
            last_oev = 0
            n_s = {}; n_pt = {}; n_red = {}; n_recip = {}; n_nrm = {}
            for h in range(NH):
                hp = h % 2
                W("tensor", hds[h % 2], ld_h4[h])
                for t in range(17):
                    tl = t - 1
                    if t <= 15:
                        qt = t
                        Lk = (qt + 1) * 128
                        par = qt % 2
                        P = ps4[par]
                        prev = dv_ptev.get((h, qt - 2))
                        if prev is None and qt < 2:
                            prev = dv_ptev.get((h - 1, 14 + par))
                        if prev:
                            W("tensor", dv, prev)
                        elif h == 0 and qt < 2:
                            W("tensor", dv, dv.n)
                            W("tensor", ac, ac.n)
                        if par == 1 and Lk > 1536 and last_oev:
                            W("tensor", ac, last_oev)
                        nnb = (Lk + 511) // 512
                        for nb in range(nnb):
                            N = min(512, Lk - nb * 512)
                            lastc = nb == nnb - 1
                            MM(P[:, nb * 512:nb * 512 + N], qh[:, hp, bass.ts(qt, 128)],
                               kh[:, hp, nb * 512:nb * 512 + N], True, not lastc)
                            if lastc:
                                n_s[t] = MM(P[:, bass.ts(qt, 128)], masktb, identb,
                                            False, True, inc=pe)
                    if tl >= 0:
                        parl = tl % 2
                        Pl = ps4[parl]
                        Lkl = (tl + 1) * 128
                        W("vector", ac, ac_exp[(h, tl)])
                        EM["vector"](lambda e, qt=tl: e.reciprocal(
                            rinv_sb[:, qt:qt + 1], rsum_sb[:, qt:qt + 1]))
                        EM["vector"](lambda e: e.drain())
                        EM["vector"](lambda e, L=Lkl, qt=tl, par=parl: e.tensor_scalar(
                            pn[:, par, 0:L], pn[:, par, 0:L], rinv_sb[:, qt:qt + 1],
                            None, ALU.mult))
                        n_nrm[tl] = INC("vector", dv)
                        W("tensor", dv, n_nrm[tl])
                        nkt = tl + 1
                        for kt in range(nkt):
                            n_pt[tl] = MM(Pl[:, bass.ts(kt, 128)], pn[:, parl, bass.ts(kt, 128)],
                                          identb, True, True, inc=pe if kt == nkt - 1 else None)
                    if t <= 15:
                        W("vector", pe, n_s[t])
                        EM["vector"](lambda e, L=Lk, qt=t, P=P: e.reduce_max(
                            nmax_sb[:, qt:qt + 1], P[:, 0:L], axis=AX.X, negate=True))
                        n_red[t] = INC("vector", dv)
                    if tl >= 0:
                        W("vector", pe, n_pt[tl])
                        EM["vector"](lambda e, nkt=tl + 1, q4=tl % 4, P=Pl: e.tensor_copy(
                            PT[:, 0:nkt, bass.ts(q4, 128)],
                            P[:, 0:nkt * 128].rearrange("p (a b) -> p a b", b=128)))
                        dv_ptev[(h, tl)] = INC("vector", dv)
                    if t <= 15:
                        W("scalar", dv, n_red[t])
                        EM["scalar"](lambda e, L=Lk, qt=t, par=par, P=P: e.activation(
                            pn[:, par, 0:L], P[:, 0:L], AF.Exp,
                            bias=nmax_sb[:, qt:qt + 1], scale=1.0,
                            accum_out=rsum_sb[:, qt:qt + 1]))
                        ac_exp[(h, t)] = INC("scalar", ac)
                    if t >= 4 and t % 4 == 0:
                        qg = t // 4 - 1
                        for r in range(1, 4):
                            kt = qg * 4 + r
                            EM["vector"](lambda e, kt=kt, r=r: e.memset(PT[:, kt, 0:r * 128], 0.0))
                        n_z = INC("vector", dv)
                        W("tensor", dv, n_z)
                        W("tensor", ac, ac_exp[(h, qg * 4 + 3)])
                        nkt_g = qg * 4 + 4
                        for kt in range(nkt_g):
                            n_pv = MM(psB[:, 1536:2048], vh[:, hp, bass.ts(kt, 128)], PT[:, kt, :],
                                      kt == 0, kt == nkt_g - 1,
                                      inc=pe if kt == nkt_g - 1 else None)
                        W("scalar", pe, n_pv)
                        if qg == 0 and h >= 2:
                            W("scalar", sts, sts.n)
                        EM["scalar"](lambda e, qg=qg, hp=hp: e.activation(
                            orow[:, hp, bass.ts(qg, 512)], psB[:, 1536:2048], AF.Identity,
                            bias=0.0, scale=1.0))
                        n_oev = INC("scalar", ac)
                        last_oev = n_oev
                        if qg == 3:
                            ac_oev.append(n_oev)
                pe_head.append(pe.n)
                W("sync", ac, ac_oev[h])
                st_o.append(DMA("sync", oT_d[bass.ts(h, 128), :], orow[:, h % 2, :], sts))
                W("sync", sts, sts.n)
                ld_p5oT = DMA("sync", oT5[:, h, :], oT_d[bass.ts(h, 128), :], lds)
                if h == 0:
                    for k5 in range(NK):
                        DMA("gpsimd", wp5[:, k5, :], wproj[k5], ldg)
                if h + 2 < NH:
                    W("sync", pe, pe_head[h])
                    emit_head_loads(h + 2)

            sts_attn_done = sts.n
            pe_attn_done = pe.n

        if max_phase >= 5:
            P5 = TB + 56 * KB
            oT = cv(P5, [16, 2048], BF16)
            wp = cv(TB + 120 * KB, [16, 2048], BF16)
            P5S = TB
            xres = cv(P5S, [2, 2048], F32)
            rsb = cv(P5S + 16 * KB, [2048], F32)
            sqs = cv(P5S + 24 * KB, [2048], F32)
            x1sb = cv(P5S + 32 * KB, [2, 2048], F32)

            W("sync", sts, sts_attn_done)
            W("sync", pe, pe_attn_done)
            ld_xr = {0: DMA("sync", xres[:, 0, :], x_tiles[0], lds),
                     1: DMA("sync", xres[:, 1, :], x_tiles[1], lds)}

            pe_p5 = []; dv_p5 = {}; ac_p5 = {}; st_x1 = []
            ps5 = [psA, psB]
            W("tensor", lds, lds.n)
            W("tensor", ldg, ldg.n)
            W("tensor", sts, sts_attn_done)
            for i in range(NT):
                if i <= 1:
                    W("tensor", ac, ac.n)
                    W("tensor", dv, dv.n)
                else:
                    W("tensor", dv, dv_p5[(i - 2, "bf")])
                for nb in range(4):
                    for k in range(NK):
                        n_ = MM(ps5[i % 2][:, bass.ts(nb, 512)], oT[:, k, bass.ts(i, 128)],
                                wp[:, k, bass.ts(nb, 512)], k == 0, k == NK - 1,
                                inc=pe if (nb == 3 and k == NK - 1) else None)
                pe_p5.append(n_)
                W("vector", pe, pe_p5[i])
                W("vector", lds, lds.n)
                if i >= 1:
                    W("vector", ac, ac_p5[(i - 1, "x1")])
                EM["vector"](lambda e, i=i: e.scalar_tensor_tensor(rsb[:], ps5[i % 2][:], 1.0, bpbc[:], ALU.mult, ALU.add))
                dv_p5[(i, "bf")] = INC("vector", dv)
                EM["vector"](lambda e, i=i: e.tensor_tensor(rsb[:], rsb[:], xres[:, i % 2, :], ALU.add))
                EM["vector"](lambda e: e.drain())
                EM["vector"](lambda e: e.reduce_sum(s_sum[:], rsb[:], axis=AX.X))
                EM["vector"](lambda e: e.drain())
                EM["vector"](lambda e: e.tensor_scalar(s_nm[:], s_sum[:], -1.0 / 2048.0, None, ALU.mult))
                dv_p5[(i, "r")] = INC("vector", dv)
                W("scalar", dv, dv_p5[(i, "r")])
                EM["scalar"](lambda e: e.activation(sqs[:], rsb[:], AF.Square, bias=s_nm[:, 0:1],
                                                    scale=1.0, accum_out=s_varp[:, 0:1]))
                EM["scalar"](lambda e: e.drain())
                EM["scalar"](lambda e: e.activation(s_std[:], s_varp[:], AF.Sqrt,
                                                    bias=eps5[:, 0:1], scale=1.0 / 2048.0))
                ac_p5[(i, "sq")] = INC("scalar", ac)
                W("vector", ac, ac_p5[(i, "sq")])
                EM["vector"](lambda e: e.reciprocal(s_rstd[:], s_std[:]))
                EM["vector"](lambda e: e.drain())
                EM["vector"](lambda e: e.tensor_tensor(s_nmr[:], s_nm[:], s_rstd[:], ALU.mult))
                dv_p5[(i, "n")] = INC("vector", dv)
                W("scalar", dv, dv_p5[(i, "n")])
                if i >= 2:
                    W("scalar", sts, sts.n)
                EM["scalar"](lambda e, i=i: e.activation(x1sb[:, i % 2, :], rsb[:], AF.Identity,
                                                         bias=s_nmr[:, 0:1], scale=s_rstd[:, 0:1]))
                ac_p5[(i, "x1")] = INC("scalar", ac)
                W("sync", ac, ac_p5[(i, "x1")])
                st_x1.append(DMA("sync", x1_tiles[i], x1sb[:, i % 2, :], sts))
                if i + 2 < NT:
                    W("sync", dv, dv_p5[(i, "r")])
                    ld_xr[i + 2] = DMA("sync", xres[:, i % 2, :], x_tiles[i + 2], lds)
            sts_x1_done = sts.n
            pe_p5_done = pe.n

        if max_phase >= 6:
            x1T = cv(XT, [16, 2048], BF16)
            P5B = TB + 64 * KB
            x1bf = cv(P5B, [2, 2048], BF16)

            pe_x1 = []; dv_tx1 = []; ld_x1 = {}
            for i in range(NT):
                if i < 2:
                    if i == 0:
                        W("gpsimd", sts, sts_x1_done)
                        W("gpsimd", pe, pe_p5_done)
                    ld_x1[i] = DMA("gpsimd", x1bf[:, i % 2, :], x1_tiles[i], xls[i % 2])
                W("tensor", xls[i % 2], ld_x1[i])
                if i == 0:
                    W("tensor", dv, dv.n)
                    W("tensor", ac, ac.n)
                else:
                    W("tensor", dv, dv_tx1[i - 1])
                for f in range(NK):
                    n_ = MM(psA[:, bass.ts(f, 128)], x1bf[:, i % 2, bass.ts(f, 128)], identb,
                            True, True, inc=pe if f == NK - 1 else None)
                pe_x1.append(n_)
                W("vector", pe, pe_x1[i])
                EM["vector"](lambda e, i=i: e.tensor_copy(x1T[:, :, bass.ts(i, 128)], psA[:]))
                dv_tx1.append(INC("vector", dv))
                if i + 2 < NT:
                    W("gpsimd", pe, pe_x1[i])
                    ld_x1[i + 2] = DMA("gpsimd", x1bf[:, i % 2, :], x1_tiles[i + 2], xls[i % 2])

        if max_phase >= 7:
            P6 = TB + 64 * KB
            wg6 = cv(P6, [16, 1408], BF16)
            wu6 = cv(P6 + 45 * KB, [16, 1408], BF16)
            sgt = cv(P6 + 90 * KB, [2048], BF16)
            hrow = cv(P6 + 94 * KB, [2, 2048], BF16)

            wgu_rk = wgu_gd.rearrange("(r k p) c -> r k p c", r=8, p=128)
            hT_rows = hT_d.rearrange("(k p) t -> k p t", p=128)
            pe_p6 = {}; st_h6 = []
            ac_p6s = None; dv_p6h = None
            for a in range(4):
                ld_g = 0
                for k in range(NK):
                    if k == 0:
                        if a == 0:
                            W("sync", pe, pe.n)
                        else:
                            W("sync", pe, pe_p6[(a - 1, 10)])
                if a == 0:
                    W("sync", cc, cc_done)
                for k in range(NK):
                    ld_g = DMA("sync", wg6[:, k, :], wgu_rk[a, k], lds)
                ld_u = 0
                for k in range(NK):
                    if k == 0:
                        if a == 0:
                            W("gpsimd", pe, pe.n)
                        else:
                            W("gpsimd", pe, pe_p6[(a - 1, 10)])
                for k in range(NK):
                    ld_u = DMA("gpsimd", wu6[:, k, :], wgu_rk[4 + a, k], ldg)
                for mpl in range(11):
                    mp = a * 11 + mpl
                    W("tensor", lds, ld_g)
                    W("tensor", ldg, ld_u)
                    if mp == 0:
                        W("tensor", dv, dv.n)
                        W("tensor", ac, ac.n)
                    else:
                        W("tensor", ac, ac_p6s)
                        W("tensor", dv, dv_p6h)
                    for nb in range(4):
                        for k in range(NK):
                            n_g = MM(psA[:, bass.ts(nb, 512)], wg6[:, k, bass.ts(mpl, 128)],
                                     x1T[:, k, bass.ts(nb, 512)], k == 0, k == NK - 1,
                                     inc=pe if (nb == 3 and k == NK - 1) else None)
                    for nb in range(4):
                        for k in range(NK):
                            n_u = MM(psB[:, bass.ts(nb, 512)], wu6[:, k, bass.ts(mpl, 128)],
                                     x1T[:, k, bass.ts(nb, 512)], k == 0, k == NK - 1,
                                     inc=pe if (nb == 3 and k == NK - 1) else None)
                    W("scalar", pe, n_g)
                    if mp >= 1:
                        W("scalar", dv, dv_p6h)
                    EM["scalar"](lambda e: e.activation(sgt[:], psA[:], AF.Silu, bias=0.0, scale=1.0))
                    ac_p6s = INC("scalar", ac)
                    W("vector", ac, ac_p6s)
                    W("vector", pe, n_u)
                    if mp >= 2:
                        W("vector", sts, sts.n)
                    EM["vector"](lambda e, mp=mp: e.tensor_tensor(hrow[:, mp % 2, :], sgt[:],
                                                                  psB[:], ALU.mult))
                    dv_p6h = INC("vector", dv)
                    pe_p6[(a, mpl)] = pe.n
                    W("sync", dv, dv_p6h)
                    st_h6.append(DMA("sync", hT_rows[0].tensor.ap()
                                     .rearrange("(k p) t -> k p t", p=128)[mp] if False else
                                     hT_d[bass.ts(mp, 128), :], hrow[:, mp % 2, :], sts))
            sts_h_done = sts.n

        if max_phase >= 8:
            P7 = TB
            hT7 = cv(P7, [2, 44, 512], BF16)
            dT = cv(P7 + 88 * KB, [16, 512], F32)
            wdc = cv(P7 + 120 * KB, [2, 8, 512], BF16)
            EPI = P7 + 136 * KB
            sq7 = cv(EPI, [2048], F32)
            x1r7 = cv(EPI + 8 * KB, [2, 2048], F32)
            dsb = cv(EPI + 24 * KB, [2048], F32)
            r2sb = cv(EPI + 32 * KB, [2048], F32)
            outsb = cv(EPI + 40 * KB, [2048], F32)

            wdn_pkc8 = wdn_gd.rearrange("(r K p) c -> r p K c", r=8, p=128)
            hT_pkc = hT_d.rearrange("(K p) t -> p K t", p=128)
            accs = [psA[:, bass.ts(m4, 512)] for m4 in range(4)]

            st_y = []
            pe_dn_hist = {}
            for tg in range(4):
                tb = tg % 2
                ld_h7 = 0
                for gb in range(11):
                    if gb == 0:
                        W("gpsimd", ldg, ldg.n)
                        if tg == 0:
                            W("gpsimd", sts, sts_h_done)
                            W("gpsimd", pe, pe.n)
                            W("gpsimd", ac, ac.n)
                            W("gpsimd", dv, dv.n)
                        elif tg >= 2:
                            W("gpsimd", pe, pe_dn_hist[tg - 2])
                    ld_h7 = DMA("gpsimd", hT7[:, tb, gb * 4:gb * 4 + 4, :],
                                hT_pkc[:, gb * 4:gb * 4 + 4, bass.ts(tg, 512)], ldg)
                for mg in range(4):
                    pe_p7mm = []
                    GS = [8, 8, 8, 8, 8, 4]
                    GO = [0, 8, 16, 24, 32, 40]
                    def emit_wdc_load(g):
                        n = GS[g]
                        DMA("sync", wdc[:, g % 2, 0:n, 0:256],
                            wdn_pkc8[2 * mg, :, GO[g]:GO[g] + n, :], wds[g % 2])
                        return DMA("sync", wdc[:, g % 2, 0:n, 256:512],
                                   wdn_pkc8[2 * mg + 1, :, GO[g]:GO[g] + n, :], wds[g % 2])
                    ld_wd = {}
                    for g in range(2):
                        if g == 0:
                            W("sync", pe, pe.n)
                            if tg == 0 and mg == 0:
                                W("sync", cc, cc_done)
                                W("sync", ac, ac.n)
                                W("sync", dv, dv.n)
                        ld_wd[g] = emit_wdc_load(g)
                    for kq in range(44):
                        g = kq // 8
                        gk = kq % 8
                        if gk == 0:
                            W("tensor", wds[g % 2], ld_wd[g])
                        if kq == 0:
                            W("tensor", ldg, ldg.n)
                            if mg == 0:
                                if tg == 0:
                                    W("tensor", dv, dv.n)
                                    W("tensor", ac, ac.n)
                                else:
                                    W("tensor", dv, dv_dT)
                            if mg > 0 or tg > 0:
                                W("tensor", dv, dv_dT_last)
                        for m4 in range(4):
                            n_ = MM(accs[m4], wdc[:, g % 2, gk, bass.ts(m4, 128)],
                                    hT7[:, tb, kq, :], kq == 0, kq == 43,
                                    inc=pe if (m4 == 3) else None)
                        pe_p7mm.append(n_)
                        if gk == GS[g] - 1 and g + 2 < 6:
                            W("sync", pe, pe_p7mm[kq])
                            ld_wd[g + 2] = emit_wdc_load(g + 2)
                    W("vector", pe, pe_p7mm[-1])
                    for m4 in range(4):
                        m = mg * 4 + m4
                        EM["vector"](lambda e, m=m, m4=m4: e.tensor_copy(dT[:, m, :], accs[m4]))
                    dv_dT_last = INC("vector", dv)
                dv_dT = dv_dT_last
                pe_dn_hist[tg] = pe.n
                for il in range(4):
                    i = tg * 4 + il
                    if il < 2:
                        if il == 0 and tg == 0:
                            W("scalar", dv, dv_dT)
                        n_x1r = DMA("scalar", x1r7[:, il % 2, :], x1_tiles[i], sts)
                        if il == 0:
                            ld_x1r = {}
                        ld_x1r[il] = n_x1r
                    W("tensor", dv, dv_dT)
                    if il >= 1:
                        W("tensor", dv, dv_p7tr[il - 1])
                    elif tg > 0:
                        W("tensor", dv, dv_p7tr[3])
                        W("tensor", ac, ac.n)
                    for m in range(NK):
                        n_tr = MM(psB[:, bass.ts(m, 128)], dT[:, m, bass.ts(il, 128)], identf,
                                  True, True, inc=pe if m == NK - 1 else None)
                    if il == 0:
                        dv_p7tr = []
                    W("vector", pe, n_tr)
                    if il >= 1 or tg > 0:
                        W("vector", ac, ac_ep_last)
                    EM["vector"](lambda e: e.tensor_copy(dsb[:], psB[:]))
                    dv_p7tr.append(INC("vector", dv))
                    W("scalar", dv, dv_p7tr[il])
                    EM["scalar"](lambda e: e.activation(sq7[:], dsb[:], AF.Square, bias=0.0,
                                                        scale=1.0, accum_out=s_varp[:, 0:1]))
                    EM["scalar"](lambda e: e.drain())
                    EM["scalar"](lambda e: e.activation(s_std[:], s_varp[:], AF.Sqrt,
                                                        bias=eps6[:, 0:1], scale=1.0 / 2048.0))
                    n_ac1 = INC("scalar", ac)
                    W("vector", ac, n_ac1)
                    W("vector", sts, sts.n)
                    EM["vector"](lambda e: e.reciprocal(s_rstd[:], s_std[:]))
                    EM["vector"](lambda e: e.drain())
                    EM["vector"](lambda e, il=il: e.scalar_tensor_tensor(
                        r2sb[:], dsb[:], s_rstd[:, 0:1], x1r7[:, il % 2, :], ALU.mult, ALU.add))
                    EM["vector"](lambda e: e.drain())
                    EM["vector"](lambda e: e.reduce_sum(s_sum[:], r2sb[:], axis=AX.X))
                    EM["vector"](lambda e: e.drain())
                    EM["vector"](lambda e: e.tensor_scalar(s_nm[:], s_sum[:], -1.0 / 2048.0, None, ALU.mult))
                    n_dv2 = INC("vector", dv)
                    W("scalar", dv, n_dv2)
                    EM["scalar"](lambda e: e.activation(sq7[:], r2sb[:], AF.Square, bias=s_nm[:, 0:1],
                                                        scale=1.0, accum_out=s_varp[:, 0:1]))
                    EM["scalar"](lambda e: e.drain())
                    EM["scalar"](lambda e: e.activation(s_std[:], s_varp[:], AF.Sqrt,
                                                        bias=eps5[:, 0:1], scale=1.0 / 2048.0))
                    n_ac3 = INC("scalar", ac)
                    W("vector", ac, n_ac3)
                    EM["vector"](lambda e: e.reciprocal(s_rstd[:], s_std[:]))
                    EM["vector"](lambda e: e.drain())
                    EM["vector"](lambda e: e.tensor_tensor(s_nmr[:], s_nm[:], s_rstd[:], ALU.mult))
                    n_dv4 = INC("vector", dv)
                    W("scalar", dv, n_dv4)
                    W("scalar", sts, sts.n)
                    EM["scalar"](lambda e, il=il: e.activation(outsb[:], r2sb[:], AF.Identity,
                                                               bias=s_nmr[:, 0:1], scale=s_rstd[:, 0:1]))
                    EM["scalar"](lambda e: e.drain())
                    st_y.append(DMA("scalar", y_tiles[i], outsb[:], sts))
                    ac_ep_last = INC("scalar", ac)
                    if il + 2 < 4:
                        ld_x1r[il + 2] = DMA("scalar", x1r7[:, il % 2, :], x1_tiles[i + 2], sts)

        W("sync", sts, sts.n)
        W("sync", stg, stg.n)
        W("gpsimd", ldg, ldg.n)
        W("sync", lds, lds.n)

        @block.sync
        def _(e):
            for f in ops["sync"]:
                f(e)

        @block.tensor
        def _(e):
            for f in ops["tensor"]:
                f(e)

        @block.vector
        def _(e):
            for f in ops["vector"]:
                f(e)

        @block.scalar
        def _(e):
            for f in ops["scalar"]:
                f(e)

        @block.gpsimd
        def _(e):
            for f in ops["gpsimd"]:
                f(e)

    return nc


def prep_inputs(L0_attn_x, L0_attn_w_attn, L0_attn_b_attn, L0_attn_w_proj,
                L0_attn_b_proj, L0_ffn_gate_up_packed, L0_ffn_gate_up_scales,
                L0_ffn_down_packed, L0_ffn_down_scales):
    x = np.ascontiguousarray(np.asarray(L0_attn_x, dtype=np.float32))
    wa = np.asarray(L0_attn_w_attn, dtype=np.float32)
    wattn = np.ascontiguousarray(wa.reshape(16, 128, 48, 128).transpose(2, 1, 0, 3)
                                 .reshape(48, 128, 2048))
    battn = np.ascontiguousarray(np.asarray(L0_attn_b_attn, dtype=np.float32).reshape(1, 6144))
    wproj = np.ascontiguousarray(np.asarray(L0_attn_w_proj, dtype=np.float32).reshape(16, 128, 2048))
    bprojv = np.ascontiguousarray(np.asarray(L0_attn_b_proj, dtype=np.float32).reshape(1, 2048))
    gu_pkt_f = np.asarray(L0_ffn_gate_up_packed, dtype=np.int32).T.reshape(2, 128, GU_COLS)
    gu_sc_f = np.repeat(np.asarray(L0_ffn_gate_up_scales, dtype=np.float16),
                        16, axis=0).reshape(2, 128, GU_COLS)
    dn_pkt_f = np.asarray(L0_ffn_down_packed, dtype=np.int32).T
    dn_sc_f = np.repeat(np.asarray(L0_ffn_down_scales, dtype=np.float16), 16, axis=0)
    identb = np.eye(128, dtype=ml_dtypes.bfloat16)
    identf = np.eye(128, dtype=np.float32)
    mask = np.triu(np.full((128, 128), NEG, dtype=np.float32), k=1)
    masktb = np.ascontiguousarray(mask.T).astype(ml_dtypes.bfloat16)
    ones1 = np.ones((1, 128), dtype=np.float32)

    common = dict(wattn=wattn, battn=battn, wproj=wproj, bproj=bprojv,
                  identb=identb, identf=identf, mask=mask, masktb=masktb, ones1=ones1)
    in_maps = []
    for b in range(8):
        m = dict(common)
        m["x"] = np.ascontiguousarray(x[b])
        m["gu_pkt"] = np.ascontiguousarray(gu_pkt_f[:, :, b * 1408:(b + 1) * 1408])
        m["gu_sc"] = np.ascontiguousarray(gu_sc_f[:, :, b * 1408:(b + 1) * 1408])
        m["dn_pkt"] = np.ascontiguousarray(dn_pkt_f[:, b * 256:(b + 1) * 256])
        m["dn_sc"] = np.ascontiguousarray(dn_sc_f[:, b * 256:(b + 1) * 256])
        in_maps.append(m)
    return in_maps


def kernel(**inputs):
    in_maps = prep_inputs(**inputs)
    nc = build_program(dbg=False)
    res = run_bass_kernel_spmd(nc, in_maps, core_ids=list(range(8)))
    out = np.stack([np.asarray(r["y"], dtype=np.float32) for r in res.results], axis=0)
    return out


# version 21
# speedup vs baseline: 1.1875x; 1.0119x over previous
import math

import numpy as np
import ml_dtypes

import concourse.bass as bass
import concourse.mybir as mybir
from concourse.bass_utils import run_bass_kernel_spmd

F32 = mybir.dt.float32
BF16 = mybir.dt.bfloat16
F16 = mybir.dt.float16
I32 = mybir.dt.int32
AF = mybir.ActivationFunctionType
ALU = mybir.AluOpType
AX = mybir.AxisListType

T = 2048
C = 2048
NT = 16
NK = 16
NH = 16
HS = 128
I_FFN = 5632
GU_COLS = 2 * I_FFN
CSC = 1.0 / math.sqrt(HS)
NEG = -1.0e30
KB = 1024


class Ctr:
    def __init__(self, h):
        self.h = h
        self.n = 0

    def bump(self, k):
        self.n += k
        return self.n


def build_program(dbg=False, max_phase=9):
    nc = bass.Bass("TRN2", target_bir_lowering=False)

    x_in = nc.dram_tensor("x", [T, C], F32, kind="ExternalInput").ap()
    wattn = nc.dram_tensor("wattn", [48, 128, 2048], F32, kind="ExternalInput").ap()
    battn = nc.dram_tensor("battn", [1, 6144], F32, kind="ExternalInput").ap()
    wproj = nc.dram_tensor("wproj", [16, 128, 2048], F32, kind="ExternalInput").ap()
    bproj = nc.dram_tensor("bproj", [1, 2048], F32, kind="ExternalInput").ap()
    gu_pkt = nc.dram_tensor("gu_pkt", [2, 128, 1408], I32, kind="ExternalInput").ap()
    gu_sc = nc.dram_tensor("gu_sc", [2, 128, 1408], F16, kind="ExternalInput").ap()
    dn_pkt = nc.dram_tensor("dn_pkt", [704, 256], I32, kind="ExternalInput").ap()
    dn_sc = nc.dram_tensor("dn_sc", [704, 256], F16, kind="ExternalInput").ap()
    identb_in = nc.dram_tensor("identb", [128, 128], BF16, kind="ExternalInput").ap()
    identf_in = nc.dram_tensor("identf", [128, 128], F32, kind="ExternalInput").ap()
    mask_in = nc.dram_tensor("mask", [128, 128], F32, kind="ExternalInput").ap()
    masktb_in = nc.dram_tensor("masktb", [128, 128], BF16, kind="ExternalInput").ap()
    ones_in = nc.dram_tensor("ones1", [1, 128], F32, kind="ExternalInput").ap()

    y_out = nc.dram_tensor("y", [T, C], F32, kind="ExternalOutput").ap()

    skind = "ExternalOutput" if dbg else "Internal"
    qkT_d = nc.dram_tensor("qkT_d", [4096, T], BF16, kind=skind).ap()
    v_d = nc.dram_tensor("v_d", [NH, 128, T], BF16, kind=skind).ap()
    oT_d = nc.dram_tensor("oT_d", [C, T], BF16, kind=skind).ap()
    x1_d = nc.dram_tensor("x1_d", [T, C], F32, kind=skind).ap()
    wgu_sh = nc.dram_tensor("wgu_sh", [C, 1408], BF16, kind="Internal").ap()
    wdn_sh = nc.dram_tensor("wdn_sh", [I_FFN, 256], BF16, kind="Internal").ap()
    wgu_gd = nc.dram_tensor("wgu_gd", [8 * C, 1408], BF16, kind="Internal", addr_space="Shared").ap()
    wdn_gd = nc.dram_tensor("wdn_gd", [8 * I_FFN, 256], BF16, kind="Internal", addr_space="Shared").ap()
    hT_d = nc.dram_tensor("hT_d", [I_FFN, T], BF16, kind=skind).ap()

    ops = {k: [] for k in ("sync", "tensor", "vector", "scalar", "gpsimd")}
    EM = {k: ops[k].append for k in ops}

    def W(eng, ctr, val):
        if val and val > 0:
            EM[eng](lambda e, h=ctr.h, v=val: e.wait_ge(h, v))

    def DMA(eng, dst, src, ctr):
        n = ctr.bump(16)
        EM[eng](lambda e, d=dst, s=src, h=ctr.h: e.dma_start(d, s).then_inc(h, 16))
        return n

    def INC(eng, ctr, k=1):
        n = ctr.bump(k)
        if eng in ("vector", "scalar", "gpsimd"):
            EM[eng](lambda e, h=ctr.h, v=k: e.drain().then_inc(h, v))
        else:
            EM[eng](lambda e, h=ctr.h, v=k: e.nop().then_inc(h, v))
        return n

    def MM(psum_ap, lhsT, rhs, start, stop, inc=None):
        if inc is None:
            EM["tensor"](lambda e, o=psum_ap, l=lhsT, r=rhs, st=start, sp=stop:
                         e.matmul(o, l, r, start=st, stop=sp))
            return None
        n = inc.bump(1)
        EM["tensor"](lambda e, o=psum_ap, l=lhsT, r=rhs, st=start, sp=stop, h=inc.h:
                     e.matmul(o, l, r, start=st, stop=sp).then_inc(h, 1))
        return n

    from contextlib import ExitStack
    with (
        nc.sbuf_tensor("arena", [128, 100 * KB], BF16) as arena_t,
        nc.psum_tensor("psA", [128, 2048], F32) as psA_t,
        nc.psum_tensor("psB", [128, 2048], F32) as psB_t,
        ExitStack() as _sems,
        nc.Block() as block,
    ):
        (lds_h, ldg_h, sts_h, stg_h, pe_h, dv_h, ac_h, cc_h,
         wma_h, wmb_h, wda_h, wdb_h, hda_h, hdb_h, xla_h, xlb_h) = [
            _sems.enter_context(nc.semaphore(nm)) for nm in
            ("lds", "ldg", "sts", "stg", "pe", "dv", "ac", "cc",
             "wma", "wmb", "wda", "wdb", "hda", "hdb", "xla", "xlb")]
        arena = arena_t.ap()
        psA = psA_t.ap()
        psB = psB_t.ap()
        lds = Ctr(lds_h); ldg = Ctr(ldg_h); sts = Ctr(sts_h); stg = Ctr(stg_h)
        pe = Ctr(pe_h); dv = Ctr(dv_h); ac = Ctr(ac_h); cc = Ctr(cc_h)
        wms = [Ctr(wma_h), Ctr(wmb_h)]
        wds = [Ctr(wda_h), Ctr(wdb_h)]
        hds = [Ctr(hda_h), Ctr(hdb_h)]
        xls = [Ctr(xla_h), Ctr(xlb_h)]

        def cv(off_bytes, shape, dtype):
            assert off_bytes % 64 == 0
            nel = int(np.prod(shape))
            esz = {F32: 4, BF16: 2, F16: 2, I32: 4}[dtype]
            o = off_bytes // 2
            nbf = nel * esz // 2
            assert off_bytes + nel * esz <= 200 * KB, (off_bytes, nel, esz)
            apx = arena[:, o:o + nbf]
            if dtype != BF16:
                apx = apx.bitcast(dtype)
            if len(shape) == 1:
                return apx
            pat = "p (" + " ".join(f"d{i}" for i in range(len(shape))) + ") -> p " + \
                  " ".join(f"d{i}" for i in range(len(shape)))
            kw = {f"d{i}": s for i, s in enumerate(shape)}
            return apx.rearrange(pat, **kw)

        identb = cv(0x0000, [128], BF16)
        masktb = cv(0x2C00, [128], BF16)
        identf = cv(0x0100, [128], F32)
        mask_sb = cv(0x0300, [128], F32)
        battn_sb = cv(0x0500, [64], F32)
        battnq_sb = cv(0x0600, [16], F32)
        bpbc = cv(0x0700, [2048], F32)
        nmax_sb = cv(0x2700, [16], F32)
        rsum_sb = cv(0x2740, [16], F32)
        rinv_sb = cv(0x2780, [16], F32)
        s_sum = cv(0x27C0, [1], F32)
        s_nm = cv(0x2800, [1], F32)
        s_varp = cv(0x2840, [1], F32)
        s_rec = cv(0x2880, [1], F32)
        s_std = cv(0x2B80, [1], F32)
        eps5 = cv(0x2D00, [1], F32)
        eps6 = cv(0x2D40, [1], F32)
        s_rstd = cv(0x28C0, [1], F32)
        s_nmr = cv(0x2900, [1], F32)
        onesf = cv(0x2940, [128], F32)
        TB = 0x3400

        x_tiles = x_in.rearrange("(i p) c -> i p c", p=128)
        x1_tiles = x1_d.rearrange("(i p) c -> i p c", p=128)
        y_tiles = y_out.rearrange("(i p) c -> i p c", p=128)

        brow = cv(TB, [6144], F32)
        bprow = cv(TB + 24 * KB, [2048], F32)

        DMA("sync", identb, identb_in, lds)
        DMA("sync", identf, identf_in, lds)
        DMA("sync", mask_sb, mask_in, lds)
        DMA("sync", brow[0:1, :], battn, lds)
        DMA("sync", bprow[0:1, :], bproj, lds)
        DMA("sync", masktb, masktb_in, lds)
        n_ld0 = DMA("sync", onesf[0:1, :], ones_in, lds)

        W("tensor", lds, n_ld0)
        for m in range(48):
            MM(psA[:, m:m + 1], brow[0:1, bass.ts(m, 128)], onesf[0:1, 0:1], True, True)
        for nb in range(4):
            n_pe0 = MM(psB[:, bass.ts(nb, 512)], onesf[0:1, :], bprow[0:1, bass.ts(nb, 512)],
                       True, True, inc=pe if nb == 3 else None)

        EM["vector"](lambda e: e.memset(eps5[:], 1e-5))
        EM["vector"](lambda e: e.memset(eps6[:], 1e-6))
        W("vector", pe, n_pe0)
        EM["vector"](lambda e: e.tensor_copy(battn_sb[:, 0:48], psA[:, 0:48]))
        EM["vector"](lambda e: e.tensor_copy(bpbc[:], psB[:]))
        EM["vector"](lambda e: e.drain())
        EM["vector"](lambda e: e.tensor_scalar(battnq_sb[:], battn_sb[:, 0:16], CSC, None, ALU.mult))
        dv_p0 = INC("vector", dv)

        DQ = TB + 100 * KB
        dq_w = cv(DQ, [2, 1408], I32)
        dq_sc = cv(DQ + 12 * KB, [2, 1408], BF16)
        dq_mi = cv(DQ + 18 * KB, [1408], I32)
        dq_si = cv(DQ + 24 * KB, [1408], I32)
        dq_m = cv(DQ + 30 * KB, [1408], BF16)
        dq_sg = cv(DQ + 33 * KB, [1408], BF16)
        dq_r1 = cv(DQ + 36 * KB, [1408], BF16)
        dq_r2 = cv(DQ + 39 * KB, [1408], BF16)
        dq_t0 = cv(DQ + 42 * KB, [1408], BF16)
        dq_mag = cv(DQ + 45 * KB, [1408], BF16)
        dq_wv = cv(DQ + 48 * KB, [2, 1408], BF16)

        wgu_sc8 = wgu_sh.rearrange("(a r e) c -> a r e c", a=2, e=8)
        wdn_sc8 = wdn_sh.rearrange("(r e) c -> r e c", e=8)

        dqjobs = [("gu", cc_) for cc_ in range(2)] + [("dn", cc_) for cc_ in range(6)]
        dqstate = {"it": 0, "panel": -1, "ld_done": {}, "panels": [], "done": False,
                   "cc_emitted": False}
        ld_dqw = []; ld_dqs = []; dv_dq_panel = []; st_dq = []

        def dq_geom(pidx):
            kind, cc_ = dqjobs[pidx]
            if kind == "gu":
                cols, npart = 1408, 128
                wsrc = gu_pkt[cc_]
                ssrc = gu_sc[cc_]
            else:
                cols = 256
                npart = 128 if cc_ < 5 else 64
                wsrc = dn_pkt[cc_ * 128:cc_ * 128 + npart, :]
                ssrc = dn_sc[cc_ * 128:cc_ * 128 + npart, :]
            return kind, cc_, cols, npart, wsrc, ssrc

        def emit_dq_load(pidx):
            kind, cc_, cols, npart, wsrc, ssrc = dq_geom(pidx)
            pj = pidx % 2
            if pidx >= 2:
                W("sync", dv, dv_dq_panel[pidx - 2])
                W("gpsimd", dv, dv_dq_panel[pidx - 2])
            ld_dqw.append(DMA("sync", dq_w[0:npart, pj, 0:cols], wsrc, lds))
            ld_dqs.append(DMA("gpsimd", dq_sc[0:npart, pj, 0:cols], ssrc, ldg))

        def emit_dq_piece():
            st = dqstate
            if st["done"]:
                return
            it = st["it"]
            pidx, j = divmod(it, 8)
            if pidx >= len(dqjobs):
                st["done"] = True
                return
            if j == 0:
                if pidx == 0:
                    emit_dq_load(0)
                if pidx + 1 < len(dqjobs):
                    emit_dq_load(pidx + 1)
            kind, cc_, cols, npart, wsrc, ssrc = dq_geom(pidx)
            pj = pidx % 2
            W("vector", lds, lds.n)
            W("vector", ldg, ldg.n)
            wv_ = dq_w[0:npart, pj, 0:cols]
            sc_ = dq_sc[0:npart, pj, 0:cols]
            wb = dq_wv[0:npart, it % 2, 0:cols]
            mi = dq_mi[0:npart, 0:cols]; si = dq_si[0:npart, 0:cols]
            mg = dq_m[0:npart, 0:cols]; sg = dq_sg[0:npart, 0:cols]
            r1 = dq_r1[0:npart, 0:cols]; r2 = dq_r2[0:npart, 0:cols]
            t0 = dq_t0[0:npart, 0:cols]; mag = dq_mag[0:npart, 0:cols]
            EM["vector"](lambda e, w=wv_, j=j, o=mi:
                         e.tensor_scalar(o, w, 4 * j, 7, ALU.logical_shift_right, ALU.bitwise_and))
            EM["vector"](lambda e, w=wv_, j=j, o=si:
                         e.tensor_scalar(o, w, 4 * j + 3, 1, ALU.logical_shift_right, ALU.bitwise_and))
            EM["vector"](lambda e: e.drain())
            EM["vector"](lambda e, i=mi, o=mg: e.tensor_copy(o, i))
            EM["vector"](lambda e, i=si, o=sg: e.tensor_scalar(o, i, -2.0, 1.0, ALU.mult, ALU.add))
            EM["vector"](lambda e: e.drain())
            EM["vector"](lambda e, i=mg, o=r1: e.tensor_scalar(o, i, 4.0, -4.0, ALU.max, ALU.add))
            EM["vector"](lambda e, i=mg, o=r2: e.tensor_scalar(o, i, 6.0, -6.0, ALU.max, ALU.add))
            EM["vector"](lambda e: e.drain())
            EM["vector"](lambda e, i=mg, r=r2, o=t0:
                         e.scalar_tensor_tensor(o, i, 0.5, r, ALU.mult, ALU.add))
            EM["vector"](lambda e: e.drain())
            EM["vector"](lambda e, i=r1, r=t0, o=mag:
                         e.scalar_tensor_tensor(o, i, 0.5, r, ALU.mult, ALU.add))
            EM["vector"](lambda e: e.drain())
            EM["vector"](lambda e, a=mag, b=sg: e.tensor_tensor(a, a, b, ALU.mult))
            EM["vector"](lambda e: e.drain())
            if it >= 2:
                W("vector", stg, stg.n)
            EM["vector"](lambda e, a=mag, b=sc_, o=wb: e.tensor_tensor(o, a, b, ALU.mult))
            ndv = INC("vector", dv)
            if j == 7:
                dv_dq_panel.append(ndv)
            W("gpsimd", dv, ndv)
            if kind == "gu":
                dst = wgu_sc8[cc_, :, j, :]
            else:
                dst = wdn_sc8[:, j, :][cc_ * 128:cc_ * 128 + npart, :]
            st_dq.append(DMA("gpsimd", dst, wb, stg))
            st["it"] = it + 1
            if st["it"] == len(dqjobs) * 8:
                st["done"] = True

        def emit_dq_collective():
            if dqstate["cc_emitted"]:
                return
            while not dqstate["done"]:
                emit_dq_piece()
            W("gpsimd", stg, stg.n)
            EM["gpsimd"](lambda e: e.collective_compute(
                "AllGather", ALU.bypass, replica_groups=[list(range(8))],
                ins=[wgu_sh.opt()], outs=[wgu_gd.opt()]).then_inc(cc_h, 1))
            cc.bump(1)
            EM["gpsimd"](lambda e: e.collective_compute(
                "AllGather", ALU.bypass, replica_groups=[list(range(8))],
                ins=[wdn_sh.opt()], outs=[wdn_gd.opt()]).then_inc(cc_h, 1))
            dqstate["cc_emitted"] = True
            return cc.bump(1)


        if max_phase >= 1:
            XT = TB
            xT = cv(XT, [16, 2048], BF16)
            P1B = TB + 64 * KB
            xbf = cv(P1B, [2, 2048], BF16)

            pe_x = []; dv_tx = []; ld_x = {}
            for i in range(NT):
                if i < 2:
                    if i == 0:
                        W("gpsimd", pe, n_pe0)
                    ld_x[i] = DMA("gpsimd", xbf[:, i % 2, :], x_tiles[i], xls[i % 2])
                W("tensor", xls[i % 2], ld_x[i])
                if i == 0:
                    W("tensor", dv, dv_p0)
                else:
                    W("tensor", dv, dv_tx[i - 1])
                for f in range(NK):
                    n_ = MM(psA[:, bass.ts(f, 128)], xbf[:, i % 2, bass.ts(f, 128)], identb,
                            True, True, inc=pe if f == NK - 1 else None)
                pe_x.append(n_)
                W("vector", pe, pe_x[i])
                EM["vector"](lambda e, i=i: e.tensor_copy(xT[:, :, bass.ts(i, 128)], psA[:]))
                dv_tx.append(INC("vector", dv))
                if i + 2 < NT:
                    W("gpsimd", pe, pe_x[i])
                    ld_x[i + 2] = DMA("gpsimd", xbf[:, i % 2, :], x_tiles[i + 2], xls[i % 2])

        if max_phase >= 2:
            wm = cv(P1B + 8 * KB, [2, 2048], BF16)
            qkvrow = cv(P1B + 16 * KB, [2, 2048], BF16)
            vtile = cv(P1B + 24 * KB, [2048], BF16)

            pe_qkv = []; ac_qkv = []; dv_v = []; st_qk = {}; st_v = []; pe_v = []
            ld_w2 = {}
            ps2 = [psA, psB]

            def emit_tr_v(mv):
                h = mv - 32
                W("tensor", ac, ac_qkv[mv])
                for t in range(NT):
                    n_ = MM(ps2[mv % 2][:, bass.ts(t, 128)],
                            qkvrow[:, mv % 2, bass.ts(t, 128)], identb,
                            True, True, inc=pe if t == NT - 1 else None)
                pe_v.append(n_)
                W("vector", pe, pe_v[h])
                if h >= 1:
                    W("vector", sts, sts.n)
                EM["vector"](lambda e: e.tensor_copy(vtile[:], ps2[mv % 2][:]))
                dv_v.append(INC("vector", dv))
                W("sync", dv, dv_v[h])
                st_v.append(DMA("sync", v_d[h], vtile[:], sts))

            for m in range(49):
                if m < 2:
                    if m == 0:
                        W("gpsimd", pe, pe_x[NT - 1])
                    ld_w2[m] = DMA("gpsimd", wm[:, m % 2, :], wattn[m], wms[m % 2])
                if m <= 47:
                    W("tensor", wms[m % 2], ld_w2[m])
                    if m == 0:
                        W("tensor", dv, dv_tx[NT - 1])
                    if m >= 2:
                        W("tensor", ac, ac_qkv[m - 2])
                        if m - 2 >= 32:
                            W("tensor", dv, dv_v[m - 34])
                    for nb in range(4):
                        for k in range(NK):
                            n_ = MM(ps2[m % 2][:, bass.ts(nb, 512)], wm[:, m % 2, bass.ts(k, 128)],
                                    xT[:, k, bass.ts(nb, 512)], k == 0, k == NK - 1,
                                    inc=pe if (nb == 3 and k == NK - 1) else None)
                    pe_qkv.append(n_)
                if m >= 1 and m - 1 >= 32:
                    emit_tr_v(m - 1)
                if m <= 47:
                    W("scalar", pe, pe_qkv[m])
                    if m >= 2:
                        mp = m - 2
                        if mp < 32:
                            W("scalar", sts, sts.n)
                        else:
                            W("scalar", pe, pe_v[mp - 32])
                    if m < 16:
                        EM["scalar"](lambda e, m=m: e.activation(
                            qkvrow[:, m % 2, :], ps2[m % 2][:], AF.Identity,
                            bias=battnq_sb[:, m:m + 1], scale=CSC))
                    else:
                        EM["scalar"](lambda e, m=m: e.activation(
                            qkvrow[:, m % 2, :], ps2[m % 2][:], AF.Identity,
                            bias=battn_sb[:, m:m + 1], scale=1.0))
                    ac_qkv.append(INC("scalar", ac))
                    if m < 32:
                        W("sync", ac, ac_qkv[m])
                        st_qk[m] = DMA("sync", qkT_d[bass.ts(m, 128), :], qkvrow[:, m % 2, :], sts)
                    if m + 2 < 48:
                        W("gpsimd", pe, pe_qkv[m])
                        ld_w2[m + 2] = DMA("gpsimd", wm[:, m % 2, :], wattn[m + 2], wms[m % 2])
                    if max_phase >= 3:
                        emit_dq_piece()

            pe_p2_done = pe.n
            sts_p2_done = sts.n

        if max_phase >= 3:
            cc_done = emit_dq_collective()
            dv_dq_done = dv.n
            stg_dq_done = stg.n

        if max_phase >= 4:
            AT = TB
            qh = cv(AT, [2, 2048], BF16)
            kh = cv(AT + 8 * KB, [2, 2048], BF16)
            vh = cv(AT + 16 * KB, [2, 2048], BF16)
            pn = cv(AT + 24 * KB, [2, 2048], BF16)
            PT = cv(AT + 32 * KB, [16, 512], BF16)
            orow = cv(AT + 48 * KB, [2, 2048], BF16)

            pe_head = []; st_o = []; ld_h4 = []
            oT5 = cv(TB + 56 * KB, [16, 2048], BF16)
            wp5 = cv(TB + 120 * KB, [16, 2048], BF16)

            def emit_head_loads(h):
                if h == 0:
                    W("sync", dv, dv_dq_done)
                    W("sync", stg, stg_dq_done)
                DMA("sync", qh[:, h % 2, :], qkT_d[bass.ts(h, 128), :], hds[h % 2])
                DMA("sync", kh[:, h % 2, :], qkT_d[2048 + h * 128:2176 + h * 128, :], hds[h % 2])
                ld_h4.append(DMA("sync", vh[:, h % 2, :], v_d[h], hds[h % 2]))

            emit_head_loads(0)
            emit_head_loads(1)

            ac_exp = {}; dv_ptev = {}; ac_oev = []
            ps4 = [psA, psB]
            last_oev = 0
            n_s = {}; n_pt = {}; n_red = {}; n_recip = {}; n_nrm = {}
            for h in range(NH):
                hp = h % 2
                W("tensor", hds[h % 2], ld_h4[h])
                for t in range(17):
                    tl = t - 1
                    if t <= 15:
                        qt = t
                        Lk = (qt + 1) * 128
                        par = qt % 2
                        P = ps4[par]
                        prev = dv_ptev.get((h, qt - 2))
                        if prev is None and qt < 2:
                            prev = dv_ptev.get((h - 1, 14 + par))
                        if prev:
                            W("tensor", dv, prev)
                        elif h == 0 and qt < 2:
                            W("tensor", dv, dv.n)
                            W("tensor", ac, ac.n)
                        if par == 1 and Lk > 1536 and last_oev:
                            W("tensor", ac, last_oev)
                        nnb = (Lk + 511) // 512
                        for nb in range(nnb):
                            N = min(512, Lk - nb * 512)
                            lastc = nb == nnb - 1
                            MM(P[:, nb * 512:nb * 512 + N], qh[:, hp, bass.ts(qt, 128)],
                               kh[:, hp, nb * 512:nb * 512 + N], True, not lastc)
                            if lastc:
                                n_s[t] = MM(P[:, bass.ts(qt, 128)], masktb, identb,
                                            False, True, inc=pe)
                    if tl >= 0:
                        parl = tl % 2
                        Pl = ps4[parl]
                        Lkl = (tl + 1) * 128
                        W("vector", ac, ac_exp[(h, tl)])
                        EM["vector"](lambda e, qt=tl: e.reciprocal(
                            rinv_sb[:, qt:qt + 1], rsum_sb[:, qt:qt + 1]))
                        EM["vector"](lambda e: e.drain())
                        EM["vector"](lambda e, L=Lkl, qt=tl, par=parl: e.tensor_scalar(
                            pn[:, par, 0:L], pn[:, par, 0:L], rinv_sb[:, qt:qt + 1],
                            None, ALU.mult))
                        n_nrm[tl] = INC("vector", dv)
                        W("tensor", dv, n_nrm[tl])
                        nkt = tl + 1
                        for kt in range(nkt):
                            n_pt[tl] = MM(Pl[:, bass.ts(kt, 128)], pn[:, parl, bass.ts(kt, 128)],
                                          identb, True, True, inc=pe if kt == nkt - 1 else None)
                    if t <= 15:
                        W("vector", pe, n_s[t])
                        EM["vector"](lambda e, L=Lk, qt=t, P=P: e.reduce_max(
                            nmax_sb[:, qt:qt + 1], P[:, 0:L], axis=AX.X, negate=True))
                        n_red[t] = INC("vector", dv)
                    if tl >= 0:
                        W("vector", pe, n_pt[tl])
                        EM["vector"](lambda e, nkt=tl + 1, q4=tl % 4, P=Pl: e.tensor_copy(
                            PT[:, 0:nkt, bass.ts(q4, 128)],
                            P[:, 0:nkt * 128].rearrange("p (a b) -> p a b", b=128)))
                        dv_ptev[(h, tl)] = INC("vector", dv)
                    if t <= 15:
                        W("scalar", dv, n_red[t])
                        EM["scalar"](lambda e, L=Lk, qt=t, par=par, P=P: e.activation(
                            pn[:, par, 0:L], P[:, 0:L], AF.Exp,
                            bias=nmax_sb[:, qt:qt + 1], scale=1.0,
                            accum_out=rsum_sb[:, qt:qt + 1]))
                        ac_exp[(h, t)] = INC("scalar", ac)
                    if t >= 4 and t % 4 == 0:
                        qg = t // 4 - 1
                        for r in range(1, 4):
                            kt = qg * 4 + r
                            EM["vector"](lambda e, kt=kt, r=r: e.memset(PT[:, kt, 0:r * 128], 0.0))
                        n_z = INC("vector", dv)
                        W("tensor", dv, n_z)
                        W("tensor", ac, ac_exp[(h, qg * 4 + 3)])
                        nkt_g = qg * 4 + 4
                        for kt in range(nkt_g):
                            n_pv = MM(psB[:, 1536:2048], vh[:, hp, bass.ts(kt, 128)], PT[:, kt, :],
                                      kt == 0, kt == nkt_g - 1,
                                      inc=pe if kt == nkt_g - 1 else None)
                        W("scalar", pe, n_pv)
                        if qg == 0 and h >= 2:
                            W("scalar", sts, sts.n)
                        EM["scalar"](lambda e, qg=qg, hp=hp: e.activation(
                            orow[:, hp, bass.ts(qg, 512)], psB[:, 1536:2048], AF.Identity,
                            bias=0.0, scale=1.0))
                        n_oev = INC("scalar", ac)
                        last_oev = n_oev
                        if qg == 3:
                            ac_oev.append(n_oev)
                pe_head.append(pe.n)
                W("sync", ac, ac_oev[h])
                st_o.append(DMA("sync", oT_d[bass.ts(h, 128), :], orow[:, h % 2, :], sts))
                W("sync", sts, sts.n)
                ld_p5oT = DMA("sync", oT5[:, h, :], oT_d[bass.ts(h, 128), :], lds)
                if h == 0:
                    for k5 in range(NK):
                        DMA("gpsimd", wp5[:, k5, :], wproj[k5], ldg)
                if h + 2 < NH:
                    W("sync", pe, pe_head[h])
                    emit_head_loads(h + 2)

            sts_attn_done = sts.n
            pe_attn_done = pe.n

        if max_phase >= 5:
            P5 = TB + 56 * KB
            oT = cv(P5, [16, 2048], BF16)
            wp = cv(TB + 120 * KB, [16, 2048], BF16)
            P5S = TB
            xres = cv(P5S, [2, 2048], F32)
            rsb = cv(P5S + 16 * KB, [2048], F32)
            sqs = cv(P5S + 24 * KB, [2048], F32)
            x1sb = cv(P5S + 32 * KB, [2, 2048], F32)

            W("sync", sts, sts_attn_done)
            W("sync", pe, pe_attn_done)
            ld_xr = {0: DMA("sync", xres[:, 0, :], x_tiles[0], xls[0]),
                     1: DMA("sync", xres[:, 1, :], x_tiles[1], xls[1])}

            pe_p5 = []; dv_p5 = {}; ac_p5 = {}; st_x1 = []
            ps5 = [psA, psB]
            W("tensor", lds, lds.n)
            W("tensor", ldg, ldg.n)
            W("tensor", sts, sts_attn_done)
            for i in range(NT):
                if i <= 1:
                    W("tensor", ac, ac.n)
                    W("tensor", dv, dv.n)
                else:
                    W("tensor", dv, dv_p5[(i - 2, "bf")])
                for nb in range(4):
                    for k in range(NK):
                        n_ = MM(ps5[i % 2][:, bass.ts(nb, 512)], oT[:, k, bass.ts(i, 128)],
                                wp[:, k, bass.ts(nb, 512)], k == 0, k == NK - 1,
                                inc=pe if (nb == 3 and k == NK - 1) else None)
                pe_p5.append(n_)
                W("vector", pe, pe_p5[i])
                W("vector", xls[i % 2], ld_xr[i])
                if i >= 1:
                    W("vector", ac, ac_p5[(i - 1, "x1")])
                EM["vector"](lambda e, i=i: e.scalar_tensor_tensor(rsb[:], ps5[i % 2][:], 1.0, bpbc[:], ALU.mult, ALU.add))
                dv_p5[(i, "bf")] = INC("vector", dv)
                EM["vector"](lambda e, i=i: e.tensor_tensor(rsb[:], rsb[:], xres[:, i % 2, :], ALU.add))
                EM["vector"](lambda e: e.drain())
                EM["vector"](lambda e: e.reduce_sum(s_sum[:], rsb[:], axis=AX.X))
                EM["vector"](lambda e: e.drain())
                EM["vector"](lambda e: e.tensor_scalar(s_nm[:], s_sum[:], -1.0 / 2048.0, None, ALU.mult))
                dv_p5[(i, "r")] = INC("vector", dv)
                W("scalar", dv, dv_p5[(i, "r")])
                EM["scalar"](lambda e: e.activation(sqs[:], rsb[:], AF.Square, bias=s_nm[:, 0:1],
                                                    scale=1.0, accum_out=s_varp[:, 0:1]))
                EM["scalar"](lambda e: e.drain())
                EM["scalar"](lambda e: e.activation(s_std[:], s_varp[:], AF.Sqrt,
                                                    bias=eps5[:, 0:1], scale=1.0 / 2048.0))
                ac_p5[(i, "sq")] = INC("scalar", ac)
                W("vector", ac, ac_p5[(i, "sq")])
                EM["vector"](lambda e: e.reciprocal(s_rstd[:], s_std[:]))
                EM["vector"](lambda e: e.drain())
                EM["vector"](lambda e: e.tensor_tensor(s_nmr[:], s_nm[:], s_rstd[:], ALU.mult))
                dv_p5[(i, "n")] = INC("vector", dv)
                W("scalar", dv, dv_p5[(i, "n")])
                if i >= 2:
                    W("scalar", sts, sts.n)
                EM["scalar"](lambda e, i=i: e.activation(x1sb[:, i % 2, :], rsb[:], AF.Identity,
                                                         bias=s_nmr[:, 0:1], scale=s_rstd[:, 0:1]))
                ac_p5[(i, "x1")] = INC("scalar", ac)
                W("sync", ac, ac_p5[(i, "x1")])
                st_x1.append(DMA("sync", x1_tiles[i], x1sb[:, i % 2, :], sts))
                if i + 2 < NT:
                    W("sync", dv, dv_p5[(i, "r")])
                    ld_xr[i + 2] = DMA("sync", xres[:, i % 2, :], x_tiles[i + 2], xls[i % 2])
            sts_x1_done = sts.n
            pe_p5_done = pe.n

        if max_phase >= 6:
            x1T = cv(XT, [16, 2048], BF16)
            P5B = TB + 64 * KB
            x1bf = cv(P5B, [2, 2048], BF16)

            pe_x1 = []; dv_tx1 = []; ld_x1 = {}
            for i in range(NT):
                if i < 2:
                    if i == 0:
                        W("gpsimd", sts, sts_x1_done)
                        W("gpsimd", pe, pe_p5_done)
                    ld_x1[i] = DMA("gpsimd", x1bf[:, i % 2, :], x1_tiles[i], xls[i % 2])
                W("tensor", xls[i % 2], ld_x1[i])
                if i == 0:
                    W("tensor", dv, dv.n)
                    W("tensor", ac, ac.n)
                else:
                    W("tensor", dv, dv_tx1[i - 1])
                for f in range(NK):
                    n_ = MM(psA[:, bass.ts(f, 128)], x1bf[:, i % 2, bass.ts(f, 128)], identb,
                            True, True, inc=pe if f == NK - 1 else None)
                pe_x1.append(n_)
                W("vector", pe, pe_x1[i])
                EM["vector"](lambda e, i=i: e.tensor_copy(x1T[:, :, bass.ts(i, 128)], psA[:]))
                dv_tx1.append(INC("vector", dv))
                if i + 2 < NT:
                    W("gpsimd", pe, pe_x1[i])
                    ld_x1[i + 2] = DMA("gpsimd", x1bf[:, i % 2, :], x1_tiles[i + 2], xls[i % 2])

        if max_phase >= 7:
            P6 = TB + 64 * KB
            wg6 = cv(P6, [16, 1408], BF16)
            wu6 = cv(P6 + 45 * KB, [16, 1408], BF16)
            sgt = cv(P6 + 90 * KB, [2048], BF16)
            hrow = cv(P6 + 94 * KB, [2, 2048], BF16)

            wgu_rk = wgu_gd.rearrange("(r k p) c -> r k p c", r=8, p=128)
            hT_rows = hT_d.rearrange("(k p) t -> k p t", p=128)
            pe_p6 = {}; st_h6 = []
            ac_p6s = None; dv_p6h = None
            for a in range(4):
                ld_g = 0
                for k in range(NK):
                    if k == 0:
                        if a == 0:
                            W("sync", pe, pe.n)
                        else:
                            W("sync", pe, pe_p6[(a - 1, 10)])
                if a == 0:
                    W("sync", cc, cc_done)
                for k in range(NK):
                    ld_g = DMA("sync", wg6[:, k, :], wgu_rk[a, k], lds)
                ld_u = 0
                for k in range(NK):
                    if k == 0:
                        if a == 0:
                            W("gpsimd", pe, pe.n)
                        else:
                            W("gpsimd", pe, pe_p6[(a - 1, 10)])
                for k in range(NK):
                    ld_u = DMA("gpsimd", wu6[:, k, :], wgu_rk[4 + a, k], ldg)
                for mpl in range(11):
                    mp = a * 11 + mpl
                    W("tensor", lds, ld_g)
                    W("tensor", ldg, ld_u)
                    if mp == 0:
                        W("tensor", dv, dv.n)
                        W("tensor", ac, ac.n)
                    else:
                        W("tensor", ac, ac_p6s)
                        W("tensor", dv, dv_p6h)
                    for nb in range(4):
                        for k in range(NK):
                            n_g = MM(psA[:, bass.ts(nb, 512)], wg6[:, k, bass.ts(mpl, 128)],
                                     x1T[:, k, bass.ts(nb, 512)], k == 0, k == NK - 1,
                                     inc=pe if (nb == 3 and k == NK - 1) else None)
                    for nb in range(4):
                        for k in range(NK):
                            n_u = MM(psB[:, bass.ts(nb, 512)], wu6[:, k, bass.ts(mpl, 128)],
                                     x1T[:, k, bass.ts(nb, 512)], k == 0, k == NK - 1,
                                     inc=pe if (nb == 3 and k == NK - 1) else None)
                    W("scalar", pe, n_g)
                    if mp >= 1:
                        W("scalar", dv, dv_p6h)
                    EM["scalar"](lambda e: e.activation(sgt[:], psA[:], AF.Silu, bias=0.0, scale=1.0))
                    ac_p6s = INC("scalar", ac)
                    W("vector", ac, ac_p6s)
                    W("vector", pe, n_u)
                    if mp >= 2:
                        W("vector", sts, sts.n)
                    EM["vector"](lambda e, mp=mp: e.tensor_tensor(hrow[:, mp % 2, :], sgt[:],
                                                                  psB[:], ALU.mult))
                    dv_p6h = INC("vector", dv)
                    pe_p6[(a, mpl)] = pe.n
                    W("sync", dv, dv_p6h)
                    st_h6.append(DMA("sync", hT_rows[0].tensor.ap()
                                     .rearrange("(k p) t -> k p t", p=128)[mp] if False else
                                     hT_d[bass.ts(mp, 128), :], hrow[:, mp % 2, :], sts))
            sts_h_done = sts.n

        if max_phase >= 8:
            P7 = TB
            hT7 = cv(P7, [2, 44, 512], BF16)
            dT = cv(P7 + 88 * KB, [16, 512], F32)
            wdc = cv(P7 + 120 * KB, [2, 8, 512], BF16)
            EPI = P7 + 136 * KB
            sq7 = cv(EPI, [2048], F32)
            x1r7 = cv(EPI + 8 * KB, [2, 2048], F32)
            dsb = cv(EPI + 24 * KB, [2048], F32)
            r2sb = cv(EPI + 32 * KB, [2048], F32)
            outsb = cv(EPI + 40 * KB, [2048], F32)

            wdn_pkc8 = wdn_gd.rearrange("(r K p) c -> r p K c", r=8, p=128)
            hT_pkc = hT_d.rearrange("(K p) t -> p K t", p=128)
            accs = [psA[:, bass.ts(m4, 512)] for m4 in range(4)]

            st_y = []
            pe_dn_hist = {}
            for tg in range(4):
                tb = tg % 2
                ld_h7 = 0
                for gb in range(11):
                    if gb == 0:
                        W("gpsimd", ldg, ldg.n)
                        if tg == 0:
                            W("gpsimd", sts, sts_h_done)
                            W("gpsimd", pe, pe.n)
                            W("gpsimd", ac, ac.n)
                            W("gpsimd", dv, dv.n)
                        elif tg >= 2:
                            W("gpsimd", pe, pe_dn_hist[tg - 2])
                    ld_h7 = DMA("gpsimd", hT7[:, tb, gb * 4:gb * 4 + 4, :],
                                hT_pkc[:, gb * 4:gb * 4 + 4, bass.ts(tg, 512)], ldg)
                for mg in range(4):
                    pe_p7mm = []
                    GS = [8, 8, 8, 8, 8, 4]
                    GO = [0, 8, 16, 24, 32, 40]
                    def emit_wdc_load(g):
                        n = GS[g]
                        DMA("sync", wdc[:, g % 2, 0:n, 0:256],
                            wdn_pkc8[2 * mg, :, GO[g]:GO[g] + n, :], wds[g % 2])
                        return DMA("sync", wdc[:, g % 2, 0:n, 256:512],
                                   wdn_pkc8[2 * mg + 1, :, GO[g]:GO[g] + n, :], wds[g % 2])
                    ld_wd = {}
                    for g in range(2):
                        if g == 0:
                            W("sync", pe, pe.n)
                            if tg == 0 and mg == 0:
                                W("sync", cc, cc_done)
                                W("sync", ac, ac.n)
                                W("sync", dv, dv.n)
                        ld_wd[g] = emit_wdc_load(g)
                    for kq in range(44):
                        g = kq // 8
                        gk = kq % 8
                        if gk == 0:
                            W("tensor", wds[g % 2], ld_wd[g])
                        if kq == 0:
                            W("tensor", ldg, ldg.n)
                            if mg == 0:
                                if tg == 0:
                                    W("tensor", dv, dv.n)
                                    W("tensor", ac, ac.n)
                                else:
                                    W("tensor", dv, dv_dT)
                            if mg > 0 or tg > 0:
                                W("tensor", dv, dv_dT_last)
                        for m4 in range(4):
                            n_ = MM(accs[m4], wdc[:, g % 2, gk, bass.ts(m4, 128)],
                                    hT7[:, tb, kq, :], kq == 0, kq == 43,
                                    inc=pe if (m4 == 3) else None)
                        pe_p7mm.append(n_)
                        if gk == GS[g] - 1 and g + 2 < 6:
                            W("sync", pe, pe_p7mm[kq])
                            ld_wd[g + 2] = emit_wdc_load(g + 2)
                    W("vector", pe, pe_p7mm[-1])
                    for m4 in range(4):
                        m = mg * 4 + m4
                        EM["vector"](lambda e, m=m, m4=m4: e.tensor_copy(dT[:, m, :], accs[m4]))
                    dv_dT_last = INC("vector", dv)
                dv_dT = dv_dT_last
                pe_dn_hist[tg] = pe.n
                for il in range(4):
                    i = tg * 4 + il
                    if il < 2:
                        if il == 0 and tg == 0:
                            W("scalar", dv, dv_dT)
                        n_x1r = DMA("scalar", x1r7[:, il % 2, :], x1_tiles[i], sts)
                        if il == 0:
                            ld_x1r = {}
                        ld_x1r[il] = n_x1r
                    W("tensor", dv, dv_dT)
                    if il >= 1:
                        W("tensor", dv, dv_p7tr[il - 1])
                    elif tg > 0:
                        W("tensor", dv, dv_p7tr[3])
                        W("tensor", ac, ac.n)
                    for m in range(NK):
                        n_tr = MM(psB[:, bass.ts(m, 128)], dT[:, m, bass.ts(il, 128)], identf,
                                  True, True, inc=pe if m == NK - 1 else None)
                    if il == 0:
                        dv_p7tr = []
                    W("vector", pe, n_tr)
                    if il >= 1 or tg > 0:
                        W("vector", ac, ac_ep_last)
                    EM["vector"](lambda e: e.tensor_copy(dsb[:], psB[:]))
                    dv_p7tr.append(INC("vector", dv))
                    W("scalar", dv, dv_p7tr[il])
                    EM["scalar"](lambda e: e.activation(sq7[:], dsb[:], AF.Square, bias=0.0,
                                                        scale=1.0, accum_out=s_varp[:, 0:1]))
                    EM["scalar"](lambda e: e.drain())
                    EM["scalar"](lambda e: e.activation(s_std[:], s_varp[:], AF.Sqrt,
                                                        bias=eps6[:, 0:1], scale=1.0 / 2048.0))
                    n_ac1 = INC("scalar", ac)
                    W("vector", ac, n_ac1)
                    W("vector", sts, sts.n)
                    EM["vector"](lambda e: e.reciprocal(s_rstd[:], s_std[:]))
                    EM["vector"](lambda e: e.drain())
                    EM["vector"](lambda e, il=il: e.scalar_tensor_tensor(
                        r2sb[:], dsb[:], s_rstd[:, 0:1], x1r7[:, il % 2, :], ALU.mult, ALU.add))
                    EM["vector"](lambda e: e.drain())
                    EM["vector"](lambda e: e.reduce_sum(s_sum[:], r2sb[:], axis=AX.X))
                    EM["vector"](lambda e: e.drain())
                    EM["vector"](lambda e: e.tensor_scalar(s_nm[:], s_sum[:], -1.0 / 2048.0, None, ALU.mult))
                    n_dv2 = INC("vector", dv)
                    W("scalar", dv, n_dv2)
                    EM["scalar"](lambda e: e.activation(sq7[:], r2sb[:], AF.Square, bias=s_nm[:, 0:1],
                                                        scale=1.0, accum_out=s_varp[:, 0:1]))
                    EM["scalar"](lambda e: e.drain())
                    EM["scalar"](lambda e: e.activation(s_std[:], s_varp[:], AF.Sqrt,
                                                        bias=eps5[:, 0:1], scale=1.0 / 2048.0))
                    n_ac3 = INC("scalar", ac)
                    W("vector", ac, n_ac3)
                    EM["vector"](lambda e: e.reciprocal(s_rstd[:], s_std[:]))
                    EM["vector"](lambda e: e.drain())
                    EM["vector"](lambda e: e.tensor_tensor(s_nmr[:], s_nm[:], s_rstd[:], ALU.mult))
                    n_dv4 = INC("vector", dv)
                    W("scalar", dv, n_dv4)
                    W("scalar", sts, sts.n)
                    EM["scalar"](lambda e, il=il: e.activation(outsb[:], r2sb[:], AF.Identity,
                                                               bias=s_nmr[:, 0:1], scale=s_rstd[:, 0:1]))
                    EM["scalar"](lambda e: e.drain())
                    st_y.append(DMA("scalar", y_tiles[i], outsb[:], sts))
                    ac_ep_last = INC("scalar", ac)
                    if il + 2 < 4:
                        ld_x1r[il + 2] = DMA("scalar", x1r7[:, il % 2, :], x1_tiles[i + 2], sts)

        W("sync", sts, sts.n)
        W("sync", stg, stg.n)
        W("gpsimd", ldg, ldg.n)
        W("sync", lds, lds.n)

        @block.sync
        def _(e):
            for f in ops["sync"]:
                f(e)

        @block.tensor
        def _(e):
            for f in ops["tensor"]:
                f(e)

        @block.vector
        def _(e):
            for f in ops["vector"]:
                f(e)

        @block.scalar
        def _(e):
            for f in ops["scalar"]:
                f(e)

        @block.gpsimd
        def _(e):
            for f in ops["gpsimd"]:
                f(e)

    return nc


def prep_inputs(L0_attn_x, L0_attn_w_attn, L0_attn_b_attn, L0_attn_w_proj,
                L0_attn_b_proj, L0_ffn_gate_up_packed, L0_ffn_gate_up_scales,
                L0_ffn_down_packed, L0_ffn_down_scales):
    x = np.ascontiguousarray(np.asarray(L0_attn_x, dtype=np.float32))
    wa = np.asarray(L0_attn_w_attn, dtype=np.float32)
    wattn = np.ascontiguousarray(wa.reshape(16, 128, 48, 128).transpose(2, 1, 0, 3)
                                 .reshape(48, 128, 2048))
    battn = np.ascontiguousarray(np.asarray(L0_attn_b_attn, dtype=np.float32).reshape(1, 6144))
    wproj = np.ascontiguousarray(np.asarray(L0_attn_w_proj, dtype=np.float32).reshape(16, 128, 2048))
    bprojv = np.ascontiguousarray(np.asarray(L0_attn_b_proj, dtype=np.float32).reshape(1, 2048))
    gu_pkt_f = np.asarray(L0_ffn_gate_up_packed, dtype=np.int32).T.reshape(2, 128, GU_COLS)
    gu_sc_f = np.repeat(np.asarray(L0_ffn_gate_up_scales, dtype=np.float16),
                        16, axis=0).reshape(2, 128, GU_COLS)
    dn_pkt_f = np.asarray(L0_ffn_down_packed, dtype=np.int32).T
    dn_sc_f = np.repeat(np.asarray(L0_ffn_down_scales, dtype=np.float16), 16, axis=0)
    identb = np.eye(128, dtype=ml_dtypes.bfloat16)
    identf = np.eye(128, dtype=np.float32)
    mask = np.triu(np.full((128, 128), NEG, dtype=np.float32), k=1)
    masktb = np.ascontiguousarray(mask.T).astype(ml_dtypes.bfloat16)
    ones1 = np.ones((1, 128), dtype=np.float32)

    common = dict(wattn=wattn, battn=battn, wproj=wproj, bproj=bprojv,
                  identb=identb, identf=identf, mask=mask, masktb=masktb, ones1=ones1)
    in_maps = []
    for b in range(8):
        m = dict(common)
        m["x"] = np.ascontiguousarray(x[b])
        m["gu_pkt"] = np.ascontiguousarray(gu_pkt_f[:, :, b * 1408:(b + 1) * 1408])
        m["gu_sc"] = np.ascontiguousarray(gu_sc_f[:, :, b * 1408:(b + 1) * 1408])
        m["dn_pkt"] = np.ascontiguousarray(dn_pkt_f[:, b * 256:(b + 1) * 256])
        m["dn_sc"] = np.ascontiguousarray(dn_sc_f[:, b * 256:(b + 1) * 256])
        in_maps.append(m)
    return in_maps


def kernel(**inputs):
    in_maps = prep_inputs(**inputs)
    nc = build_program(dbg=False)
    res = run_bass_kernel_spmd(nc, in_maps, core_ids=list(range(8)))
    out = np.stack([np.asarray(r["y"], dtype=np.float32) for r in res.results], axis=0)
    return out


# version 22 (final)
# speedup vs baseline: 1.2005x; 1.0110x over previous
import math

import numpy as np
import ml_dtypes

import concourse.bass as bass
import concourse.mybir as mybir
from concourse.bass_utils import run_bass_kernel_spmd

F32 = mybir.dt.float32
BF16 = mybir.dt.bfloat16
F16 = mybir.dt.float16
I32 = mybir.dt.int32
AF = mybir.ActivationFunctionType
ALU = mybir.AluOpType
AX = mybir.AxisListType

T = 2048
C = 2048
NT = 16
NK = 16
NH = 16
HS = 128
I_FFN = 5632
GU_COLS = 2 * I_FFN
CSC = 1.0 / math.sqrt(HS)
NEG = -1.0e30
KB = 1024


class Ctr:
    def __init__(self, h):
        self.h = h
        self.n = 0

    def bump(self, k):
        self.n += k
        return self.n


def build_program(dbg=False, max_phase=9):
    nc = bass.Bass("TRN2", target_bir_lowering=False)

    x_in = nc.dram_tensor("x", [T, C], F32, kind="ExternalInput").ap()
    wattn = nc.dram_tensor("wattn", [48, 128, 2048], F32, kind="ExternalInput").ap()
    battn = nc.dram_tensor("battn", [1, 6144], F32, kind="ExternalInput").ap()
    wproj = nc.dram_tensor("wproj", [16, 128, 2048], F32, kind="ExternalInput").ap()
    bproj = nc.dram_tensor("bproj", [1, 2048], F32, kind="ExternalInput").ap()
    gu_pkt = nc.dram_tensor("gu_pkt", [2, 128, 1408], I32, kind="ExternalInput").ap()
    gu_sc = nc.dram_tensor("gu_sc", [2, 128, 1408], F16, kind="ExternalInput").ap()
    dn_pkt = nc.dram_tensor("dn_pkt", [704, 256], I32, kind="ExternalInput").ap()
    dn_sc = nc.dram_tensor("dn_sc", [704, 256], F16, kind="ExternalInput").ap()
    identb_in = nc.dram_tensor("identb", [128, 128], BF16, kind="ExternalInput").ap()
    identf_in = nc.dram_tensor("identf", [128, 128], F32, kind="ExternalInput").ap()
    mask_in = nc.dram_tensor("mask", [128, 128], F32, kind="ExternalInput").ap()
    masktb_in = nc.dram_tensor("masktb", [128, 128], BF16, kind="ExternalInput").ap()
    ones_in = nc.dram_tensor("ones1", [1, 128], F32, kind="ExternalInput").ap()

    y_out = nc.dram_tensor("y", [T, C], F32, kind="ExternalOutput").ap()

    skind = "ExternalOutput" if dbg else "Internal"
    qkT_d = nc.dram_tensor("qkT_d", [4096, T], BF16, kind=skind).ap()
    v_d = nc.dram_tensor("v_d", [NH, 128, T], BF16, kind=skind).ap()
    oT_d = nc.dram_tensor("oT_d", [C, T], BF16, kind=skind).ap()
    x1_d = nc.dram_tensor("x1_d", [T, C], F32, kind=skind).ap()
    wgu_sh = nc.dram_tensor("wgu_sh", [C, 1408], BF16, kind="Internal").ap()
    wdn_sh = nc.dram_tensor("wdn_sh", [I_FFN, 256], BF16, kind="Internal").ap()
    wgu_gd = nc.dram_tensor("wgu_gd", [8 * C, 1408], BF16, kind="Internal", addr_space="Shared").ap()
    wdn_gd = nc.dram_tensor("wdn_gd", [8 * I_FFN, 256], BF16, kind="Internal", addr_space="Shared").ap()
    hT_d = nc.dram_tensor("hT_d", [I_FFN, T], BF16, kind=skind).ap()

    ops = {k: [] for k in ("sync", "tensor", "vector", "scalar", "gpsimd")}
    EM = {k: ops[k].append for k in ops}

    def W(eng, ctr, val):
        if val and val > 0:
            EM[eng](lambda e, h=ctr.h, v=val: e.wait_ge(h, v))

    def DMA(eng, dst, src, ctr):
        n = ctr.bump(16)
        EM[eng](lambda e, d=dst, s=src, h=ctr.h: e.dma_start(d, s).then_inc(h, 16))
        return n

    def INC(eng, ctr, k=1):
        n = ctr.bump(k)
        if eng in ("vector", "scalar", "gpsimd"):
            EM[eng](lambda e, h=ctr.h, v=k: e.drain().then_inc(h, v))
        else:
            EM[eng](lambda e, h=ctr.h, v=k: e.nop().then_inc(h, v))
        return n

    def MM(psum_ap, lhsT, rhs, start, stop, inc=None):
        if inc is None:
            EM["tensor"](lambda e, o=psum_ap, l=lhsT, r=rhs, st=start, sp=stop:
                         e.matmul(o, l, r, start=st, stop=sp))
            return None
        n = inc.bump(1)
        EM["tensor"](lambda e, o=psum_ap, l=lhsT, r=rhs, st=start, sp=stop, h=inc.h:
                     e.matmul(o, l, r, start=st, stop=sp).then_inc(h, 1))
        return n

    from contextlib import ExitStack
    with (
        nc.sbuf_tensor("arena", [128, 100 * KB], BF16) as arena_t,
        nc.psum_tensor("psA", [128, 2048], F32) as psA_t,
        nc.psum_tensor("psB", [128, 2048], F32) as psB_t,
        ExitStack() as _sems,
        nc.Block() as block,
    ):
        (lds_h, ldg_h, sts_h, stg_h, pe_h, dv_h, ac_h, cc_h,
         wma_h, wmb_h, wda_h, wdb_h, hda_h, hdb_h, xla_h, xlb_h) = [
            _sems.enter_context(nc.semaphore(nm)) for nm in
            ("lds", "ldg", "sts", "stg", "pe", "dv", "ac", "cc",
             "wma", "wmb", "wda", "wdb", "hda", "hdb", "xla", "xlb")]
        arena = arena_t.ap()
        psA = psA_t.ap()
        psB = psB_t.ap()
        lds = Ctr(lds_h); ldg = Ctr(ldg_h); sts = Ctr(sts_h); stg = Ctr(stg_h)
        pe = Ctr(pe_h); dv = Ctr(dv_h); ac = Ctr(ac_h); cc = Ctr(cc_h)
        wms = [Ctr(wma_h), Ctr(wmb_h)]
        wds = [Ctr(wda_h), Ctr(wdb_h)]
        hds = [Ctr(hda_h), Ctr(hdb_h)]
        xls = [Ctr(xla_h), Ctr(xlb_h)]

        def cv(off_bytes, shape, dtype):
            assert off_bytes % 64 == 0
            nel = int(np.prod(shape))
            esz = {F32: 4, BF16: 2, F16: 2, I32: 4}[dtype]
            o = off_bytes // 2
            nbf = nel * esz // 2
            assert off_bytes + nel * esz <= 200 * KB, (off_bytes, nel, esz)
            apx = arena[:, o:o + nbf]
            if dtype != BF16:
                apx = apx.bitcast(dtype)
            if len(shape) == 1:
                return apx
            pat = "p (" + " ".join(f"d{i}" for i in range(len(shape))) + ") -> p " + \
                  " ".join(f"d{i}" for i in range(len(shape)))
            kw = {f"d{i}": s for i, s in enumerate(shape)}
            return apx.rearrange(pat, **kw)

        identb = cv(0x0000, [128], BF16)
        masktb = cv(0x2C00, [128], BF16)
        identf = cv(0x0100, [128], F32)
        mask_sb = cv(0x0300, [128], F32)
        battn_sb = cv(0x0500, [64], F32)
        battnq_sb = cv(0x0600, [16], F32)
        bpbc = cv(0x0700, [2048], F32)
        nmax_sb = cv(0x2700, [16], F32)
        rsum_sb = cv(0x2740, [16], F32)
        rinv_sb = cv(0x2780, [16], F32)
        s_sum = cv(0x27C0, [1], F32)
        s_nm = cv(0x2800, [1], F32)
        s_varp = cv(0x2840, [1], F32)
        s_rec = cv(0x2880, [1], F32)
        s_std = cv(0x2B80, [1], F32)
        eps5 = cv(0x2D00, [1], F32)
        eps6 = cv(0x2D40, [1], F32)
        s_rstd = cv(0x28C0, [1], F32)
        s_nmr = cv(0x2900, [1], F32)
        onesf = cv(0x2940, [128], F32)
        TB = 0x3400

        x_tiles = x_in.rearrange("(i p) c -> i p c", p=128)
        x1_tiles = x1_d.rearrange("(i p) c -> i p c", p=128)
        y_tiles = y_out.rearrange("(i p) c -> i p c", p=128)

        brow = cv(TB, [6144], F32)
        bprow = cv(TB + 24 * KB, [2048], F32)

        DMA("sync", identb, identb_in, lds)
        DMA("sync", identf, identf_in, lds)
        DMA("sync", mask_sb, mask_in, lds)
        DMA("sync", brow[0:1, :], battn, lds)
        DMA("sync", bprow[0:1, :], bproj, lds)
        DMA("sync", masktb, masktb_in, lds)
        n_ld0 = DMA("sync", onesf[0:1, :], ones_in, lds)

        W("tensor", lds, n_ld0)
        for m in range(48):
            MM(psA[:, m:m + 1], brow[0:1, bass.ts(m, 128)], onesf[0:1, 0:1], True, True)
        for nb in range(4):
            n_pe0 = MM(psB[:, bass.ts(nb, 512)], onesf[0:1, :], bprow[0:1, bass.ts(nb, 512)],
                       True, True, inc=pe if nb == 3 else None)

        EM["vector"](lambda e: e.memset(eps5[:], 1e-5))
        EM["vector"](lambda e: e.memset(eps6[:], 1e-6))
        W("vector", pe, n_pe0)
        EM["vector"](lambda e: e.tensor_copy(battn_sb[:, 0:48], psA[:, 0:48]))
        EM["vector"](lambda e: e.tensor_copy(bpbc[:], psB[:]))
        EM["vector"](lambda e: e.drain())
        EM["vector"](lambda e: e.tensor_scalar(battnq_sb[:], battn_sb[:, 0:16], CSC, None, ALU.mult))
        dv_p0 = INC("vector", dv)

        DQ = TB + 100 * KB
        dq_w = cv(DQ, [2, 1408], I32)
        dq_sc = cv(DQ + 12 * KB, [2, 1408], BF16)
        dq_mi = cv(DQ + 18 * KB, [1408], I32)
        dq_si = cv(DQ + 24 * KB, [1408], I32)
        dq_m = cv(DQ + 30 * KB, [1408], BF16)
        dq_sg = cv(DQ + 33 * KB, [1408], BF16)
        dq_r1 = cv(DQ + 36 * KB, [1408], BF16)
        dq_r2 = cv(DQ + 39 * KB, [1408], BF16)
        dq_t0 = cv(DQ + 42 * KB, [1408], BF16)
        dq_mag = cv(DQ + 45 * KB, [1408], BF16)
        dq_wv = cv(DQ + 48 * KB, [2, 1408], BF16)

        wgu_sc8 = wgu_sh.rearrange("(a r e) c -> a r e c", a=2, e=8)
        wdn_sc8 = wdn_sh.rearrange("(r e) c -> r e c", e=8)

        dqjobs = [("gu", cc_) for cc_ in range(2)] + [("dn", cc_) for cc_ in range(6)]
        dqstate = {"it": 0, "panel": -1, "ld_done": {}, "panels": [], "done": False,
                   "cc_emitted": False}
        ld_dqw = []; ld_dqs = []; dv_dq_panel = []; st_dq = []

        def dq_geom(pidx):
            kind, cc_ = dqjobs[pidx]
            if kind == "gu":
                cols, npart = 1408, 128
                wsrc = gu_pkt[cc_]
                ssrc = gu_sc[cc_]
            else:
                cols = 256
                npart = 128 if cc_ < 5 else 64
                wsrc = dn_pkt[cc_ * 128:cc_ * 128 + npart, :]
                ssrc = dn_sc[cc_ * 128:cc_ * 128 + npart, :]
            return kind, cc_, cols, npart, wsrc, ssrc

        def emit_dq_load(pidx):
            kind, cc_, cols, npart, wsrc, ssrc = dq_geom(pidx)
            pj = pidx % 2
            if pidx >= 2:
                W("sync", dv, dv_dq_panel[pidx - 2])
                W("gpsimd", dv, dv_dq_panel[pidx - 2])
            ld_dqw.append(DMA("sync", dq_w[0:npart, pj, 0:cols], wsrc, lds))
            ld_dqs.append(DMA("gpsimd", dq_sc[0:npart, pj, 0:cols], ssrc, ldg))

        def emit_dq_piece():
            st = dqstate
            if st["done"]:
                return
            it = st["it"]
            pidx, j = divmod(it, 8)
            if pidx >= len(dqjobs):
                st["done"] = True
                return
            if j == 0:
                if pidx == 0:
                    emit_dq_load(0)
                if pidx + 1 < len(dqjobs):
                    emit_dq_load(pidx + 1)
            kind, cc_, cols, npart, wsrc, ssrc = dq_geom(pidx)
            pj = pidx % 2
            W("vector", lds, lds.n)
            W("vector", ldg, ldg.n)
            wv_ = dq_w[0:npart, pj, 0:cols]
            sc_ = dq_sc[0:npart, pj, 0:cols]
            wb = dq_wv[0:npart, it % 2, 0:cols]
            mi = dq_mi[0:npart, 0:cols]; si = dq_si[0:npart, 0:cols]
            mg = dq_m[0:npart, 0:cols]; sg = dq_sg[0:npart, 0:cols]
            r1 = dq_r1[0:npart, 0:cols]; r2 = dq_r2[0:npart, 0:cols]
            t0 = dq_t0[0:npart, 0:cols]; mag = dq_mag[0:npart, 0:cols]
            EM["vector"](lambda e, w=wv_, j=j, o=mi:
                         e.tensor_scalar(o, w, 4 * j, 7, ALU.logical_shift_right, ALU.bitwise_and))
            EM["vector"](lambda e, w=wv_, j=j, o=si:
                         e.tensor_scalar(o, w, 4 * j + 3, 1, ALU.logical_shift_right, ALU.bitwise_and))
            EM["vector"](lambda e: e.drain())
            EM["vector"](lambda e, i=mi, o=mg: e.tensor_copy(o, i))
            EM["vector"](lambda e, i=si, o=sg: e.tensor_scalar(o, i, -2.0, 1.0, ALU.mult, ALU.add))
            EM["vector"](lambda e: e.drain())
            EM["vector"](lambda e, i=mg, o=r1: e.tensor_scalar(o, i, 4.0, -4.0, ALU.max, ALU.add))
            EM["vector"](lambda e, i=mg, o=r2: e.tensor_scalar(o, i, 6.0, -6.0, ALU.max, ALU.add))
            EM["vector"](lambda e: e.drain())
            EM["vector"](lambda e, i=mg, r=r2, o=t0:
                         e.scalar_tensor_tensor(o, i, 0.5, r, ALU.mult, ALU.add))
            EM["vector"](lambda e: e.drain())
            EM["vector"](lambda e, i=r1, r=t0, o=mag:
                         e.scalar_tensor_tensor(o, i, 0.5, r, ALU.mult, ALU.add))
            EM["vector"](lambda e: e.drain())
            EM["vector"](lambda e, a=mag, b=sg: e.tensor_tensor(a, a, b, ALU.mult))
            EM["vector"](lambda e: e.drain())
            if it >= 2:
                W("vector", stg, stg.n)
            EM["vector"](lambda e, a=mag, b=sc_, o=wb: e.tensor_tensor(o, a, b, ALU.mult))
            ndv = INC("vector", dv)
            if j == 7:
                dv_dq_panel.append(ndv)
            W("gpsimd", dv, ndv)
            if kind == "gu":
                dst = wgu_sc8[cc_, :, j, :]
            else:
                dst = wdn_sc8[:, j, :][cc_ * 128:cc_ * 128 + npart, :]
            st_dq.append(DMA("gpsimd", dst, wb, stg))
            st["it"] = it + 1
            if st["it"] == len(dqjobs) * 8:
                st["done"] = True

        def emit_dq_collective():
            if dqstate["cc_emitted"]:
                return
            while not dqstate["done"]:
                emit_dq_piece()
            W("gpsimd", stg, stg.n)
            EM["gpsimd"](lambda e: e.collective_compute(
                "AllGather", ALU.bypass, replica_groups=[list(range(8))],
                ins=[wgu_sh.opt()], outs=[wgu_gd.opt()]).then_inc(cc_h, 1))
            cc.bump(1)
            EM["gpsimd"](lambda e: e.collective_compute(
                "AllGather", ALU.bypass, replica_groups=[list(range(8))],
                ins=[wdn_sh.opt()], outs=[wdn_gd.opt()]).then_inc(cc_h, 1))
            dqstate["cc_emitted"] = True
            return cc.bump(1)


        if max_phase >= 1:
            XT = TB
            xT = cv(XT, [16, 2048], BF16)
            P1B = TB + 64 * KB
            xbf = cv(P1B, [2, 2048], BF16)

            pe_x = []; dv_tx = []; ld_x = {}
            for i in range(NT):
                if i < 2:
                    if i == 0:
                        W("gpsimd", pe, n_pe0)
                    ld_x[i] = DMA("gpsimd", xbf[:, i % 2, :], x_tiles[i], xls[i % 2])
                W("tensor", xls[i % 2], ld_x[i])
                if i == 0:
                    W("tensor", dv, dv_p0)
                else:
                    W("tensor", dv, dv_tx[i - 1])
                for f in range(NK):
                    n_ = MM(psA[:, bass.ts(f, 128)], xbf[:, i % 2, bass.ts(f, 128)], identb,
                            True, True, inc=pe if f == NK - 1 else None)
                pe_x.append(n_)
                W("vector", pe, pe_x[i])
                EM["vector"](lambda e, i=i: e.tensor_copy(xT[:, :, bass.ts(i, 128)], psA[:]))
                dv_tx.append(INC("vector", dv))
                if i + 2 < NT:
                    W("gpsimd", pe, pe_x[i])
                    ld_x[i + 2] = DMA("gpsimd", xbf[:, i % 2, :], x_tiles[i + 2], xls[i % 2])

        if max_phase >= 2:
            wm = cv(P1B + 8 * KB, [2, 2048], BF16)
            qkvrow = cv(P1B + 16 * KB, [2, 2048], BF16)
            vtile = cv(P1B + 24 * KB, [2048], BF16)

            pe_qkv = []; ac_qkv = []; dv_v = []; st_qk = {}; st_v = []; pe_v = []
            ld_w2 = {}
            ps2 = [psA, psB]

            def emit_tr_v(mv):
                h = mv - 32
                W("tensor", ac, ac_qkv[mv])
                for t in range(NT):
                    n_ = MM(ps2[mv % 2][:, bass.ts(t, 128)],
                            qkvrow[:, mv % 2, bass.ts(t, 128)], identb,
                            True, True, inc=pe if t == NT - 1 else None)
                pe_v.append(n_)
                W("vector", pe, pe_v[h])
                if h >= 1:
                    W("vector", sts, sts.n)
                EM["vector"](lambda e: e.tensor_copy(vtile[:], ps2[mv % 2][:]))
                dv_v.append(INC("vector", dv))
                W("sync", dv, dv_v[h])
                st_v.append(DMA("sync", v_d[h], vtile[:], sts))

            for m in range(49):
                if m < 2:
                    if m == 0:
                        W("gpsimd", pe, pe_x[NT - 1])
                    ld_w2[m] = DMA("gpsimd", wm[:, m % 2, :], wattn[m], wms[m % 2])
                if m <= 47:
                    W("tensor", wms[m % 2], ld_w2[m])
                    if m == 0:
                        W("tensor", dv, dv_tx[NT - 1])
                    if m >= 2:
                        W("tensor", ac, ac_qkv[m - 2])
                        if m - 2 >= 32:
                            W("tensor", dv, dv_v[m - 34])
                    for nb in range(4):
                        for k in range(NK):
                            n_ = MM(ps2[m % 2][:, bass.ts(nb, 512)], wm[:, m % 2, bass.ts(k, 128)],
                                    xT[:, k, bass.ts(nb, 512)], k == 0, k == NK - 1,
                                    inc=pe if (nb == 3 and k == NK - 1) else None)
                    pe_qkv.append(n_)
                if m >= 1 and m - 1 >= 32:
                    emit_tr_v(m - 1)
                if m <= 47:
                    W("scalar", pe, pe_qkv[m])
                    if m >= 2:
                        mp = m - 2
                        if mp < 32:
                            W("scalar", sts, sts.n)
                        else:
                            W("scalar", pe, pe_v[mp - 32])
                    if m < 16:
                        EM["scalar"](lambda e, m=m: e.activation(
                            qkvrow[:, m % 2, :], ps2[m % 2][:], AF.Identity,
                            bias=battnq_sb[:, m:m + 1], scale=CSC))
                    else:
                        EM["scalar"](lambda e, m=m: e.activation(
                            qkvrow[:, m % 2, :], ps2[m % 2][:], AF.Identity,
                            bias=battn_sb[:, m:m + 1], scale=1.0))
                    ac_qkv.append(INC("scalar", ac))
                    if m < 32:
                        W("sync", ac, ac_qkv[m])
                        st_qk[m] = DMA("sync", qkT_d[bass.ts(m, 128), :], qkvrow[:, m % 2, :], sts)
                    if m + 2 < 48:
                        W("gpsimd", pe, pe_qkv[m])
                        ld_w2[m + 2] = DMA("gpsimd", wm[:, m % 2, :], wattn[m + 2], wms[m % 2])
                    if max_phase >= 3:
                        emit_dq_piece()

            pe_p2_done = pe.n
            sts_p2_done = sts.n

        if max_phase >= 3:
            cc_done = emit_dq_collective()
            dv_dq_done = dv.n
            stg_dq_done = stg.n

        if max_phase >= 4:
            AT = TB
            qh = cv(AT, [2, 2048], BF16)
            kh = cv(AT + 8 * KB, [2, 2048], BF16)
            vh = cv(AT + 16 * KB, [2, 2048], BF16)
            pn = cv(AT + 24 * KB, [2, 2048], BF16)
            PT = cv(AT + 32 * KB, [16, 512], BF16)
            orow = cv(AT + 48 * KB, [2, 2048], BF16)

            pe_head = []; st_o = []; ld_h4 = []
            oT5 = cv(TB + 56 * KB, [16, 2048], BF16)
            wp5 = cv(TB + 120 * KB, [16, 2048], BF16)

            def emit_head_loads(h):
                if h == 0:
                    W("sync", dv, dv_dq_done)
                    W("sync", stg, stg_dq_done)
                DMA("sync", qh[:, h % 2, :], qkT_d[bass.ts(h, 128), :], hds[h % 2])
                DMA("sync", kh[:, h % 2, :], qkT_d[2048 + h * 128:2176 + h * 128, :], hds[h % 2])
                ld_h4.append(DMA("sync", vh[:, h % 2, :], v_d[h], hds[h % 2]))

            emit_head_loads(0)
            emit_head_loads(1)

            ac_exp = {}; dv_ptev = {}; ac_oev = []
            ps4 = [psA, psB]
            last_oev = 0
            n_s = {}; n_pt = {}; n_red = {}; n_recip = {}; n_nrm = {}
            for h in range(NH):
                hp = h % 2
                W("tensor", hds[h % 2], ld_h4[h])
                for t in range(17):
                    tl = t - 1
                    if t <= 15:
                        qt = t
                        Lk = (qt + 1) * 128
                        par = qt % 2
                        P = ps4[par]
                        prev = dv_ptev.get((h, qt - 2))
                        if prev is None and qt < 2:
                            prev = dv_ptev.get((h - 1, 14 + par))
                        if prev:
                            W("tensor", dv, prev)
                        elif h == 0 and qt < 2:
                            W("tensor", dv, dv.n)
                            W("tensor", ac, ac.n)
                        if par == 1 and Lk > 1536 and last_oev:
                            W("tensor", ac, last_oev)
                        nnb = (Lk + 511) // 512
                        for nb in range(nnb):
                            N = min(512, Lk - nb * 512)
                            lastc = nb == nnb - 1
                            MM(P[:, nb * 512:nb * 512 + N], qh[:, hp, bass.ts(qt, 128)],
                               kh[:, hp, nb * 512:nb * 512 + N], True, not lastc)
                            if lastc:
                                n_s[t] = MM(P[:, bass.ts(qt, 128)], masktb, identb,
                                            False, True, inc=pe)
                    if tl >= 0:
                        parl = tl % 2
                        Pl = ps4[parl]
                        Lkl = (tl + 1) * 128
                        W("vector", ac, ac_exp[(h, tl)])
                        EM["vector"](lambda e, qt=tl: e.reciprocal(
                            rinv_sb[:, qt:qt + 1], rsum_sb[:, qt:qt + 1]))
                        EM["vector"](lambda e: e.drain())
                        EM["vector"](lambda e, L=Lkl, qt=tl, par=parl: e.tensor_scalar(
                            pn[:, par, 0:L], pn[:, par, 0:L], rinv_sb[:, qt:qt + 1],
                            None, ALU.mult))
                        n_nrm[tl] = INC("vector", dv)
                        W("tensor", dv, n_nrm[tl])
                        nkt = tl + 1
                        for kt in range(nkt):
                            n_pt[tl] = MM(Pl[:, bass.ts(kt, 128)], pn[:, parl, bass.ts(kt, 128)],
                                          identb, True, True, inc=pe if kt == nkt - 1 else None)
                    if t <= 15:
                        W("vector", pe, n_s[t])
                        EM["vector"](lambda e, L=Lk, qt=t, P=P: e.reduce_max(
                            nmax_sb[:, qt:qt + 1], P[:, 0:L], axis=AX.X, negate=True))
                        n_red[t] = INC("vector", dv)
                    if tl >= 0:
                        W("vector", pe, n_pt[tl])
                        EM["vector"](lambda e, nkt=tl + 1, q4=tl % 4, P=Pl: e.tensor_copy(
                            PT[:, 0:nkt, bass.ts(q4, 128)],
                            P[:, 0:nkt * 128].rearrange("p (a b) -> p a b", b=128)))
                        dv_ptev[(h, tl)] = INC("vector", dv)
                    if t <= 15:
                        W("scalar", dv, n_red[t])
                        EM["scalar"](lambda e, L=Lk, qt=t, par=par, P=P: e.activation(
                            pn[:, par, 0:L], P[:, 0:L], AF.Exp,
                            bias=nmax_sb[:, qt:qt + 1], scale=1.0,
                            accum_out=rsum_sb[:, qt:qt + 1]))
                        ac_exp[(h, t)] = INC("scalar", ac)
                    if t >= 4 and t % 4 == 0:
                        qg = t // 4 - 1
                        for r in range(1, 4):
                            kt = qg * 4 + r
                            EM["vector"](lambda e, kt=kt, r=r: e.memset(PT[:, kt, 0:r * 128], 0.0))
                        n_z = INC("vector", dv)
                        W("tensor", dv, n_z)
                        W("tensor", ac, ac_exp[(h, qg * 4 + 3)])
                        nkt_g = qg * 4 + 4
                        for kt in range(nkt_g):
                            n_pv = MM(psB[:, 1536:2048], vh[:, hp, bass.ts(kt, 128)], PT[:, kt, :],
                                      kt == 0, kt == nkt_g - 1,
                                      inc=pe if kt == nkt_g - 1 else None)
                        W("scalar", pe, n_pv)
                        if qg == 0 and h >= 2:
                            W("scalar", sts, sts.n)
                        EM["scalar"](lambda e, qg=qg, hp=hp: e.activation(
                            orow[:, hp, bass.ts(qg, 512)], psB[:, 1536:2048], AF.Identity,
                            bias=0.0, scale=1.0))
                        n_oev = INC("scalar", ac)
                        last_oev = n_oev
                        if qg == 3:
                            ac_oev.append(n_oev)
                pe_head.append(pe.n)
                W("sync", ac, ac_oev[h])
                st_o.append(DMA("sync", oT_d[bass.ts(h, 128), :], orow[:, h % 2, :], sts))
                W("sync", sts, sts.n)
                ld_p5oT = DMA("sync", oT5[:, h, :], oT_d[bass.ts(h, 128), :], lds)
                if h == 0:
                    for k5 in range(NK):
                        DMA("gpsimd", wp5[:, k5, :], wproj[k5], ldg)
                if h + 2 < NH:
                    W("sync", pe, pe_head[h])
                    emit_head_loads(h + 2)

            sts_attn_done = sts.n
            pe_attn_done = pe.n

        if max_phase >= 5:
            P5 = TB + 56 * KB
            oT = cv(P5, [16, 2048], BF16)
            wp = cv(TB + 120 * KB, [16, 2048], BF16)
            P5S = TB
            xres = cv(P5S, [2, 2048], F32)
            rsb = cv(P5S + 16 * KB, [2048], F32)
            sqs = cv(P5S + 24 * KB, [2048], F32)
            x1sb = cv(P5S + 32 * KB, [2, 2048], F32)

            W("sync", sts, sts_attn_done)
            W("sync", pe, pe_attn_done)
            ld_xr = {0: DMA("sync", xres[:, 0, :], x_tiles[0], xls[0]),
                     1: DMA("sync", xres[:, 1, :], x_tiles[1], xls[1])}

            pe_p5 = []; dv_p5 = {}; ac_p5 = {}; st_x1 = []
            ps5 = [psA, psB]
            W("tensor", lds, lds.n)
            W("tensor", ldg, ldg.n)
            W("tensor", sts, sts_attn_done)
            for i in range(NT):
                if i <= 1:
                    W("tensor", ac, ac.n)
                    W("tensor", dv, dv.n)
                else:
                    W("tensor", dv, dv_p5[(i - 2, "bf")])
                for nb in range(4):
                    for k in range(NK):
                        n_ = MM(ps5[i % 2][:, bass.ts(nb, 512)], oT[:, k, bass.ts(i, 128)],
                                wp[:, k, bass.ts(nb, 512)], k == 0, k == NK - 1,
                                inc=pe if (nb == 3 and k == NK - 1) else None)
                pe_p5.append(n_)
                W("vector", pe, pe_p5[i])
                W("vector", xls[i % 2], ld_xr[i])
                if i >= 1:
                    W("vector", ac, ac_p5[(i - 1, "x1")])
                EM["vector"](lambda e, i=i: e.scalar_tensor_tensor(rsb[:], ps5[i % 2][:], 1.0, bpbc[:], ALU.mult, ALU.add))
                dv_p5[(i, "bf")] = INC("vector", dv)
                EM["vector"](lambda e, i=i: e.tensor_tensor(rsb[:], rsb[:], xres[:, i % 2, :], ALU.add))
                EM["vector"](lambda e: e.drain())
                EM["vector"](lambda e: e.reduce_sum(s_sum[:], rsb[:], axis=AX.X))
                EM["vector"](lambda e: e.drain())
                EM["vector"](lambda e: e.tensor_scalar(s_nm[:], s_sum[:], -1.0 / 2048.0, None, ALU.mult))
                dv_p5[(i, "r")] = INC("vector", dv)
                W("scalar", dv, dv_p5[(i, "r")])
                EM["scalar"](lambda e: e.activation(sqs[:], rsb[:], AF.Square, bias=s_nm[:, 0:1],
                                                    scale=1.0, accum_out=s_varp[:, 0:1]))
                EM["scalar"](lambda e: e.drain())
                EM["scalar"](lambda e: e.activation(s_std[:], s_varp[:], AF.Sqrt,
                                                    bias=eps5[:, 0:1], scale=1.0 / 2048.0))
                ac_p5[(i, "sq")] = INC("scalar", ac)
                W("vector", ac, ac_p5[(i, "sq")])
                EM["vector"](lambda e: e.reciprocal(s_rstd[:], s_std[:]))
                EM["vector"](lambda e: e.drain())
                EM["vector"](lambda e: e.tensor_tensor(s_nmr[:], s_nm[:], s_rstd[:], ALU.mult))
                dv_p5[(i, "n")] = INC("vector", dv)
                W("scalar", dv, dv_p5[(i, "n")])
                if i >= 2:
                    W("scalar", sts, sts.n)
                EM["scalar"](lambda e, i=i: e.activation(x1sb[:, i % 2, :], rsb[:], AF.Identity,
                                                         bias=s_nmr[:, 0:1], scale=s_rstd[:, 0:1]))
                ac_p5[(i, "x1")] = INC("scalar", ac)
                W("sync", ac, ac_p5[(i, "x1")])
                st_x1.append(DMA("sync", x1_tiles[i], x1sb[:, i % 2, :], sts))
                if i + 2 < NT:
                    W("sync", dv, dv_p5[(i, "r")])
                    ld_xr[i + 2] = DMA("sync", xres[:, i % 2, :], x_tiles[i + 2], xls[i % 2])
            sts_x1_done = sts.n
            pe_p5_done = pe.n

        if max_phase >= 6:
            x1T = cv(XT, [16, 2048], BF16)
            P5B = TB + 64 * KB
            x1bf = cv(P5B, [2, 2048], BF16)

            pe_x1 = []; dv_tx1 = []; ld_x1 = {}
            for i in range(NT):
                if i < 2:
                    if i == 0:
                        W("gpsimd", sts, sts_x1_done)
                        W("gpsimd", pe, pe_p5_done)
                    ld_x1[i] = DMA("gpsimd", x1bf[:, i % 2, :], x1_tiles[i], xls[i % 2])
                W("tensor", xls[i % 2], ld_x1[i])
                if i == 0:
                    W("tensor", dv, dv.n)
                    W("tensor", ac, ac.n)
                else:
                    W("tensor", dv, dv_tx1[i - 1])
                for f in range(NK):
                    n_ = MM(psA[:, bass.ts(f, 128)], x1bf[:, i % 2, bass.ts(f, 128)], identb,
                            True, True, inc=pe if f == NK - 1 else None)
                pe_x1.append(n_)
                W("vector", pe, pe_x1[i])
                EM["vector"](lambda e, i=i: e.tensor_copy(x1T[:, :, bass.ts(i, 128)], psA[:]))
                dv_tx1.append(INC("vector", dv))
                if i + 2 < NT:
                    W("gpsimd", pe, pe_x1[i])
                    ld_x1[i + 2] = DMA("gpsimd", x1bf[:, i % 2, :], x1_tiles[i + 2], xls[i % 2])

        if max_phase >= 7:
            P6 = TB + 64 * KB
            wg6 = cv(P6, [16, 1408], BF16)
            wu6 = cv(P6 + 45 * KB, [16, 1408], BF16)
            sgt = cv(P6 + 90 * KB, [2048], BF16)
            hrow = cv(P6 + 94 * KB, [2, 2048], BF16)

            wgu_rk = wgu_gd.rearrange("(r k p) c -> r k p c", r=8, p=128)
            hT_rows = hT_d.rearrange("(k p) t -> k p t", p=128)
            pe_p6 = {}; st_h6 = []
            ac_p6s = None; dv_p6h = None
            for a in range(4):
                ld_g = 0
                for k in range(NK):
                    if k == 0:
                        if a == 0:
                            W("sync", pe, pe.n)
                        else:
                            W("sync", pe, pe_p6[(a - 1, 10)])
                if a == 0:
                    W("sync", cc, cc_done)
                for k in range(NK):
                    ld_g = DMA("sync", wg6[:, k, :], wgu_rk[a, k], lds)
                ld_u = 0
                for k in range(NK):
                    if k == 0:
                        if a == 0:
                            W("gpsimd", pe, pe.n)
                        else:
                            W("gpsimd", pe, pe_p6[(a - 1, 10)])
                for k in range(NK):
                    ld_u = DMA("gpsimd", wu6[:, k, :], wgu_rk[4 + a, k], ldg)
                for mpl in range(11):
                    mp = a * 11 + mpl
                    W("tensor", lds, ld_g)
                    W("tensor", ldg, ld_u)
                    if mp == 0:
                        W("tensor", dv, dv.n)
                        W("tensor", ac, ac.n)
                    else:
                        W("tensor", ac, ac_p6s)
                        W("tensor", dv, dv_p6h)
                    for nb in range(4):
                        for k in range(NK):
                            n_g = MM(psA[:, bass.ts(nb, 512)], wg6[:, k, bass.ts(mpl, 128)],
                                     x1T[:, k, bass.ts(nb, 512)], k == 0, k == NK - 1,
                                     inc=pe if (nb == 3 and k == NK - 1) else None)
                    for nb in range(4):
                        for k in range(NK):
                            n_u = MM(psB[:, bass.ts(nb, 512)], wu6[:, k, bass.ts(mpl, 128)],
                                     x1T[:, k, bass.ts(nb, 512)], k == 0, k == NK - 1,
                                     inc=pe if (nb == 3 and k == NK - 1) else None)
                    W("scalar", pe, n_g)
                    if mp >= 1:
                        W("scalar", dv, dv_p6h)
                    EM["scalar"](lambda e: e.activation(sgt[:], psA[:], AF.Silu, bias=0.0, scale=1.0))
                    ac_p6s = INC("scalar", ac)
                    W("vector", ac, ac_p6s)
                    W("vector", pe, n_u)
                    if mp >= 2:
                        W("vector", sts, sts.n)
                    EM["vector"](lambda e, mp=mp: e.tensor_tensor(hrow[:, mp % 2, :], sgt[:],
                                                                  psB[:], ALU.mult))
                    dv_p6h = INC("vector", dv)
                    pe_p6[(a, mpl)] = pe.n
                    W("sync", dv, dv_p6h)
                    st_h6.append(DMA("sync", hT_rows[0].tensor.ap()
                                     .rearrange("(k p) t -> k p t", p=128)[mp] if False else
                                     hT_d[bass.ts(mp, 128), :], hrow[:, mp % 2, :], sts))
            sts_h_done = sts.n

        if max_phase >= 8:
            P7 = TB
            hT7 = cv(P7, [2, 44, 512], BF16)
            dT = cv(P7 + 88 * KB, [16, 512], F32)
            wdc = cv(P7 + 120 * KB, [2, 8, 512], BF16)
            EPI = P7 + 136 * KB
            sq7 = cv(EPI, [2048], F32)
            x1r7 = cv(EPI + 8 * KB, [2, 2048], F32)
            dsb = cv(EPI + 24 * KB, [2048], F32)
            r2sb = cv(EPI + 32 * KB, [2048], F32)
            outsb = cv(EPI + 40 * KB, [2048], F32)

            wdn_pkc8 = wdn_gd.rearrange("(r K p) c -> r p K c", r=8, p=128)
            hT_pkc = hT_d.rearrange("(K p) t -> p K t", p=128)
            accs = [psA[:, bass.ts(m4, 512)] for m4 in range(4)]

            st_y = []
            pe_dn_hist = {}
            for tg in range(4):
                tb = tg % 2
                ld_h7 = 0
                for gb in range(11):
                    if gb == 0:
                        W("gpsimd", ldg, ldg.n)
                        if tg == 0:
                            W("gpsimd", sts, sts_h_done)
                            W("gpsimd", pe, pe.n)
                            W("gpsimd", ac, ac.n)
                            W("gpsimd", dv, dv.n)
                        elif tg >= 2:
                            W("gpsimd", pe, pe_dn_hist[tg - 2])
                    ld_h7 = DMA("gpsimd", hT7[:, tb, gb * 4:gb * 4 + 4, :],
                                hT_pkc[:, gb * 4:gb * 4 + 4, bass.ts(tg, 512)], ldg)
                for mg in range(4):
                    pe_p7mm = []
                    GS = [8, 8, 8, 8, 8, 4]
                    GO = [0, 8, 16, 24, 32, 40]
                    def emit_wdc_load(g):
                        n = GS[g]
                        DMA("sync", wdc[:, g % 2, 0:n, 0:256],
                            wdn_pkc8[2 * mg, :, GO[g]:GO[g] + n, :], wds[g % 2])
                        return DMA("sync", wdc[:, g % 2, 0:n, 256:512],
                                   wdn_pkc8[2 * mg + 1, :, GO[g]:GO[g] + n, :], wds[g % 2])
                    ld_wd = {}
                    for g in range(2):
                        if g == 0:
                            W("sync", pe, pe.n)
                            if tg == 0 and mg == 0:
                                W("sync", cc, cc_done)
                                W("sync", ac, ac.n)
                                W("sync", dv, dv.n)
                        ld_wd[g] = emit_wdc_load(g)
                    for kq in range(44):
                        g = kq // 8
                        gk = kq % 8
                        if gk == 0:
                            W("tensor", wds[g % 2], ld_wd[g])
                        if kq == 0:
                            W("tensor", ldg, ldg.n)
                            if mg == 0:
                                if tg == 0:
                                    W("tensor", dv, dv.n)
                                    W("tensor", ac, ac.n)
                                else:
                                    W("tensor", dv, dv_dT)
                            if mg > 0 or tg > 0:
                                W("tensor", dv, dv_dT_last)
                        for m4 in range(4):
                            n_ = MM(accs[m4], wdc[:, g % 2, gk, bass.ts(m4, 128)],
                                    hT7[:, tb, kq, :], kq == 0, kq == 43,
                                    inc=pe if (m4 == 3) else None)
                        pe_p7mm.append(n_)
                        if gk == GS[g] - 1 and g + 2 < 6:
                            W("sync", pe, pe_p7mm[kq])
                            ld_wd[g + 2] = emit_wdc_load(g + 2)
                    W("vector", pe, pe_p7mm[-1])
                    for m4 in range(4):
                        m = mg * 4 + m4
                        EM["vector"](lambda e, m=m, m4=m4: e.tensor_copy(dT[:, m, :], accs[m4]))
                    dv_dT_last = INC("vector", dv)
                dv_dT = dv_dT_last
                pe_dn_hist[tg] = pe.n
                for il in range(4):
                    i = tg * 4 + il
                    if il < 2:
                        if il == 0 and tg == 0:
                            W("scalar", dv, dv_dT)
                        n_x1r = DMA("scalar", x1r7[:, il % 2, :], x1_tiles[i], xls[il % 2])
                        if il == 0:
                            ld_x1r = {}
                        ld_x1r[il] = n_x1r
                    W("tensor", dv, dv_dT)
                    if il >= 1:
                        W("tensor", dv, dv_p7tr[il - 1])
                    elif tg > 0:
                        W("tensor", dv, dv_p7tr[3])
                        W("tensor", ac, ac.n)
                    for m in range(NK):
                        n_tr = MM(psB[:, bass.ts(m, 128)], dT[:, m, bass.ts(il, 128)], identf,
                                  True, True, inc=pe if m == NK - 1 else None)
                    if il == 0:
                        dv_p7tr = []
                    W("vector", pe, n_tr)
                    if il >= 1 or tg > 0:
                        W("vector", ac, ac_ep_last)
                    EM["vector"](lambda e: e.tensor_copy(dsb[:], psB[:]))
                    dv_p7tr.append(INC("vector", dv))
                    W("scalar", dv, dv_p7tr[il])
                    EM["scalar"](lambda e: e.activation(sq7[:], dsb[:], AF.Square, bias=0.0,
                                                        scale=1.0, accum_out=s_varp[:, 0:1]))
                    EM["scalar"](lambda e: e.drain())
                    EM["scalar"](lambda e: e.activation(s_std[:], s_varp[:], AF.Sqrt,
                                                        bias=eps6[:, 0:1], scale=1.0 / 2048.0))
                    n_ac1 = INC("scalar", ac)
                    W("vector", ac, n_ac1)
                    W("vector", xls[il % 2], ld_x1r[il])
                    EM["vector"](lambda e: e.reciprocal(s_rstd[:], s_std[:]))
                    EM["vector"](lambda e: e.drain())
                    EM["vector"](lambda e, il=il: e.scalar_tensor_tensor(
                        r2sb[:], dsb[:], s_rstd[:, 0:1], x1r7[:, il % 2, :], ALU.mult, ALU.add))
                    EM["vector"](lambda e: e.drain())
                    EM["vector"](lambda e: e.reduce_sum(s_sum[:], r2sb[:], axis=AX.X))
                    EM["vector"](lambda e: e.drain())
                    EM["vector"](lambda e: e.tensor_scalar(s_nm[:], s_sum[:], -1.0 / 2048.0, None, ALU.mult))
                    n_dv2 = INC("vector", dv)
                    W("scalar", dv, n_dv2)
                    EM["scalar"](lambda e: e.activation(sq7[:], r2sb[:], AF.Square, bias=s_nm[:, 0:1],
                                                        scale=1.0, accum_out=s_varp[:, 0:1]))
                    EM["scalar"](lambda e: e.drain())
                    EM["scalar"](lambda e: e.activation(s_std[:], s_varp[:], AF.Sqrt,
                                                        bias=eps5[:, 0:1], scale=1.0 / 2048.0))
                    n_ac3 = INC("scalar", ac)
                    W("vector", ac, n_ac3)
                    EM["vector"](lambda e: e.reciprocal(s_rstd[:], s_std[:]))
                    EM["vector"](lambda e: e.drain())
                    EM["vector"](lambda e: e.tensor_tensor(s_nmr[:], s_nm[:], s_rstd[:], ALU.mult))
                    n_dv4 = INC("vector", dv)
                    W("scalar", dv, n_dv4)
                    W("scalar", sts, sts.n)
                    EM["scalar"](lambda e, il=il: e.activation(outsb[:], r2sb[:], AF.Identity,
                                                               bias=s_nmr[:, 0:1], scale=s_rstd[:, 0:1]))
                    EM["scalar"](lambda e: e.drain())
                    st_y.append(DMA("scalar", y_tiles[i], outsb[:], sts))
                    ac_ep_last = INC("scalar", ac)
                    if il + 2 < 4:
                        ld_x1r[il + 2] = DMA("scalar", x1r7[:, il % 2, :], x1_tiles[i + 2], xls[il % 2])

        W("sync", sts, sts.n)
        W("sync", stg, stg.n)
        W("gpsimd", ldg, ldg.n)
        W("sync", lds, lds.n)

        @block.sync
        def _(e):
            for f in ops["sync"]:
                f(e)

        @block.tensor
        def _(e):
            for f in ops["tensor"]:
                f(e)

        @block.vector
        def _(e):
            for f in ops["vector"]:
                f(e)

        @block.scalar
        def _(e):
            for f in ops["scalar"]:
                f(e)

        @block.gpsimd
        def _(e):
            for f in ops["gpsimd"]:
                f(e)

    return nc


def prep_inputs(L0_attn_x, L0_attn_w_attn, L0_attn_b_attn, L0_attn_w_proj,
                L0_attn_b_proj, L0_ffn_gate_up_packed, L0_ffn_gate_up_scales,
                L0_ffn_down_packed, L0_ffn_down_scales):
    x = np.ascontiguousarray(np.asarray(L0_attn_x, dtype=np.float32))
    wa = np.asarray(L0_attn_w_attn, dtype=np.float32)
    wattn = np.ascontiguousarray(wa.reshape(16, 128, 48, 128).transpose(2, 1, 0, 3)
                                 .reshape(48, 128, 2048))
    battn = np.ascontiguousarray(np.asarray(L0_attn_b_attn, dtype=np.float32).reshape(1, 6144))
    wproj = np.ascontiguousarray(np.asarray(L0_attn_w_proj, dtype=np.float32).reshape(16, 128, 2048))
    bprojv = np.ascontiguousarray(np.asarray(L0_attn_b_proj, dtype=np.float32).reshape(1, 2048))
    gu_pkt_f = np.asarray(L0_ffn_gate_up_packed, dtype=np.int32).T.reshape(2, 128, GU_COLS)
    gu_sc_f = np.repeat(np.asarray(L0_ffn_gate_up_scales, dtype=np.float16),
                        16, axis=0).reshape(2, 128, GU_COLS)
    dn_pkt_f = np.asarray(L0_ffn_down_packed, dtype=np.int32).T
    dn_sc_f = np.repeat(np.asarray(L0_ffn_down_scales, dtype=np.float16), 16, axis=0)
    identb = np.eye(128, dtype=ml_dtypes.bfloat16)
    identf = np.eye(128, dtype=np.float32)
    mask = np.triu(np.full((128, 128), NEG, dtype=np.float32), k=1)
    masktb = np.ascontiguousarray(mask.T).astype(ml_dtypes.bfloat16)
    ones1 = np.ones((1, 128), dtype=np.float32)

    common = dict(wattn=wattn, battn=battn, wproj=wproj, bproj=bprojv,
                  identb=identb, identf=identf, mask=mask, masktb=masktb, ones1=ones1)
    in_maps = []
    for b in range(8):
        m = dict(common)
        m["x"] = np.ascontiguousarray(x[b])
        m["gu_pkt"] = np.ascontiguousarray(gu_pkt_f[:, :, b * 1408:(b + 1) * 1408])
        m["gu_sc"] = np.ascontiguousarray(gu_sc_f[:, :, b * 1408:(b + 1) * 1408])
        m["dn_pkt"] = np.ascontiguousarray(dn_pkt_f[:, b * 256:(b + 1) * 256])
        m["dn_sc"] = np.ascontiguousarray(dn_sc_f[:, b * 256:(b + 1) * 256])
        in_maps.append(m)
    return in_maps


def kernel(**inputs):
    in_maps = prep_inputs(**inputs)
    nc = build_program(dbg=False)
    res = run_bass_kernel_spmd(nc, in_maps, core_ids=list(range(8)))
    out = np.stack([np.asarray(r["y"], dtype=np.float32) for r in res.results], axis=0)
    return out
